# version 02
# baseline (speedup 1.0000x reference)
import numpy as np
import ml_dtypes
import concourse.bass as bass
import concourse.mybir as mybir
from concourse.bass_utils import run_bass_kernel_spmd
from contextlib import ExitStack

BF = ml_dtypes.bfloat16
F32 = mybir.dt.float32
BF16 = mybir.dt.bfloat16
AF = mybir.ActivationFunctionType
ALU = mybir.AluOpType
AX = mybir.AxisListType

T = 4096
D = 1024
ND = D // 128
TC = 512
NTC = T // TC
NCORE = 8
DSTATE = 16


class Prog:
    ENGINES = ("sync", "gpsimd", "tensor", "vector", "scalar")

    def __init__(self, nc):
        self.nc = nc
        self.ops = {e: [] for e in self.ENGINES}
        self.counts = {}
        self.order = []

    def _sem(self, name):
        if name not in self.counts:
            self.counts[name] = 0
            self.order.append(name)

    def op(self, eng, fn, waits=(), sem=None, inc=1):
        ws = tuple((s, v) for (s, v) in waits if s is not None)
        tok = None
        if sem is not None:
            self._sem(sem)
            self.counts[sem] += inc
            tok = (sem, self.counts[sem])
        self.ops[eng].append((fn, ws, sem, inc))
        return tok

    def dma(self, eng, fn, waits=(), sem=None):
        return self.op(eng, fn, waits, sem, 16)

    def fence(self):
        toks = tuple((n, c) for n, c in self.counts.items())
        for eng in self.ENGINES:
            self.op(eng, lambda e: None, waits=toks)

    def build(self, ctx):
        sems = {n: ctx.enter_context(self.nc.semaphore(n)) for n in self.order}
        block = ctx.enter_context(self.nc.Block())
        for eng_name in self.ENGINES:
            oplist = self.ops[eng_name]
            if not oplist:
                continue

            def mk(oplist):
                def emit(e):
                    hw = {}
                    for (fn, ws, sem, inc) in oplist:
                        for (s, v) in ws:
                            if hw.get(s, -1) < v:
                                e.wait_ge(sems[s], v)
                                hw[s] = v
                        ins = fn(e)
                        if sem is not None:
                            ins.then_inc(sems[sem], inc)
                return emit

            getattr(block, eng_name)(mk(oplist))


def tmax(*toks):
    best = {}
    def add(t):
        if t is None:
            return
        if isinstance(t, list) or (isinstance(t, tuple) and t and isinstance(t[0], (tuple, list))):
            for u in t:
                add(u)
        elif isinstance(t, tuple) and len(t) == 2 and isinstance(t[0], str):
            s, v = t
            if best.get(s, -1) < v:
                best[s] = v
        else:
            for u in t:
                add(u)
    for t in toks:
        add(t)
    return tuple(best.items())



import contextlib
_nullctx = contextlib.nullcontext


class _SB:
    pass


SHARED_SBUF = {"cst", "ones_col_bf", "ones_row_f", "identf", "E8", "SB16"}
SBUF_TOP = 229344


class PhaseAlloc:
    def __init__(self, nc):
        self.nc = nc
        self.base = None
        self.bump = 0
        self.peak = 0
        self.shared = {}
        self.pref = "P"

    def start(self, pref):
        if self.base is None:
            self.base = self.nc._sbuf_addr_for_side("left")
        self.bump = 0
        self.pref = pref

    def __call__(self, name, shape, dt=F32):
        if name in SHARED_SBUF:
            if name not in self.shared:
                assert self.base is None, f"shared alloc {name} after phase start"
                self.shared[name] = self.nc.alloc_sbuf_tensor("sh_" + name, list(shape), dt)
            return self.shared[name]
        esz = 2 if dt == BF16 else 4
        elems = 1
        for d_ in shape[1:]:
            elems *= d_
        off = (self.base + self.bump + 31) // 32 * 32
        t = self.nc.alloc_sbuf_tensor_at(f"{self.pref}_{name}", list(shape), dt, offset=off)
        self.bump = off - self.base + elems * esz
        self.peak = max(self.peak, self.base + self.bump)
        assert self.base + self.bump <= SBUF_TOP, \
            f"phase {self.pref} SBUF overflow: {self.base + self.bump} > {SBUF_TOP} at {name}"
        return t

    def prealloc_shared(self, specs):
        for (name, shape, dt) in specs:
            if name not in self.shared:
                self.shared[name] = self.nc.alloc_sbuf_tensor("sh_" + name, list(shape), dt)


def _consts():
    c = {}
    c["ident"] = np.eye(128, dtype=BF)
    LR = np.zeros((128, 16 * 128), dtype=np.float32)
    LS = np.zeros((128, 16 * 128), dtype=np.float32)
    for j in range(16):
        for p in range(128):
            k = j * 8 + p // 16
            LR[k, j * 128 + p] = 1.0
            LS[p, j * 128 + k] = 1.0
    c["LR"] = LR.astype(BF)
    c["LS"] = LS.astype(BF)
    SB16 = np.zeros((16, 128), dtype=np.float32)
    for m in range(128):
        SB16[m % 16, m] = 1.0
    c["SB16"] = SB16.astype(BF)
    cp = np.zeros((128, 8), np.float32)
    cp[:, 0] = 1.0; cp[:, 1] = -1.0; cp[:, 2] = -0.5; cp[:, 3] = 1e-5
    cp[:, 4] = 1e-6; cp[:, 5] = 0.125; cp[:, 6] = 0.0
    c["cst"] = cp
    c["ones_col_bf"] = np.ones((128, 1), dtype=BF)
    c["ones_row_f"] = np.ones((1, 128), dtype=np.float32)
    c["ones_row_bf"] = np.ones((1, 128), dtype=BF)
    c["neg_row_bf"] = -np.ones((1, 128), dtype=BF)
    half = 32
    inv = 1.0 / 10000.0 ** (np.arange(half, dtype=np.float64) / half)
    ang = np.arange(T, dtype=np.float64)[:, None] * inv[None, :]
    cos = np.cos(ang).astype(np.float32)
    sin = np.sin(ang).astype(np.float32)
    cos_bc = np.empty((128, T), np.float32)
    sin_sgn = np.empty((128, T), np.float32)
    for p in range(128):
        cos_bc[p] = cos[:, p % 32]
        sin_sgn[p] = -sin[:, p % 32] if (p % 64) < 32 else sin[:, p % 32]
    c["cos_bc"] = cos_bc.astype(BF)
    c["sin_sgn"] = sin_sgn.astype(BF)
    RS = np.zeros((128, 128), np.float32)
    for p in range(128):
        q = p + 32 if (p % 64) < 32 else p - 32
        RS[q, p] = 1.0
    c["Rswap"] = RS.astype(BF)
    return c

_CONSTS = _consts()
_NOISE = None


def _gumbel_noise():
    global _NOISE
    if _NOISE is None:
        import jax
        cpu = jax.devices("cpu")[0]
        with jax.default_device(cpu):
            g = jax.random.gumbel(jax.random.key(42), (8, 4096, 8), dtype=jax.numpy.float32)
            _NOISE = np.asarray(g)
    return _NOISE



def emit_layernorm(p, sb, io, xc, xc_tok, kA=6, kB=7, kS1=4, kS2=5, PS=None, eps_col=3):
    psA, psB, psS1, psS2 = PS[kA], PS[kB], PS[kS1], PS[kS2]
    xcw = xc[:].rearrange("p c t -> p (c t)")
    t_xb = p.op("vector", lambda e: e.tensor_copy(sb.xb[:].rearrange("p c t -> p (c t)"), xcw),
                waits=xc_tok, sem="V")
    t_sq = p.op("scalar", lambda e: e.activation(sb.sq[:].rearrange("p c t -> p (c t)"), xcw, AF.Square),
                waits=xc_tok, sem="S")
    tsx = tsq = None
    for c in range(ND):
        sp = (c == ND - 1)
        tsx = p.op("tensor", (lambda c=c, sp=sp: lambda e: e.matmul(
            psS1[0:1, :], sb.ones_col_bf[:], sb.xb[:, c, :], start=(c == 0), stop=sp))(),
            waits=tmax(t_xb, io.get(f"ps{kS1}_free")) if c == 0 else (), sem="T" if sp else None)
    for c in range(ND):
        sp = (c == ND - 1)
        tsq = p.op("tensor", (lambda c=c, sp=sp: lambda e: e.matmul(
            psS2[0:1, :], sb.ones_col_bf[:], sb.sq[:, c, :], start=(c == 0), stop=sp))(),
            waits=tmax(t_sq, io.get(f"ps{kS2}_free")) if c == 0 else (), sem="T" if sp else None)
    t1 = p.op("vector", lambda e: e.tensor_scalar_mul(sb.row_nm[:], psS1[0:1, :], -1.0 / D),
              waits=(tsx,), sem="V")
    t2 = p.op("vector", lambda e: e.tensor_scalar_mul(sb.row_t1[:], psS2[0:1, :], 1.0 / D),
              waits=(tsq,), sem="V")
    t3 = p.op("vector", lambda e: e.tensor_mul(sb.row_t2[:], sb.row_nm[:], sb.row_nm[:]),
              waits=(t1,), sem="V")
    t4 = p.op("vector", lambda e: e.tensor_sub(sb.row_t1[:], sb.row_t1[:], sb.row_t2[:]),
              waits=(t2, t3), sem="V")
    t5 = p.op("scalar", lambda e: e.activation(sb.row_t2[:], sb.row_t1[:], AF.Ln,
                                               bias=sb.cst[0:1, eps_col:eps_col + 1]),
              waits=(t4,), sem="S")
    t6 = p.op("scalar", lambda e: e.activation(sb.row_a[:], sb.row_t2[:], AF.Exp, scale=-0.5),
              waits=(t5,), sem="S")
    t7 = p.op("vector", lambda e: e.tensor_mul(sb.row_b[:], sb.row_nm[:], sb.row_a[:]),
              waits=(t6,), sem="V")
    tA = p.op("tensor", lambda e: e.matmul(psA[:], sb.ones_row_f[:], sb.row_a[:], start=True, stop=True),
              waits=tmax(t6, io.get(f"ps{kA}_free")), sem="T")
    tB = p.op("tensor", lambda e: e.matmul(psB[:], sb.ones_row_f[:], sb.row_b[:], start=True, stop=True),
              waits=tmax(t7, io.get(f"ps{kB}_free")), sem="T")
    last = None
    for c in range(ND):
        tm_ = p.op("vector", (lambda c=c: lambda e: e.tensor_mul(sb.xhat[:, c, :], xc[:, c, :], psA[:]))(),
                   waits=tmax(tA, xc_tok) if c == 0 else (), sem="V")
        last = p.op("vector", (lambda c=c: lambda e: e.tensor_add(
            sb.xhat[:, c, :], sb.xhat[:, c, :], psB[:]))(),
            waits=tmax(tm_, (tB,) if c == 0 else None), sem="V")
    t_cast = p.op("vector", lambda e: e.tensor_copy(
        sb.xhatb[:].rearrange("p c t -> p (c t)"), sb.xhat[:].rearrange("p c t -> p (c t)")),
        waits=(last,), sem="V")
    io[f"ps{kS1}_free"] = t1
    io[f"ps{kS2}_free"] = t2
    io[f"ps{kA}_free"] = last
    io[f"ps{kB}_free"] = last
    return last, t_cast



def emit_A(nc, p, palloc, PS, dr):
    xin = dr["xin"]
    xout = dr["xout"]

    with _nullctx():
        sb = _SB()
        def alloc(name, shape, dt=F32):
            t = palloc(name, shape, dt)
            setattr(sb, name, t)
            return t
        alloc("xc0", [128, ND, TC]); alloc("xc1", [128, ND, TC])
        alloc("xb", [128, ND, TC], BF16)
        alloc("sq", [128, ND, TC], BF16)
        alloc("xhat", [128, ND, TC])
        alloc("xhatb", [128, ND, TC], BF16)
        alloc("dt_scr", [128, TC])
        alloc("dth", [128, ND, TC], BF16); alloc("dtl", [128, ND, TC], BF16)
        alloc("uh", [128, ND, TC], BF16); alloc("ul", [128, ND, TC], BF16)
        alloc("u_tmp", [128, TC])
        alloc("zb", [128, TC]); alloc("azb", [128, TC])
        alloc("ebuf", [128, TC]); alloc("lbuf", [128, TC])
        alloc("rowB", [16, TC]); alloc("rowC", [16, TC])
        alloc("Bh", [16, TC], BF16); alloc("Bl", [16, TC], BF16)
        alloc("Ch", [16, TC], BF16); alloc("Cl", [16, TC], BF16)
        alloc("Bbc", [128, TC]); alloc("Cbc", [128, TC])
        alloc("dA0", [128, TC]); alloc("dA1", [128, TC])
        alloc("dB0", [128, TC]); alloc("dB1", [128, TC])
        alloc("st0", [128, TC]); alloc("st1", [128, TC])
        alloc("tmp0", [128, TC], BF16); alloc("tmp1", [128, TC], BF16)
        alloc("state_cols", [128, 128])
        alloc("ybf", [128, ND, TC], BF16)
        for r in ("row_nm", "row_t1", "row_t2", "row_a", "row_b"):
            alloc(r, [1, TC])
        alloc("wdtT", [1, D])
        alloc("bdt", [128, ND])
        alloc("wxpT", [128, ND, 32], BF16)
        alloc("bxp", [16, 2])
        alloc("wou0", [128, ND, 128], BF16)
        alloc("wou1", [128, ND, 128], BF16)
        alloc("bout", [128, ND])
        alloc("Acl", [128, 128])
        alloc("negA", [128, 128])
        alloc("LR", [128, 2048], BF16)
        alloc("LS", [128, 2048], BF16)
        alloc("SB16", [16, 128], BF16)
        alloc("ones_col_bf", [128, 1], BF16)
        alloc("ones_row_f", [1, 128])
        alloc("cst", [128, 8])


        wtoks = []
        for nm_ in ("wdtT", "bdt", "bxp", "bout", "Acl", "LR", "LS", "SB16",
                    "ones_col_bf", "ones_row_f", "cst"):
            wtoks.append(p.dma("sync", (lambda nm_=nm_: lambda e: e.dma_start(
                out=getattr(sb, nm_)[:], in_=dr[nm_]))(), sem="wld"))
        wtoks.append(p.dma("gpsimd", lambda e: e.dma_start(
            out=sb.wxpT[:], in_=dr["wxpT"].rearrange("(c p) e -> p c e", p=128)), sem="wldg"))
        wtok = tmax(*wtoks)
        t_ea = p.op("scalar", lambda e: e.activation(sb.negA[:], sb.Acl[:], AF.Exp),
                    waits=wtok, sem="S")
        t_na = p.op("vector", lambda e: e.tensor_scalar_mul(sb.negA[:], sb.negA[:], -1.0),
                    waits=(t_ea,), sem="V")
        wtok = tmax(wtok, t_na)
        t_sc = p.op("vector", lambda e: e.memset(sb.state_cols[:], 0.0), sem="V")

        io = {}
        store_toks = []
        xc_consumed = {0: (), 1: ()}
        scan_prev = {}
        scan_prev_all = t_sc
        for i in range(NTC):
            s = i % 2
            xc = sb.xc0 if s == 0 else sb.xc1
            ldx = p.dma("sync", (lambda i=i, xc=xc: lambda e: e.dma_start(
                out=xc[:], in_=xin[:, i * TC:(i + 1) * TC].rearrange("(c p) t -> p c t", p=128)))(),
                waits=xc_consumed[s], sem=f"xld{s}")
            xc_tok = tmax(ldx, wtok)

            t_apply, t_cast = emit_layernorm(p, sb, io, xc, xc_tok, kA=6, kB=7, kS1=4, kS2=5, PS=PS)

            dthl_toks = []
            u_toks = []
            prev_dt_readers = ()
            for c in range(ND):
                tz = p.op("tensor", (lambda c=c: lambda e: e.matmul(
                    PS[4][:], sb.wdtT[0:1, c * 128:(c + 1) * 128], sb.xhat[0:1, 0, :],
                    start=True, stop=True))(),
                    waits=tmax(t_apply, io.get("ps4_free")), sem="T")
                tzb = p.op("vector", (lambda c=c: lambda e: e.tensor_scalar_add(
                    sb.zb[:], PS[4][:], sb.bdt[:, c:c + 1]))(), waits=(tz,), sem="V")
                io["ps4_free"] = tzb
                tazb = p.op("scalar", lambda e: e.activation(sb.azb[:], sb.zb[:], AF.Abs),
                            waits=(tzb,), sem="S")
                te_ = p.op("scalar", lambda e: e.activation(sb.ebuf[:], sb.azb[:], AF.Exp, scale=-1.0),
                           waits=(tazb,), sem="S")
                tl_ = p.op("scalar", lambda e: e.activation(sb.lbuf[:], sb.ebuf[:], AF.Ln,
                                                            bias=sb.cst[:, 0:1]),
                           waits=(te_,), sem="S")
                tdt = p.op("vector", (lambda c=c: lambda e: e.scalar_tensor_tensor(
                    sb.dt_scr[:], sb.zb[:], 0.0, sb.lbuf[:], ALU.max, ALU.add))(),
                    waits=tmax(tl_, tzb, prev_dt_readers), sem="V")
                tdh = p.op("vector", (lambda c=c: lambda e: e.tensor_copy(
                    sb.dth[:, c, :], sb.dt_scr[:]))(), waits=(tdt,), sem="V")
                tdl = p.op("vector", (lambda c=c: lambda e: e.tensor_sub(
                    sb.dtl[:, c, :], sb.dt_scr[:], sb.dth[:, c, :]))(), waits=(tdh,), sem="V")
                tu = p.op("vector", (lambda c=c: lambda e: e.tensor_mul(
                    sb.u_tmp[:], sb.dt_scr[:], sb.xhat[:, c, :]))(), waits=(tdt,), sem="V")
                th_ = p.op("vector", (lambda c=c: lambda e: e.tensor_copy(
                    sb.uh[:, c, :], sb.u_tmp[:]))(), waits=(tu,), sem="V")
                tl2 = p.op("vector", (lambda c=c: lambda e: e.tensor_sub(
                    sb.ul[:, c, :], sb.u_tmp[:], sb.uh[:, c, :]))(), waits=(th_,), sem="V")
                dthl_toks.append(tmax(tdh, tdl))
                u_toks.append(tl2)
                prev_dt_readers = tmax(tdl, tu)

            tBp = tCp = None
            for c in range(ND):
                sp_ = (c == ND - 1)
                tBp = p.op("tensor", (lambda c=c, sp_=sp_: lambda e: e.matmul(
                    PS[4][0:16, :], sb.wxpT[:, c, 0:16], sb.xhatb[:, c, :], start=(c == 0), stop=sp_))(),
                    waits=tmax(t_cast, io.get("ps4_free")) if c == 0 else (), sem="T" if sp_ else None)
            for c in range(ND):
                sp_ = (c == ND - 1)
                tCp = p.op("tensor", (lambda c=c, sp_=sp_: lambda e: e.matmul(
                    PS[5][0:16, :], sb.wxpT[:, c, 16:32], sb.xhatb[:, c, :], start=(c == 0), stop=sp_))(),
                    waits=tmax(io.get("ps5_free")) if c == 0 else (), sem="T" if sp_ else None)
            tBr = p.op("vector", lambda e: e.tensor_scalar_add(sb.rowB[:], PS[4][0:16, :], sb.bxp[:, 0:1]),
                       waits=(tBp,), sem="V")
            tCr = p.op("vector", lambda e: e.tensor_scalar_add(sb.rowC[:], PS[5][0:16, :], sb.bxp[:, 1:2]),
                       waits=(tCp,), sem="V")
            io["ps4_free"] = tBr
            io["ps5_free"] = tCr
            tBh = p.op("vector", lambda e: e.tensor_copy(sb.Bh[:], sb.rowB[:]), waits=(tBr,), sem="V")
            tBl = p.op("vector", lambda e: e.tensor_sub(sb.Bl[:], sb.rowB[:], sb.Bh[:]), waits=(tBh,), sem="V")
            tCh = p.op("vector", lambda e: e.tensor_copy(sb.Ch[:], sb.rowC[:]), waits=(tCr,), sem="V")
            tCl = p.op("vector", lambda e: e.tensor_sub(sb.Cl[:], sb.rowC[:], sb.Ch[:]), waits=(tCh,), sem="V")
            p.op("tensor", lambda e: e.matmul(PS[6][:], sb.SB16[:], sb.Bh[:], start=True, stop=False),
                 waits=tmax(tBh, io.get("ps6_free")), sem=None)
            tb2 = p.op("tensor", lambda e: e.matmul(PS[6][:], sb.SB16[:], sb.Bl[:], start=False, stop=True),
                       waits=(tBl,), sem="T")
            p.op("tensor", lambda e: e.matmul(PS[7][:], sb.SB16[:], sb.Ch[:], start=True, stop=False),
                 waits=tmax(tCh, io.get("ps7_free")), sem=None)
            tc2 = p.op("tensor", lambda e: e.matmul(PS[7][:], sb.SB16[:], sb.Cl[:], start=False, stop=True),
                       waits=(tCl,), sem="T")
            tBbc = p.op("vector", lambda e: e.tensor_copy(sb.Bbc[:], PS[6][:]), waits=(tb2,), sem="V")
            tCbc = p.op("vector", lambda e: e.tensor_copy(sb.Cbc[:], PS[7][:]), waits=(tc2,), sem="V")
            io["ps6_free"] = tBbc
            io["ps7_free"] = tCbc

            y_done = {}
            rep_free = {0: None, 1: None}
            ubc_free = {0: None, 1: None}
            buf_free = {0: None, 1: None}
            for j2 in range(128):
                c, j = j2 // 16, j2 % 16
                r = j2 % 2
                ps_rep = PS[0 + r]
                ps_ubc = PS[2 + r]
                dAb = sb.dA0 if r == 0 else sb.dA1
                dBb = sb.dB0 if r == 0 else sb.dB1
                stb = sb.st0 if r == 0 else sb.st1
                tmb = sb.tmp0 if r == 0 else sb.tmp1
                ps_y = PS[4 + (c % 2)]

                p.op("tensor", (lambda c=c, j=j, ps_rep=ps_rep: lambda e: e.matmul(
                    ps_rep[:], sb.LR[:, j * 128:(j + 1) * 128], sb.dth[:, c, :], start=True, stop=False))(),
                    waits=tmax(dthl_toks[c], rep_free[r]), sem=None)
                t2_ = p.op("tensor", (lambda c=c, j=j, ps_rep=ps_rep: lambda e: e.matmul(
                    ps_rep[:], sb.LR[:, j * 128:(j + 1) * 128], sb.dtl[:, c, :], start=False, stop=True))(),
                    sem="T")
                tdA = p.op("scalar", (lambda dAb=dAb, ps_rep=ps_rep, j2=j2: lambda e: e.activation(
                    dAb[:], ps_rep[:], AF.Exp, scale=sb.negA[:, j2:j2 + 1]))(),
                    waits=tmax(t2_, buf_free[r]), sem="S")
                rep_free[r] = tdA
                p.op("tensor", (lambda c=c, j=j, ps_ubc=ps_ubc: lambda e: e.matmul(
                    ps_ubc[:], sb.LR[:, j * 128:(j + 1) * 128], sb.uh[:, c, :], start=True, stop=False))(),
                    waits=tmax(u_toks[c], ubc_free[r]), sem=None)
                t4_ = p.op("tensor", (lambda c=c, j=j, ps_ubc=ps_ubc: lambda e: e.matmul(
                    ps_ubc[:], sb.LR[:, j * 128:(j + 1) * 128], sb.ul[:, c, :], start=False, stop=True))(),
                    sem="T")
                tdB = p.op("vector", (lambda dBb=dBb, ps_ubc=ps_ubc: lambda e: e.tensor_mul(
                    dBb[:], ps_ubc[:], sb.Bbc[:]))(), waits=tmax(t4_, tBbc, buf_free[r]), sem="V")
                ubc_free[r] = tdB
                tsc = p.op("vector", (lambda dAb=dAb, dBb=dBb, stb=stb, j2=j2: lambda e: e.tensor_tensor_scan(
                    stb[:], dAb[:], dBb[:], sb.state_cols[:, j2:j2 + 1], ALU.mult, ALU.add))(),
                    waits=tmax(tdA, tdB, scan_prev.get(j2, scan_prev_all)), sem="V")
                tsv = p.op("vector", (lambda stb=stb, j2=j2: lambda e: e.tensor_copy(
                    sb.state_cols[:, j2:j2 + 1], stb[:, TC - 1:TC]))(), waits=(tsc,), sem="V")
                scan_prev[j2] = tsv
                ttm = p.op("vector", (lambda stb=stb, tmb=tmb: lambda e: e.tensor_mul(
                    tmb[:], stb[:], sb.Cbc[:]))(), waits=tmax(tsv, tCbc), sem="V")
                buf_free[r] = ttm
                ty = p.op("tensor", (lambda j=j, tmb=tmb, ps_y=ps_y: lambda e: e.matmul(
                    ps_y[:], sb.LS[:, j * 128:(j + 1) * 128], tmb[:], start=(j == 0), stop=(j == 15)))(),
                    waits=tmax(ttm, (y_done.get(c % 2) if j == 0 else None),
                               (io.get(f"ps{4 + c % 2}_free") if j == 0 else None)),
                    sem="T" if j == 15 else None)
                if j == 15:
                    tev = p.op("scalar", (lambda c=c, ps_y=ps_y: lambda e: e.activation(
                        sb.ybf[:, c, :], ps_y[:], AF.Copy))(), waits=(ty,), sem="S")
                    y_done[c % 2] = tev
            io["ps4_free"] = y_done[0]
            io["ps5_free"] = y_done[1]
            y_all = tmax(y_done[0], y_done[1])

            proj_toks = []
            for o in range(ND):
                wsl = o % 2
                wob = sb.wou0 if wsl == 0 else sb.wou1
                two = p.dma("gpsimd", (lambda o=o, wob=wob: lambda e: e.dma_start(
                    out=wob[:], in_=dr["woutT"][:, o * 128:(o + 1) * 128]
                    .rearrange("(c p) o2 -> p c o2", p=128)))(),
                    waits=tmax(io.get(f"wou{wsl}_rd")), sem=f"would{wsl}")
                ko = 6 + (o % 2)
                ps_o = PS[ko]
                tpo = None
                for c in range(ND):
                    sp_ = (c == ND - 1)
                    w0 = tmax(y_all, two, io.get(f"ps{ko}_free")) if c == 0 else ()
                    tpo = p.op("tensor", (lambda o=o, c=c, sp_=sp_, ps_o=ps_o, wob=wob: lambda e: e.matmul(
                        ps_o[:], wob[:, c, :], sb.ybf[:, c, :],
                        start=(c == 0), stop=sp_))(),
                        waits=w0, sem="T" if sp_ else None)
                io[f"wou{wsl}_rd"] = (tpo,)
                tst = p.op("vector", (lambda o=o, ps_o=ps_o, xc=xc: lambda e: e.scalar_tensor_tensor(
                    xc[:, o, :], ps_o[:], sb.bout[:, o:o + 1], xc[:, o, :], ALU.add, ALU.add))(),
                    waits=tmax(tpo, wtok), sem="V")
                io[f"ps{ko}_free"] = tst
                proj_toks.append(tst)

            tst_d = p.dma("sync", (lambda i=i, xc=xc: lambda e: e.dma_start(
                out=xout[:, i * TC:(i + 1) * TC].rearrange("(c p) t -> p c t", p=128), in_=xc[:]))(),
                waits=tuple(proj_toks), sem=f"xst{s}")
            store_toks.append(tst_d)
            xc_consumed[s] = tmax(tst_d, *proj_toks)
            scan_prev_all = scan_prev[127]

        p.op("sync", lambda e: None, waits=tmax(*store_toks))
    return None




_A_DINS = [("wdtT", [1, D], F32), ("bdt", [128, ND], F32), ("wxpT", [D, 32], F32),
           ("bxp", [16, 2], F32), ("woutT", [D, D], F32), ("bout", [128, ND], F32),
           ("Acl", [128, 128], F32), ("LR", [128, 2048], BF16), ("LS", [128, 2048], BF16),
           ("SB16", [16, 128], BF16), ("ones_col_bf", [128, 1], BF16),
           ("ones_row_f", [1, 128], F32), ("cst", [128, 8], F32)]


def _standalone(emit, dins, extra=()):
    nc = bass.Bass("TRN2", target_bir_lowering=False)
    dr = {"xin": nc.dram_tensor("xin", [D, T], F32, kind="ExternalInput").ap()}
    for (nm, shp, dt) in dins:
        dr[nm] = nc.dram_tensor(nm, shp, dt, kind="ExternalInput").ap()
    dr["xout"] = nc.dram_tensor("xout", [D, T], F32, kind="ExternalOutput").ap()
    for (nm, shp, dt, kind) in extra:
        dr[nm] = nc.dram_tensor(nm, shp, dt, kind=kind).ap()
    with ExitStack() as ctx:
        p = Prog(nc)
        PS = [ctx.enter_context(nc.psum_tensor(f"PS{k}", [128, TC], F32)) for k in range(8)]
        allocd = {}
        def palloc(name, shape, dt=F32):
            if name not in allocd:
                allocd[name] = ctx.enter_context(nc.sbuf_tensor("s_" + name, shape, dt))
            return allocd[name]
        emit(nc, p, palloc, PS, dr)
        p.build(ctx)
    return nc


def build_launchA():
    return _standalone(emit_A, _A_DINS)

def _acl(A_log):
    out = np.empty((128, 128), np.float32)
    for j2 in range(128):
        cc, j = j2 // 16, j2 % 16
        for pp in range(128):
            out[pp, j2] = A_log[cc * 128 + j * 8 + pp // 16, pp % 16]
    return out


def _prep_A(inputs, b):
    c = _CONSTS
    x = np.ascontiguousarray(np.asarray(inputs["L0_attn_x"][b]).T)
    return {
        "xin": x,
        "wdtT": np.ascontiguousarray(np.asarray(inputs["L0_attn_W_dt"]).reshape(1, D)),
        "bdt": np.ascontiguousarray(np.asarray(inputs["L0_attn_b_dt"]).reshape(ND, 128).T),
        "wxpT": np.ascontiguousarray(np.asarray(inputs["L0_attn_W_xp"]).T),
        "bxp": np.ascontiguousarray(np.asarray(inputs["L0_attn_b_xp"]).reshape(2, 16).T),
        "woutT": np.ascontiguousarray(np.asarray(inputs["L0_attn_W_out"]).T),
        "bout": np.ascontiguousarray(np.asarray(inputs["L0_attn_b_out"]).reshape(ND, 128).T),
        "Acl": _acl(np.asarray(inputs["L0_attn_A_log"])),
        "LR": c["LR"], "LS": c["LS"], "SB16": c["SB16"],
        "ones_col_bf": c["ones_col_bf"], "ones_row_f": c["ones_row_f"], "cst": c["cst"],
    }


def kernel_unfused(**inputs):
    cores = list(range(NCORE))
    resA = run_bass_kernel_spmd(build_launchA(), [_prep_A(inputs, b) for b in cores], core_ids=cores)
    x1 = [np.asarray(r["xout"]) for r in resA.results]
    resB = run_bass_kernel_spmd(build_launchB(), [_prep_B(inputs, b, x1[b]) for b in cores], core_ids=cores)
    x2 = [np.asarray(r["xout"]) for r in resB.results]
    resC = run_bass_kernel_spmd(build_launchC(), [_prep_C(inputs, b, x2[b]) for b in cores], core_ids=cores)
    x3 = [np.asarray(r["xout"]) for r in resC.results]
    resD1 = run_bass_kernel_spmd(build_launchD1(), [_prep_D1(inputs, b, x3[b]) for b in cores], core_ids=cores)
    hds = [np.asarray(r["hd"]) for r in resD1.results]
    resD2 = run_bass_kernel_spmd(build_launchD2(), [_prep_D2(inputs, b, x3[b], hds[b]) for b in cores], core_ids=cores)
    out = np.stack([np.asarray(r["xout"]).T for r in resD2.results])
    return out.astype(np.float32)



SLAB = 1024
NSLAB = T // SLAB
HB = 512
NHB = 4096 // HB
NHC = HB // 128


def emit_norm2(p, sb, io, PS, xcv, xc_tok, xhatb_v, banks=(6, 7, 4, 5), eps_col=3,
               rms=False, scale_col=None):
    kA, kB, kS1, kS2 = banks
    psA, psB, psS1, psS2 = PS[kA], PS[kB], PS[kS1], PS[kS2]
    t_sq = t_xb = None
    tsx = tsq = None
    if not rms:
        t_xb = p.op("vector", lambda e: e.tensor_copy(sb.xb[:], xcv), waits=xc_tok, sem="V")
    t_sq = p.op("scalar", lambda e: e.activation(sb.sq[:], xcv, AF.Square),
                waits=xc_tok, sem="S")
    if not rms:
        for c in range(ND):
            sp = (c == ND - 1)
            tsx = p.op("tensor", (lambda c=c, sp=sp: lambda e: e.matmul(
                psS1[0:1, :], sb.ones_col_bf[:], sb.xb[:, c, :], start=(c == 0), stop=sp))(),
                waits=tmax(t_xb, io.get(f"ps{kS1}_free")) if c == 0 else (), sem="T" if sp else None)
    for c in range(ND):
        sp = (c == ND - 1)
        tsq = p.op("tensor", (lambda c=c, sp=sp: lambda e: e.matmul(
            psS2[0:1, :], sb.ones_col_bf[:], sb.sq[:, c, :], start=(c == 0), stop=sp))(),
            waits=tmax(t_sq, io.get(f"ps{kS2}_free")) if c == 0 else (), sem="T" if sp else None)
    if rms:
        t2 = p.op("vector", lambda e: e.tensor_scalar_mul(sb.row_t1[:], psS2[0:1, :], 1.0 / D),
                  waits=(tsq,), sem="V")
        t5 = p.op("scalar", lambda e: e.activation(sb.row_t2[:], sb.row_t1[:], AF.Ln,
                                                   bias=sb.cst[0:1, eps_col:eps_col + 1]),
                  waits=(t2,), sem="S")
        t6 = p.op("scalar", lambda e: e.activation(sb.row_a[:], sb.row_t2[:], AF.Exp, scale=-0.5),
                  waits=(t5,), sem="S")
        io[f"ps{kS2}_free"] = t2
        tA = p.op("tensor", lambda e: e.matmul(psA[:], sb.ones_row_f[:], sb.row_a[:],
                                               start=True, stop=True),
                  waits=tmax(t6, io.get(f"ps{kA}_free")), sem="T")
        last = None
        for c in range(ND):
            if scale_col is not None:
                last = p.op("vector", (lambda c=c: lambda e: e.scalar_tensor_tensor(
                    xhatb_v[:, c, :], xcv[:, c, :], scale_col[:, c:c + 1], psA[:],
                    ALU.mult, ALU.mult))(),
                    waits=tmax(tA, xc_tok) if c == 0 else (), sem="V")
            else:
                last = p.op("vector", (lambda c=c: lambda e: e.tensor_mul(
                    xhatb_v[:, c, :], xcv[:, c, :], psA[:]))(),
                    waits=tmax(tA, xc_tok) if c == 0 else (), sem="V")
        io[f"ps{kA}_free"] = last
        return last
    t1 = p.op("vector", lambda e: e.tensor_scalar_mul(sb.row_nm[:], psS1[0:1, :], -1.0 / D),
              waits=(tsx,), sem="V")
    t2 = p.op("vector", lambda e: e.tensor_scalar_mul(sb.row_t1[:], psS2[0:1, :], 1.0 / D),
              waits=(tsq,), sem="V")
    t3 = p.op("vector", lambda e: e.tensor_mul(sb.row_t2[:], sb.row_nm[:], sb.row_nm[:]),
              waits=(t1,), sem="V")
    t4 = p.op("vector", lambda e: e.tensor_sub(sb.row_t1[:], sb.row_t1[:], sb.row_t2[:]),
              waits=(t2, t3), sem="V")
    t5 = p.op("scalar", lambda e: e.activation(sb.row_t2[:], sb.row_t1[:], AF.Ln,
                                               bias=sb.cst[0:1, eps_col:eps_col + 1]),
              waits=(t4,), sem="S")
    t6 = p.op("scalar", lambda e: e.activation(sb.row_a[:], sb.row_t2[:], AF.Exp, scale=-0.5),
              waits=(t5,), sem="S")
    t7 = p.op("vector", lambda e: e.tensor_mul(sb.row_b[:], sb.row_nm[:], sb.row_a[:]),
              waits=(t6,), sem="V")
    tA = p.op("tensor", lambda e: e.matmul(psA[:], sb.ones_row_f[:], sb.row_a[:], start=True, stop=True),
              waits=tmax(t6, io.get(f"ps{kA}_free")), sem="T")
    tB = p.op("tensor", lambda e: e.matmul(psB[:], sb.ones_row_f[:], sb.row_b[:], start=True, stop=True),
              waits=tmax(t7, io.get(f"ps{kB}_free")), sem="T")
    last = None
    for c in range(ND):
        tm_ = p.op("vector", (lambda c=c: lambda e: e.tensor_mul(sb.ln_scr[:], xcv[:, c, :], psA[:]))(),
                   waits=tmax(tA, xc_tok, io.get("ln_scr_free")) if c == 0 else (io.get("ln_scr_free"),),
                   sem="V")
        last = p.op("vector", (lambda c=c: lambda e: e.tensor_add(
            xhatb_v[:, c, :], sb.ln_scr[:], psB[:]))(),
            waits=tmax(tm_, (tB,) if c == 0 else None), sem="V")
        io["ln_scr_free"] = last
    io[f"ps{kS1}_free"] = t1
    io[f"ps{kS2}_free"] = t2
    io[f"ps{kA}_free"] = last
    io[f"ps{kB}_free"] = last
    return last


def emit_B(nc, p, palloc, PS, dr):
    xin = dr["xin"]
    xout = dr["xout"]

    with _nullctx():
        sb = _SB()
        def alloc(name, shape, dt=F32):
            t = palloc(name, shape, dt)
            setattr(sb, name, t)
            return t
        alloc("xc", [128, ND, SLAB])
        alloc("xhatb", [128, ND, SLAB], BF16)
        alloc("xhatw", [128, ND, SLAB], BF16)
        alloc("out_acc", [128, ND, SLAB])
        alloc("xb", [128, ND, TC], BF16)
        alloc("sq", [128, ND, TC], BF16)
        alloc("ln_scr", [128, TC])
        for r in ("row_nm", "row_t1", "row_t2", "row_a", "row_b"):
            alloc(r, [1, TC])
        alloc("wg_bf", [128, ND, 8], BF16)
        alloc("bg_bc", [128, 8])
        alloc("rsc", [128, 48])
        alloc("w_T8", [8, SLAB])
        alloc("wbc", [128, SLAB])
        alloc("W1b0", [128, ND, HB], BF16); alloc("W1b1", [128, ND, HB], BF16)
        alloc("W2b0", [128, ND, HB], BF16); alloc("W2b1", [128, ND, HB], BF16)
        alloc("W3b0", [128, NHC, D], BF16); alloc("W3b1", [128, NHC, D], BF16)
        alloc("sg0", [128, TC], BF16); alloc("sg1", [128, TC], BF16)
        alloc("h0", [128, NHC, TC], BF16); alloc("h1", [128, NHC, TC], BF16)
        alloc("identf", [128, 128])
        alloc("E8", [8, 1024])
        alloc("ones_col_bf", [128, 1], BF16)
        alloc("ones_row_f", [1, 128])
        alloc("cst", [128, 8])


        wtoks = []
        for nm_ in ("identf", "E8", "ones_col_bf", "ones_row_f", "cst", "bgrow_"):
            pass
        for nm_ in ("identf", "E8", "ones_col_bf", "ones_row_f", "cst"):
            wtoks.append(p.dma("sync", (lambda nm_=nm_: lambda e: e.dma_start(
                out=getattr(sb, nm_)[:], in_=dr[nm_]))(), sem="wld"))
        wtoks.append(p.dma("gpsimd", lambda e: e.dma_start(
            out=sb.wg_bf[:], in_=dr["wgT"].rearrange("(c p) e -> p c e", p=128)), sem="wldg"))
        wtok = tmax(*wtoks)
        tbg0 = p.dma("sync", lambda e: e.dma_start(out=sb.rsc[0:1, 0:8], in_=dr["bgrow"]), sem="wld")
        tbgm = p.op("tensor", lambda e: e.matmul(PS[6][:, 0:8], sb.ones_row_f[:], sb.rsc[0:1, 0:8],
                                                 start=True, stop=True),
                    waits=tmax(wtok, tbg0), sem="T")
        tbg = p.op("vector", lambda e: e.tensor_copy(sb.bg_bc[:], PS[6][:, 0:8]), waits=(tbgm,), sem="V")
        wtok = tmax(wtok, tbg)
        io = {"ps6_free": tbg}

        store_toks = []
        xc_consumed = ()
        acc_consumed = ()
        wslot_free = {0: (), 1: ()}
        for s in range(NSLAB):
            ldx = p.dma("sync", (lambda s=s: lambda e: e.dma_start(
                out=sb.xc[:], in_=xin[:, s * SLAB:(s + 1) * SLAB].rearrange("(c p) t -> p c t", p=128)))(),
                waits=xc_consumed, sem="xld")
            xc_tok = tmax(ldx, wtok)
            t_ms = p.op("vector", lambda e: e.memset(sb.out_acc[:].rearrange("p c t -> p (c t)"), 0.0),
                        waits=acc_consumed, sem="V")

            ln_last = None
            for q in range(SLAB // TC):
                ln_last = emit_norm2(p, sb, io, PS,
                                     sb.xc[:, :, q * TC:(q + 1) * TC], xc_tok,
                                     sb.xhatb[:, :, q * TC:(q + 1) * TC])

            wT_toks = []
            for tt in range(SLAB // 128):
                tlg = None
                for c in range(ND):
                    sp = (c == ND - 1)
                    tlg = p.op("tensor", (lambda c=c, tt=tt, sp=sp: lambda e: e.matmul(
                        PS[4][:, 0:8], sb.xhatb[:, c, tt * 128:(tt + 1) * 128], sb.wg_bf[:, c, :],
                        start=(c == 0), stop=sp))(),
                        waits=tmax(ln_last, io.get("ps4_free")) if c == 0 else (),
                        sem="T" if sp else None)
                lg = sb.rsc[:, 0:8]; msk = sb.rsc[:, 8:16]; tmp = sb.rsc[:, 16:24]
                e_ = sb.rsc[:, 24:32]; wraw = sb.rsc[:, 32:40]
                m1 = sb.rsc[:, 40:41]; m1n = sb.rsc[:, 41:42]; m2 = sb.rsc[:, 42:43]
                den = sb.rsc[:, 43:44]; dinv = sb.rsc[:, 44:45]; wt = sb.rsc[:, 0:8]
                ta = p.op("vector", (lambda lg=lg: lambda e: e.tensor_add(lg, PS[4][:, 0:8], sb.bg_bc[:]))(),
                          waits=tmax(tlg, io.get("rsc_free")), sem="V")
                io["ps4_free"] = ta
                tb = p.op("vector", (lambda: lambda e: e.reduce_max(m1, lg, axis=AX.X))(),
                          waits=(ta,), sem="V")
                tc_ = p.op("vector", (lambda: lambda e: e.tensor_single_scalar(msk, lg, m1[:, 0:1], ALU.is_equal))(),
                           waits=(tb,), sem="V")
                td = p.op("vector", (lambda: lambda e: e.tensor_scalar_mul(m1n, m1, -1.0))(),
                          waits=(tb,), sem="V")
                te_ = p.op("vector", (lambda: lambda e: e.scalar_tensor_tensor(
                    tmp, msk, -1e9, lg, ALU.mult, ALU.add))(), waits=(tc_,), sem="V")
                tf = p.op("vector", (lambda: lambda e: e.reduce_max(m2, tmp, axis=AX.X))(),
                          waits=(te_,), sem="V")
                tg = p.op("scalar", (lambda: lambda e: e.activation(e_, lg, AF.Exp, bias=m1n[:, 0:1]))(),
                          waits=(td, te_), sem="S")
                th = p.op("vector", (lambda: lambda e: e.tensor_single_scalar(msk, lg, m2[:, 0:1], ALU.is_ge))(),
                          waits=(tf, tg), sem="V")
                ti = p.op("vector", (lambda: lambda e: e.tensor_mul(wraw, e_, msk))(),
                          waits=(th, tg), sem="V")
                tj = p.op("vector", (lambda: lambda e: e.reduce_sum(den, wraw, axis=AX.X))(),
                          waits=(ti,), sem="V")
                tk = p.op("vector", (lambda: lambda e: e.reciprocal(dinv, den))(),
                          waits=(tj,), sem="V")
                tl = p.op("vector", (lambda: lambda e: e.tensor_single_scalar(wt, wraw, dinv[:, 0:1], ALU.mult))(),
                          waits=(tk,), sem="V")
                ttr = p.op("tensor", (lambda wt=wt: lambda e: e.transpose(
                    PS[5][0:8, 0:128], wt, sb.identf[:]))(),
                    waits=tmax(tl, io.get("ps5_free")), sem="T")
                tcp = p.op("vector", (lambda tt=tt: lambda e: e.tensor_copy(
                    sb.w_T8[:, tt * 128:(tt + 1) * 128], PS[5][0:8, 0:128]))(),
                    waits=(ttr,), sem="V")
                io["ps5_free"] = tcp
                io["rsc_free"] = ttr
                wT_toks.append(tcp)
            wT_tok = tmax(*wT_toks)

            for ei in range(8):
                wbc_toks = []
                for q in range(SLAB // TC):
                    tw1 = p.op("tensor", (lambda ei=ei, q=q: lambda e: e.matmul(
                        PS[6][:], sb.E8[0:8, ei * 128:(ei + 1) * 128],
                        sb.w_T8[0:8, q * TC:(q + 1) * TC], start=True, stop=True))(),
                        waits=tmax(wT_tok, io.get("ps6_free")), sem="T")
                    tw2 = p.op("vector", (lambda q=q: lambda e: e.tensor_copy(
                        sb.wbc[:, q * TC:(q + 1) * TC], PS[6][:]))(),
                        waits=tmax(tw1, io.get("wbc_free")), sem="V")
                    io["ps6_free"] = tw2
                    wbc_toks.append(tw2)
                twb = tmax(*wbc_toks)
                txw_toks = []
                for c in range(ND):
                    txw = p.op("vector", (lambda c=c: lambda e: e.tensor_mul(
                        sb.xhatw[:, c, :], sb.xhatb[:, c, :], sb.wbc[:]))(),
                        waits=tmax(twb, io.get("xhatw_free")) if c == 0 else (), sem="V")
                    txw_toks.append(txw)
                txw_tok = txw_toks[-1]

                for hb in range(NHB):
                    sl = hb % 2
                    W1b = sb.W1b0 if sl == 0 else sb.W1b1
                    W2b = sb.W2b0 if sl == 0 else sb.W2b1
                    W3b = sb.W3b0 if sl == 0 else sb.W3b1
                    t_w1 = p.dma("gpsimd", (lambda ei=ei, hb=hb, W1b=W1b: lambda e: e.dma_start(
                        out=W1b[:], in_=dr["W1T"][ei, :, hb * HB:(hb + 1) * HB]
                        .rearrange("(c p) h -> p c h", p=128)))(),
                        waits=wslot_free[sl], sem=f"w1ld{sl}")
                    t_w2 = p.dma("gpsimd", (lambda ei=ei, hb=hb, W2b=W2b: lambda e: e.dma_start(
                        out=W2b[:], in_=dr["W2T"][ei, :, hb * HB:(hb + 1) * HB]
                        .rearrange("(c p) h -> p c h", p=128)))(),
                        sem=f"w2ld{sl}")
                    t_w3 = p.dma("gpsimd", (lambda ei=ei, hb=hb, W3b=W3b: lambda e: e.dma_start(
                        out=W3b[:], in_=dr["W3T"][ei, hb * HB:(hb + 1) * HB, :]
                        .rearrange("(c p) o -> p c o", p=128)))(),
                        sem=f"w3ld{sl}")
                    wrd = []
                    for tcq in range(SLAB // TC):
                        h_sl = tcq % 2
                        hbuf = sb.h0 if h_sl == 0 else sb.h1
                        h_toks = []
                        for hc in range(NHC):
                            gr = hc % 2
                            ps_g = PS[0 + gr]
                            ps_u = PS[2 + gr]
                            sg = sb.sg0 if gr == 0 else sb.sg1
                            tgm = None
                            for c in range(ND):
                                sp = (c == ND - 1)
                                tgm = p.op("tensor", (lambda c=c, hc=hc, tcq=tcq, sp=sp, ps_g=ps_g, W1b=W1b: lambda e: e.matmul(
                                    ps_g[:], W1b[:, c, hc * 128:(hc + 1) * 128],
                                    sb.xhatb[:, c, tcq * TC:(tcq + 1) * TC], start=(c == 0), stop=sp))(),
                                    waits=tmax(t_w1, ln_last, io.get(f"ps{0 + gr}_free")) if c == 0 else (),
                                    sem="T" if sp else None)
                            tsl = p.op("scalar", (lambda sg=sg, ps_g=ps_g: lambda e: e.activation(
                                sg[:], ps_g[:], AF.Silu))(),
                                waits=tmax(tgm, io.get(f"sg{gr}_free")), sem="S")
                            io[f"ps{0 + gr}_free"] = tsl
                            tum = None
                            for c in range(ND):
                                sp = (c == ND - 1)
                                tum = p.op("tensor", (lambda c=c, hc=hc, tcq=tcq, sp=sp, ps_u=ps_u, W2b=W2b: lambda e: e.matmul(
                                    ps_u[:], W2b[:, c, hc * 128:(hc + 1) * 128],
                                    sb.xhatw[:, c, tcq * TC:(tcq + 1) * TC], start=(c == 0), stop=sp))(),
                                    waits=tmax(t_w2, txw_tok, io.get(f"ps{2 + gr}_free")) if c == 0 else (),
                                    sem="T" if sp else None)
                            wrd.append(tgm); wrd.append(tum)
                            thm = p.op("vector", (lambda hbuf=hbuf, hc=hc, sg=sg, ps_u=ps_u: lambda e: e.tensor_mul(
                                hbuf[:, hc, :], sg[:], ps_u[:]))(),
                                waits=tmax(tsl, tum, io.get(f"h{h_sl}_free")), sem="V")
                            io[f"ps{2 + gr}_free"] = thm
                            io[f"sg{gr}_free"] = thm
                            h_toks.append(thm)
                        acc_toks = []
                        for o in range(ND):
                            dr_ = o % 2
                            ps_d = PS[4 + dr_]
                            tdm = None
                            for hc in range(NHC):
                                sp = (hc == NHC - 1)
                                tdm = p.op("tensor", (lambda o=o, hc=hc, sp=sp, ps_d=ps_d, W3b=W3b, hbuf=hbuf: lambda e: e.matmul(
                                    ps_d[:], W3b[:, hc, o * 128:(o + 1) * 128], hbuf[:, hc, :],
                                    start=(hc == 0), stop=sp))(),
                                    waits=tmax(t_w3, h_toks[-1], io.get(f"ps{4 + dr_}_free")) if hc == 0 else (),
                                    sem="T" if sp else None)
                            tac = p.op("vector", (lambda o=o, tcq=tcq, ps_d=ps_d: lambda e: e.tensor_add(
                                sb.out_acc[:, o, tcq * TC:(tcq + 1) * TC],
                                sb.out_acc[:, o, tcq * TC:(tcq + 1) * TC], ps_d[:]))(),
                                waits=tmax(tdm, t_ms), sem="V")
                            io[f"ps{4 + dr_}_free"] = tac
                            wrd.append(tdm)
                            acc_toks.append(tac)
                        io[f"h{h_sl}_free"] = acc_toks[-1]
                    wslot_free[sl] = tmax(*wrd)
                io["xhatw_free"] = tmax(wslot_free[0], wslot_free[1])
                io["wbc_free"] = tmax(wslot_free[0], wslot_free[1])

            res_toks = []
            last_acc = io.get("ps4_free"), io.get("ps5_free")
            for o in range(ND):
                tr = p.op("vector", (lambda o=o: lambda e: e.tensor_add(
                    sb.xc[:, o, :], sb.xc[:, o, :], sb.out_acc[:, o, :]))(),
                    waits=tmax(*last_acc) if o == 0 else (), sem="V")
                res_toks.append(tr)
            tst_d = p.dma("sync", (lambda s=s: lambda e: e.dma_start(
                out=xout[:, s * SLAB:(s + 1) * SLAB].rearrange("(c p) t -> p c t", p=128),
                in_=sb.xc[:]))(), waits=tuple(res_toks), sem="xst")
            store_toks.append(tst_d)
            xc_consumed = tmax(tst_d, *res_toks)
            acc_consumed = tmax(*res_toks)

        p.op("sync", lambda e: None, waits=tmax(*store_toks))
    return None



_B_DINS = [("wgT", [D, 8], F32), ("bgrow", [1, 8], F32), ("W1T", [8, D, 4096], F32),
           ("W2T", [8, D, 4096], F32), ("W3T", [8, 4096, D], F32),
           ("identf", [128, 128], F32), ("E8", [8, 1024], F32),
           ("ones_col_bf", [128, 1], BF16), ("ones_row_f", [1, 128], F32),
           ("cst", [128, 8], F32)]


def build_launchB():
    return _standalone(emit_B, _B_DINS)

def _prep_B(inputs, b, xin, cache={}):
    c = _CONSTS
    key = id(inputs.get("L0_ffn_Wgate"))
    if cache.get("key") != key:
        E8 = np.zeros((8, 1024), np.float32)
        for ei in range(8):
            E8[ei, ei * 128:(ei + 1) * 128] = 1.0
        cache.clear()
        cache.update(
            key=key,
            wgT=np.ascontiguousarray(np.asarray(inputs["L0_ffn_wg"]).T),
            bgrow=np.ascontiguousarray(np.asarray(inputs["L0_ffn_bg"]).reshape(1, 8)),
            W1T=np.ascontiguousarray(np.asarray(inputs["L0_ffn_Wgate"]).transpose(0, 2, 1)),
            W2T=np.ascontiguousarray(np.asarray(inputs["L0_ffn_Wup"]).transpose(0, 2, 1)),
            W3T=np.ascontiguousarray(np.asarray(inputs["L0_ffn_Wdown"]).transpose(0, 2, 1)),
            identf=np.eye(128, dtype=np.float32), E8=E8)
    return {
        "xin": xin,
        "wgT": cache["wgT"], "bgrow": cache["bgrow"],
        "W1T": cache["W1T"], "W2T": cache["W2T"], "W3T": cache["W3T"],
        "identf": cache["identf"], "E8": cache["E8"],
        "ones_col_bf": c["ones_col_bf"], "ones_row_f": c["ones_row_f"], "cst": c["cst"],
    }



NKBMAX = T // 128


def emit_C(nc, p, palloc, PS, dr):
    xin = dr["xin"]
    xout = dr["xout"]
    KTd = dr["KTd"]

    with _nullctx():
        sb = _SB()
        def alloc(name, shape, dt=F32):
            t = palloc(name, shape, dt)
            setattr(sb, name, t)
            return t
        alloc("xc", [128, ND, TC])
        alloc("xb", [128, ND, TC], BF16)
        alloc("sq", [128, ND, TC], BF16)
        alloc("ln_scr", [128, TC])
        for r in ("row_nm", "row_t1", "row_t2", "row_a", "row_b"):
            alloc(r, [1, TC])
        alloc("xhatb", [128, ND, TC], BF16)
        alloc("kt0", [128, TC], BF16); alloc("kt1", [128, TC], BF16)
        alloc("kbuf", [128, T], BF16)
        alloc("vtok", [128, NKBMAX, 16, 65], BF16)
        alloc("QT", [128, ND, TC], BF16)
        alloc("att", [128, ND, TC], BF16)
        alloc("qsb", [128, TC], BF16)
        alloc("t1s", [128, TC], BF16)
        alloc("t2s", [128, TC], BF16)
        alloc("sqk", [128, TC], BF16)
        alloc("mk2", [1, 16])
        alloc("q2", [1, TC]); alloc("mrow", [1, TC])
        alloc("mcol", [128, 1])
        alloc("linv", [1, TC]); alloc("lbc", [64, TC])
        alloc("Pt0", [128, 2 * TC], BF16); alloc("Pt1", [128, 2 * TC], BF16)
        alloc("wk0", [128, ND, 128], BF16); alloc("wk1", [128, ND, 128], BF16)
        alloc("wv0", [128, ND, TC], BF16)
        alloc("wo0", [128, ND, 128], BF16); alloc("wo1", [128, ND, 128], BF16)
        alloc("cos_bc", [128, T], BF16)
        alloc("sin_sgn", [128, T], BF16)
        alloc("Rswap", [128, 128], BF16)
        alloc("mask01", [128, 4 * TC], BF16)
        alloc("ones_col_bf", [128, 1], BF16)
        alloc("ones_row_f", [1, 128])
        alloc("cst", [128, 8])

        PSA = [PS[0], PS[1], PS[4], PS[5]]
        PS6 = PS[6]
        PS7 = PS[7]
        psl = {6: PS6, 7: PS7, 4: PS[4], 5: PS[5]}

        wtoks = []
        for nm_ in ("cos_bc", "sin_sgn", "Rswap", "mask01", "ones_col_bf", "ones_row_f", "cst"):
            wtoks.append(p.dma("sync", (lambda nm_=nm_: lambda e: e.dma_start(
                out=getattr(sb, nm_)[:], in_=dr[nm_]))(), sem="wld"))
        wtok = tmax(*wtoks)
        t_mk = p.op("vector", lambda e: e.memset(sb.mk2[:], 0.0), sem="V")
        t_vo = p.op("vector", lambda e: e.memset(sb.vtok[:, :, :, 64:65], 1.0), sem="V")
        io = {}

        def emit_rope(dst_view, src_ps, cos_v, sin_v, rot_ps, rot_key, scale, waits_in,
                      dst_war=()):
            tev = p.op("scalar", (lambda: lambda e: e.activation(
                sb.qsb[:], src_ps[:], AF.Copy, scale=scale))(),
                waits=tmax(waits_in, io.get("qsb_free")), sem="S")
            trm = p.op("tensor", (lambda: lambda e: e.matmul(
                rot_ps[:], sb.Rswap[:], sb.qsb[:], start=True, stop=True))(),
                waits=tmax(tev, io.get(rot_key)), sem="T")
            tp1 = p.op("gpsimd", (lambda: lambda e: e.tensor_tensor(
                sb.t1s[:], sb.qsb[:], cos_v, ALU.mult))(),
                waits=tmax(tev, io.get("t1s_free")), sem="G")
            tp2 = p.op("vector", (lambda: lambda e: e.tensor_tensor(
                sb.t2s[:], rot_ps[:], sin_v, ALU.mult))(),
                waits=tmax(trm, io.get("t2s_free")), sem="V")
            io[rot_key] = tp2
            tp3 = p.op("vector", (lambda dst_view=dst_view: lambda e: e.tensor_add(
                dst_view, sb.t1s[:], sb.t2s[:]))(), waits=tmax(tp1, tp2, dst_war), sem="V")
            io["qsb_free"] = tp1
            io["t1s_free"] = tp3
            io["t2s_free"] = tp3
            return tp3, tev

        wkslot = {0: (), 1: ()}
        wvslot = {0: (), 1: ()}
        ktslot = {0: (), 1: ()}
        xc_consumed = ()
        vtok_done = []
        for i in range(NTC):
            tsl = (i * TC, (i + 1) * TC)
            ldx = p.dma("sync", (lambda i=i: lambda e: e.dma_start(
                out=sb.xc[:], in_=xin[:, i * TC:(i + 1) * TC].rearrange("(c p) t -> p c t", p=128)))(),
                waits=xc_consumed, sem="xld")
            xc_tok = tmax(ldx, wtok)
            ln_last = emit_norm2(p, sb, io, psl, sb.xc[:], xc_tok, sb.xhatb[:])

            for rc in range(ND):
                sl = rc % 2
                wk = sb.wk0 if sl == 0 else sb.wk1
                kt = sb.kt0 if sl == 0 else sb.kt1
                twk = p.dma("gpsimd", (lambda rc=rc, wk=wk: lambda e: e.dma_start(
                    out=wk[:], in_=dr["qkvT"][:, D + rc * 128:D + (rc + 1) * 128]
                    .rearrange("(c p) o -> p c o", p=128)))(),
                    waits=wkslot[sl], sem=f"wkld{sl}")
                ps_k = PSA[sl]
                tkm = None
                for c in range(ND):
                    sp = (c == ND - 1)
                    tkm = p.op("tensor", (lambda c=c, sp=sp, wk=wk, ps_k=ps_k: lambda e: e.matmul(
                        ps_k[:], wk[:, c, :], sb.xhatb[:, c, :], start=(c == 0), stop=sp))(),
                        waits=tmax(twk, ln_last, io.get(f"pa{sl}")) if c == 0 else (),
                        sem="T" if sp else None)
                wkslot[sl] = (tkm,)
                rope_tok, tev = emit_rope(kt[:], ps_k,
                                          sb.cos_bc[:, tsl[0]:tsl[1]], sb.sin_sgn[:, tsl[0]:tsl[1]],
                                          PSA[2 + sl], f"ps{4 + sl}_free", 0.125,
                                          (tkm,), dst_war=ktslot[sl])
                io[f"pa{sl}"] = tev
                tkst = p.dma("sync", (lambda rc=rc, i=i, kt=kt: lambda e: e.dma_start(
                    out=KTd[rc * 128:(rc + 1) * 128, i * TC:(i + 1) * TC], in_=kt[:]))(),
                    waits=(rope_tok,), sem=f"ktst{sl}")
                ktd_last = tkst
                tsq = p.op("scalar", (lambda kt=kt: lambda e: e.activation(
                    sb.sqk[:], kt[:], AF.Square))(),
                    waits=tmax(rope_tok, io.get("sqk_free")), sem="S")
                ktslot[sl] = tmax(tkst, tsq)
                for hh in range(2):
                    base = hh * 64
                    psn = PS6 if hh == 0 else PS7
                    rowt = sb.row_t1 if hh == 0 else sb.row_t2
                    tnm = p.op("tensor", (lambda base=base, psn=psn: lambda e: e.matmul(
                        psn[0:1, :], sb.ones_col_bf[base:base + 64, :], sb.sqk[base:base + 64, :],
                        start=True, stop=True))(),
                        waits=tmax(tsq, io.get(f"ps{6 + hh}_free")), sem="T")
                    trm_ = p.op("vector", (lambda psn=psn, rowt=rowt: lambda e: e.reduce_max(
                        rowt[0:1, 0:1], psn[0:1, :], axis=AX.X))(), waits=(tnm,), sem="V")
                    io[f"ps{6 + hh}_free"] = trm_
                    h = 2 * rc + hh
                    tmk = p.op("vector", (lambda h=h, rowt=rowt: lambda e: e.tensor_tensor(
                        sb.mk2[0:1, h:h + 1], sb.mk2[0:1, h:h + 1], rowt[0:1, 0:1], ALU.max))(),
                        waits=(trm_, t_mk), sem="V")
                    io["sqk_free"] = tmk

            for vc in range(2):
                wv = sb.wv0
                twv = p.dma("gpsimd", (lambda vc=vc, wv=wv: lambda e: e.dma_start(
                    out=wv[:], in_=dr["qkvT"][:, 2 * D + vc * TC:2 * D + (vc + 1) * TC]
                    .rearrange("(c p) o -> p c o", p=128)))(),
                    waits=tmax(wvslot[0], wvslot[1]), sem="wvld")
                for ttt in range(TC // 128):
                    ps_v = PSA[ttt % 2]
                    tvm = None
                    for c in range(ND):
                        sp = (c == ND - 1)
                        tvm = p.op("tensor", (lambda c=c, ttt=ttt, sp=sp, wv=wv, ps_v=ps_v: lambda e: e.matmul(
                            ps_v[:], sb.xhatb[:, c, ttt * 128:(ttt + 1) * 128], wv[:, c, :],
                            start=(c == 0), stop=sp))(),
                            waits=tmax(twv, ln_last, io.get(f"pa{ttt % 2}")) if c == 0 else (),
                            sem="T" if sp else None)
                    wvslot[ttt % 2] = (tvm,)
                    kb = i * 4 + ttt
                    tev2 = None
                    for hh in range(8):
                        h = vc * 8 + hh
                        tev2 = p.op("scalar", (lambda kb=kb, h=h, hh=hh, ps_v=ps_v: lambda e: e.activation(
                            sb.vtok[:, kb, h, 0:64], ps_v[:, hh * 64:(hh + 1) * 64], AF.Copy))(),
                            waits=(tvm,) if hh == 0 else (), sem="S")
                    io[f"pa{ttt % 2}"] = tev2
                    vtok_done.append(tev2)
            xc_consumed = (ln_last,)
        vtok_tok = tmax(*vtok_done, t_vo)
        kt_tok = io.get("sqk_free")
        ktd_tok = tmax(("ktst0", p.counts.get("ktst0", 0)), ("ktst1", p.counts.get("ktst1", 0)))

        store_toks = []
        kbslot = {0: (), 1: ()}
        for qc in range(NTC):
            tsl = (qc * TC, (qc + 1) * TC)
            ldx = p.dma("sync", (lambda qc=qc: lambda e: e.dma_start(
                out=sb.xc[:], in_=xin[:, qc * TC:(qc + 1) * TC].rearrange("(c p) t -> p c t", p=128)))(),
                waits=xc_consumed, sem="xld")
            xc_tok = tmax(ldx, wtok)
            ln_last = emit_norm2(p, sb, io, psl, sb.xc[:], xc_tok, sb.xhatb[:])

            q_toks = []
            for rc in range(ND):
                sl = rc % 2
                wk = sb.wk0 if sl == 0 else sb.wk1
                twk = p.dma("gpsimd", (lambda rc=rc, wk=wk: lambda e: e.dma_start(
                    out=wk[:], in_=dr["qkvT"][:, rc * 128:(rc + 1) * 128]
                    .rearrange("(c p) o -> p c o", p=128)))(),
                    waits=wkslot[sl], sem=f"wkld{sl}")
                ps_k = PSA[sl]
                tkm = None
                for c in range(ND):
                    sp = (c == ND - 1)
                    tkm = p.op("tensor", (lambda c=c, sp=sp, wk=wk, ps_k=ps_k: lambda e: e.matmul(
                        ps_k[:], wk[:, c, :], sb.xhatb[:, c, :], start=(c == 0), stop=sp))(),
                        waits=tmax(twk, ln_last, io.get(f"pa{sl}")) if c == 0 else (),
                        sem="T" if sp else None)
                wkslot[sl] = (tkm,)
                rope_tok, tev = emit_rope(sb.QT[:, rc, :], ps_k,
                                          sb.cos_bc[:, tsl[0]:tsl[1]], sb.sin_sgn[:, tsl[0]:tsl[1]],
                                          PSA[2 + sl], f"ps{4 + sl}_free", 1.0, (tkm,))
                io[f"pa{sl}"] = tev
                q_toks.append(rope_tok)
            q_tok = tmax(*q_toks)

            att_toks = []
            nkb = (qc + 1) * 4
            for h in range(16):
                rc, hh = h // 2, h % 2
                base = hh * 64
                ksl = h % 2
                kbuf = sb.kbuf[ksl * 64:(ksl + 1) * 64, :]
                tkl = p.dma("sync", (lambda h=h, nkb=nkb, ksl=ksl: lambda e: e.dma_start(
                    out=sb.kbuf[ksl * 64:(ksl + 1) * 64, 0:nkb * 128],
                    in_=KTd[h * 64:(h + 1) * 64, 0:nkb * 128]))(),
                    waits=tmax(kbslot[ksl], ktd_tok if qc == 0 and h < 2 else None),
                    sem=f"kbl{ksl}")
                tsq = p.op("scalar", (lambda rc=rc, base=base: lambda e: e.activation(
                    sb.sqk[0:64, :], sb.QT[base:base + 64, rc, :], AF.Square))(),
                    waits=tmax(q_tok, io.get("sqk_free")), sem="S")
                tq2 = p.op("tensor", (lambda: lambda e: e.matmul(
                    PS6[0:1, :], sb.ones_col_bf[0:64, :], sb.sqk[0:64, :], start=True, stop=True))(),
                    waits=tmax(tsq, io.get("ps6_free")), sem="T")
                io["sqk_free"] = tq2
                tm1 = p.op("vector", (lambda h=h: lambda e: e.tensor_single_scalar(
                    sb.q2[:], PS6[0:1, :], sb.mk2[0:1, h:h + 1], ALU.mult))(),
                    waits=tmax(tq2, kt_tok), sem="V")
                io["ps6_free"] = tm1
                tm2 = p.op("scalar", (lambda: lambda e: e.activation(
                    sb.mrow[:], sb.q2[:], AF.Ln))(), waits=(tm1,), sem="S")
                tm3 = p.op("scalar", (lambda: lambda e: e.activation(
                    sb.mrow[:], sb.mrow[:], AF.Exp, scale=0.5))(), waits=(tm2,), sem="S")
                tm4 = p.op("vector", (lambda: lambda e: e.reduce_max(
                    sb.row_t1[0:1, 0:1], sb.mrow[:], axis=AX.X, negate=True))(),
                    waits=tmax(tm3, io.get("rt1_free")), sem="V")
                tm5 = p.op("tensor", (lambda: lambda e: e.matmul(
                    PS7[:, 0:1], sb.ones_row_f[:], sb.row_t1[0:1, 0:1], start=True, stop=True))(),
                    waits=tmax(tm4, io.get("ps7_free")), sem="T")
                io["rt1_free"] = tm5
                tm6 = p.op("vector", (lambda: lambda e: e.tensor_copy(sb.mcol[:], PS7[:, 0:1]))(),
                           waits=tmax(tm5, io.get("mcol_free")), sem="V")
                io["ps7_free"] = tm6

                ps_O = PSA[h % 2]
                tOm = None
                kread = []
                for kb in range(nkb):
                    u = kb % 2
                    ps_s = PS[2 + u]
                    ptk = (kb // 2) % 2
                    pt = sb.Pt0 if ptk == 0 else sb.Pt1
                    tsm = p.op("tensor", (lambda kb=kb, rc=rc, base=base, ksl=ksl, ps_s=ps_s, u=u: lambda e: e.matmul(
                        ps_s[:], sb.kbuf[ksl * 64:ksl * 64 + 64, kb * 128:(kb + 1) * 128],
                        sb.QT[base:base + 64, rc, :], start=True, stop=True))(),
                        waits=tmax(io.get(f"psS{u}_free"), q_tok, tkl), sem="T")
                    kread.append(tsm)
                    texp = p.op("scalar", (lambda pt=pt, u=u, ps_s=ps_s: lambda e: e.activation(
                        pt[:, u * TC:(u + 1) * TC], ps_s[:], AF.Exp, bias=sb.mcol[:, 0:1]))(),
                        waits=tmax(tsm, tm6,
                                   io.get(f"pt_free{ptk}") if u == 0 else None), sem="S")
                    io[f"psS{u}_free"] = texp
                    tmask = texp
                    off = kb * 128 - qc * TC
                    if 0 <= off < TC:
                        oi = off // 128
                        tmask = p.op("gpsimd", (lambda pt=pt, u=u, oi=oi: lambda e: e.tensor_tensor(
                            pt[:, u * TC:(u + 1) * TC], pt[:, u * TC:(u + 1) * TC],
                            sb.mask01[:, oi * TC:(oi + 1) * TC], ALU.mult))(),
                            waits=(texp,), sem="G")
                    tOm = p.op("tensor", (lambda kb=kb, u=u, h=h, pt=pt, ps_O=ps_O, nkb=nkb: lambda e: e.matmul(
                        ps_O[0:65, :], sb.vtok[:, kb, h, :], pt[:, u * TC:(u + 1) * TC],
                        start=(kb == 0), stop=(kb == nkb - 1)))(),
                        waits=tmax(tmask, vtok_tok,
                                   (io.get(f"pa{h % 2}") if kb == 0 else None)),
                        sem="T")
                    if u == 1 or kb == nkb - 1:
                        io[f"pt_free{ptk}"] = tOm
                kbslot[ksl] = tmax(*kread)
                tli = p.op("vector", (lambda ps_O=ps_O: lambda e: e.reciprocal(
                    sb.linv[:], ps_O[64:65, :]))(), waits=tmax(tOm, io.get("linv_free")), sem="V")
                tlb = p.op("tensor", (lambda: lambda e: e.matmul(
                    PSA[2 + h % 2][0:64, :], sb.ones_row_f[0:1, 0:64], sb.linv[:],
                    start=True, stop=True))(),
                    waits=tmax(tli, io.get(f"ps{4 + h % 2}_free")), sem="T")
                tle = p.op("scalar", (lambda: lambda e: e.activation(
                    sb.lbc[:], PSA[2 + h % 2][0:64, :], AF.Copy))(),
                    waits=tmax(tlb, io.get("lbc_free")), sem="S")
                io[f"ps{4 + h % 2}_free"] = tle
                tat = p.op("vector", (lambda h=h, rc=rc, base=base, ps_O=ps_O: lambda e: e.tensor_mul(
                    sb.att[base:base + 64, rc, :], ps_O[0:64, :], sb.lbc[:]))(),
                    waits=(tle,), sem="V")
                io[f"pa{h % 2}"] = tat
                io["lbc_free"] = tat
                io["linv_free"] = tlb
                io["mcol_free"] = texp
                att_toks.append(tat)
            att_tok = tmax(*att_toks)

            res_toks = []
            for o in range(ND):
                wsl = o % 2
                wob = sb.wo0 if wsl == 0 else sb.wo1
                two = p.dma("gpsimd", (lambda o=o, wob=wob: lambda e: e.dma_start(
                    out=wob[:], in_=dr["woT"][:, o * 128:(o + 1) * 128]
                    .rearrange("(c p) o2 -> p c o2", p=128)))(),
                    waits=io.get(f"wo{wsl}_rd", ()), sem=f"wold{wsl}")
                ps_o = PS6 if o % 2 == 0 else PS7
                tpo = None
                for c in range(ND):
                    sp = (c == ND - 1)
                    tpo = p.op("tensor", (lambda o=o, c=c, sp=sp, ps_o=ps_o, wob=wob: lambda e: e.matmul(
                        ps_o[:], wob[:, c, :], sb.att[:, c, :],
                        start=(c == 0), stop=sp))(),
                        waits=tmax(att_tok, two, io.get(f"ps{6 + o % 2}_free")) if c == 0 else (),
                        sem="T" if sp else None)
                io[f"wo{wsl}_rd"] = (tpo,)
                tr = p.op("vector", (lambda o=o, ps_o=ps_o: lambda e: e.tensor_add(
                    sb.xc[:, o, :], sb.xc[:, o, :], ps_o[:]))(), waits=(tpo,), sem="V")
                io[f"ps{6 + o % 2}_free"] = tr
                res_toks.append(tr)
            tst_d = p.dma("sync", (lambda qc=qc: lambda e: e.dma_start(
                out=xout[:, qc * TC:(qc + 1) * TC].rearrange("(c p) t -> p c t", p=128),
                in_=sb.xc[:]))(), waits=tuple(res_toks), sem="xst")
            store_toks.append(tst_d)
            xc_consumed = tmax(tst_d, *res_toks)

        p.op("sync", lambda e: None, waits=tmax(*store_toks))
    return None



_C_DINS = [("qkvT", [D, 3 * D], F32), ("woT", [D, D], F32), ("cos_bc", [128, T], BF16),
           ("sin_sgn", [128, T], BF16), ("Rswap", [128, 128], BF16),
           ("mask01", [128, 4 * TC], BF16), ("ones_col_bf", [128, 1], BF16),
           ("ones_row_f", [1, 128], F32), ("cst", [128, 8], F32)]


def build_launchC():
    return _standalone(emit_C, _C_DINS, extra=[("KTd", [D, T], BF16, "Internal")])

def _mask01():
    m = np.zeros((128, 4 * TC), np.float32)
    for oi in range(4):
        off = oi * 128
        for pp in range(128):
            for f in range(TC):
                if off + pp <= f:
                    m[pp, oi * TC + f] = 1.0
    return m.astype(BF)

_MASK01 = None


def _prep_C(inputs, b, xin):
    global _MASK01
    c = _CONSTS
    if _MASK01 is None:
        _MASK01 = _mask01()
    return {
        "xin": xin,
        "qkvT": np.ascontiguousarray(np.asarray(inputs["L1_attn_qkv"]).T),
        "woT": np.ascontiguousarray(np.asarray(inputs["L1_attn_wo"]).T),
        "cos_bc": c["cos_bc"], "sin_sgn": c["sin_sgn"], "Rswap": c["Rswap"],
        "mask01": _MASK01,
        "ones_col_bf": c["ones_col_bf"], "ones_row_f": c["ones_row_f"], "cst": c["cst"],
    }




def emit_D1(nc, p, palloc, PS, dr):
    xin = dr["xin"]
    lnxd = dr["lnxd"]
    hd = dr["hd"]
    KTd = dr["KTd"]

    with _nullctx():
        sb = _SB()
        def alloc(name, shape, dt=F32):
            t = palloc(name, shape, dt)
            setattr(sb, name, t)
            return t
        alloc("xc", [128, ND, TC])
        alloc("lnxb", [128, ND, TC], BF16)
        alloc("xb", [128, ND, TC], BF16)
        alloc("sq", [128, ND, TC], BF16)
        alloc("ln_scr", [128, TC])
        for r in ("row_nm", "row_t1", "row_t2", "row_a", "row_b"):
            alloc(r, [1, TC])
        alloc("xhatb", [128, ND, TC], BF16)
        alloc("kt0", [128, TC], BF16); alloc("kt1", [128, TC], BF16)
        alloc("kbuf", [128, T], BF16)
        alloc("vtok", [128, NKBMAX, 16, 65], BF16)
        alloc("QT", [128, ND, TC], BF16)
        alloc("att", [128, ND, TC], BF16)
        alloc("qsb", [128, TC], BF16)
        alloc("t1s", [128, TC], BF16)
        alloc("t2s", [128, TC], BF16)
        alloc("sqk", [128, TC], BF16)
        alloc("mk2", [1, 16])
        alloc("q2", [1, TC]); alloc("mrow", [1, TC])
        alloc("mcol", [128, 1])
        alloc("linv", [1, TC]); alloc("lbc", [64, TC])
        alloc("Pt0", [128, 2 * TC], BF16)
        alloc("wk0", [128, ND, 128], BF16); alloc("wk1", [128, ND, 128], BF16)
        alloc("wv0", [128, ND, TC], BF16)
        alloc("wo0", [128, ND, 128], BF16); alloc("wo1", [128, ND, 128], BF16)
        alloc("sc1", [128, ND])
        alloc("cos_bc", [128, T], BF16)
        alloc("sin_sgn", [128, T], BF16)
        alloc("Rswap", [128, 128], BF16)
        alloc("mask01", [128, 4 * TC], BF16)
        alloc("ones_col_bf", [128, 1], BF16)
        alloc("ones_row_f", [1, 128])
        alloc("cst", [128, 8])

        PSA = [PS[0], PS[1], PS[4], PS[5]]
        PS6 = PS[6]
        PS7 = PS[7]
        psl = {6: PS6, 7: PS7, 4: PS[4], 5: PS[5]}

        wtoks = []
        for nm_ in ("cos_bc", "sin_sgn", "Rswap", "mask01", "ones_col_bf", "ones_row_f",
                    "cst", "sc1"):
            wtoks.append(p.dma("sync", (lambda nm_=nm_: lambda e: e.dma_start(
                out=getattr(sb, nm_)[:], in_=dr[nm_]))(), sem="wld"))
        wtok = tmax(*wtoks)
        t_mk = p.op("vector", lambda e: e.memset(sb.mk2[:], 0.0), sem="V")
        t_vo = p.op("vector", lambda e: e.memset(sb.vtok[:, :, :, 64:65], 1.0), sem="V")
        io = {}

        def emit_rope(dst_view, src_ps, cos_v, sin_v, rot_ps, rot_key, scale, waits_in,
                      dst_war=()):
            tev = p.op("scalar", (lambda: lambda e: e.activation(
                sb.qsb[:], src_ps[:], AF.Copy, scale=scale))(),
                waits=tmax(waits_in, io.get("qsb_free")), sem="S")
            trm = p.op("tensor", (lambda: lambda e: e.matmul(
                rot_ps[:], sb.Rswap[:], sb.qsb[:], start=True, stop=True))(),
                waits=tmax(tev, io.get(rot_key)), sem="T")
            tp1 = p.op("gpsimd", (lambda: lambda e: e.tensor_tensor(
                sb.t1s[:], sb.qsb[:], cos_v, ALU.mult))(),
                waits=tmax(tev, io.get("t1s_free")), sem="G")
            tp2 = p.op("vector", (lambda: lambda e: e.tensor_tensor(
                sb.t2s[:], rot_ps[:], sin_v, ALU.mult))(),
                waits=tmax(trm, io.get("t2s_free")), sem="V")
            io[rot_key] = tp2
            tp3 = p.op("vector", (lambda dst_view=dst_view: lambda e: e.tensor_add(
                dst_view, sb.t1s[:], sb.t2s[:]))(), waits=tmax(tp1, tp2, dst_war), sem="V")
            io["qsb_free"] = tp1
            io["t1s_free"] = tp3
            io["t2s_free"] = tp3
            return tp3, tev

        wkslot = {0: (), 1: ()}
        wvslot = {0: (), 1: ()}
        ktslot = {0: (), 1: ()}
        xc_consumed = ()
        vtok_done = []
        ktd_last = None
        for i in range(NTC):
            tsl = (i * TC, (i + 1) * TC)
            ldx = p.dma("sync", (lambda i=i: lambda e: e.dma_start(
                out=sb.xc[:], in_=xin[:, i * TC:(i + 1) * TC].rearrange("(c p) t -> p c t", p=128)))(),
                waits=xc_consumed, sem="xld")
            xc_tok = tmax(ldx, wtok)
            ln_last = emit_norm2(p, sb, io, psl, sb.xc[:], xc_tok, sb.lnxb[:])
            tlnst = p.dma("sync", (lambda i=i: lambda e: e.dma_start(
                out=lnxd[:, i * TC:(i + 1) * TC].rearrange("(c p) t -> p c t", p=128),
                in_=sb.lnxb[:]))(), waits=(ln_last,), sem="lnst")
            rms_last = emit_norm2(p, sb, io, psl, sb.lnxb[:], (ln_last,), sb.xhatb[:],
                                  eps_col=4, rms=True, scale_col=sb.sc1)
            ln_last = rms_last

            for rc in range(ND):
                sl = rc % 2
                wk = sb.wk0 if sl == 0 else sb.wk1
                kt = sb.kt0 if sl == 0 else sb.kt1
                twk = p.dma("gpsimd", (lambda rc=rc, wk=wk: lambda e: e.dma_start(
                    out=wk[:], in_=dr["wk"][:, rc * 128:(rc + 1) * 128]
                    .rearrange("(c p) o -> p c o", p=128)))(),
                    waits=wkslot[sl], sem=f"wkld{sl}")
                ps_k = PSA[sl]
                tkm = None
                for c in range(ND):
                    sp = (c == ND - 1)
                    tkm = p.op("tensor", (lambda c=c, sp=sp, wk=wk, ps_k=ps_k: lambda e: e.matmul(
                        ps_k[:], wk[:, c, :], sb.xhatb[:, c, :], start=(c == 0), stop=sp))(),
                        waits=tmax(twk, ln_last, io.get(f"pa{sl}")) if c == 0 else (),
                        sem="T" if sp else None)
                wkslot[sl] = (tkm,)
                rope_tok, tev = emit_rope(kt[:], ps_k,
                                          sb.cos_bc[:, tsl[0]:tsl[1]], sb.sin_sgn[:, tsl[0]:tsl[1]],
                                          PSA[2 + sl], f"ps{4 + sl}_free", 0.125,
                                          (tkm,), dst_war=ktslot[sl])
                io[f"pa{sl}"] = tev
                tkst = p.dma("sync", (lambda rc=rc, i=i, kt=kt: lambda e: e.dma_start(
                    out=KTd[rc * 128:(rc + 1) * 128, i * TC:(i + 1) * TC], in_=kt[:]))(),
                    waits=(rope_tok,), sem=f"ktst{sl}")
                tsq = p.op("scalar", (lambda kt=kt: lambda e: e.activation(
                    sb.sqk[:], kt[:], AF.Square))(),
                    waits=tmax(rope_tok, io.get("sqk_free")), sem="S")
                ktslot[sl] = tmax(tkst, tsq)
                for hh in range(2):
                    base = hh * 64
                    psn = PS6 if hh == 0 else PS7
                    rowt = sb.row_t1 if hh == 0 else sb.row_t2
                    tnm = p.op("tensor", (lambda base=base, psn=psn: lambda e: e.matmul(
                        psn[0:1, :], sb.ones_col_bf[base:base + 64, :], sb.sqk[base:base + 64, :],
                        start=True, stop=True))(),
                        waits=tmax(tsq, io.get(f"ps{6 + hh}_free")), sem="T")
                    trm_ = p.op("vector", (lambda psn=psn, rowt=rowt: lambda e: e.reduce_max(
                        rowt[0:1, 0:1], psn[0:1, :], axis=AX.X))(), waits=(tnm,), sem="V")
                    io[f"ps{6 + hh}_free"] = trm_
                    h = 2 * rc + hh
                    tmk = p.op("vector", (lambda h=h, rowt=rowt: lambda e: e.tensor_tensor(
                        sb.mk2[0:1, h:h + 1], sb.mk2[0:1, h:h + 1], rowt[0:1, 0:1], ALU.max))(),
                        waits=(trm_, t_mk), sem="V")
                    io["sqk_free"] = tmk

            for vc in range(2):
                wv = sb.wv0
                twv = p.dma("gpsimd", (lambda vc=vc, wv=wv: lambda e: e.dma_start(
                    out=wv[:], in_=dr["wv"][:, vc * TC:(vc + 1) * TC]
                    .rearrange("(c p) o -> p c o", p=128)))(),
                    waits=tmax(wvslot[0], wvslot[1]), sem="wvld")
                for ttt in range(TC // 128):
                    ps_v = PSA[ttt % 2]
                    tvm = None
                    for c in range(ND):
                        sp = (c == ND - 1)
                        tvm = p.op("tensor", (lambda c=c, ttt=ttt, sp=sp, wv=wv, ps_v=ps_v: lambda e: e.matmul(
                            ps_v[:], sb.xhatb[:, c, ttt * 128:(ttt + 1) * 128], wv[:, c, :],
                            start=(c == 0), stop=sp))(),
                            waits=tmax(twv, ln_last, io.get(f"pa{ttt % 2}")) if c == 0 else (),
                            sem="T" if sp else None)
                    wvslot[ttt % 2] = (tvm,)
                    kb = i * 4 + ttt
                    tev2 = None
                    for hh in range(8):
                        h = vc * 8 + hh
                        tev2 = p.op("scalar", (lambda kb=kb, h=h, hh=hh, ps_v=ps_v: lambda e: e.activation(
                            sb.vtok[:, kb, h, 0:64], ps_v[:, hh * 64:(hh + 1) * 64], AF.Copy))(),
                            waits=(tvm,) if hh == 0 else (), sem="S")
                    io[f"pa{ttt % 2}"] = tev2
                    vtok_done.append(tev2)
            xc_consumed = tmax(ln_last, tlnst)
        vtok_tok = tmax(*vtok_done, t_vo)
        kt_tok = io.get("sqk_free")
        ktd_tok = tmax(("ktst0", p.counts.get("ktst0", 0)), ("ktst1", p.counts.get("ktst1", 0)))
        lnx_tok = ("lnst", p.counts.get("lnst", 0))

        store_toks = []
        kbslot = {0: (), 1: ()}
        lnx_consumed = ()
        for qc in range(NTC):
            tsl = (qc * TC, (qc + 1) * TC)
            ldl = p.dma("sync", (lambda qc=qc: lambda e: e.dma_start(
                out=sb.lnxb[:], in_=lnxd[:, qc * TC:(qc + 1) * TC]
                .rearrange("(c p) t -> p c t", p=128)))(),
                waits=tmax(lnx_consumed, lnx_tok, xc_consumed), sem="lnld")
            lnx_tok2 = tmax(ldl, wtok)
            ln_last = emit_norm2(p, sb, io, psl, sb.lnxb[:], lnx_tok2, sb.xhatb[:],
                                 eps_col=4, rms=True, scale_col=sb.sc1)

            q_toks = []
            for rc in range(ND):
                sl = rc % 2
                wk = sb.wk0 if sl == 0 else sb.wk1
                twk = p.dma("gpsimd", (lambda rc=rc, wk=wk: lambda e: e.dma_start(
                    out=wk[:], in_=dr["wq"][:, rc * 128:(rc + 1) * 128]
                    .rearrange("(c p) o -> p c o", p=128)))(),
                    waits=wkslot[sl], sem=f"wkld{sl}")
                ps_k = PSA[sl]
                tkm = None
                for c in range(ND):
                    sp = (c == ND - 1)
                    tkm = p.op("tensor", (lambda c=c, sp=sp, wk=wk, ps_k=ps_k: lambda e: e.matmul(
                        ps_k[:], wk[:, c, :], sb.xhatb[:, c, :], start=(c == 0), stop=sp))(),
                        waits=tmax(twk, ln_last, io.get(f"pa{sl}")) if c == 0 else (),
                        sem="T" if sp else None)
                wkslot[sl] = (tkm,)
                rope_tok, tev = emit_rope(sb.QT[:, rc, :], ps_k,
                                          sb.cos_bc[:, tsl[0]:tsl[1]], sb.sin_sgn[:, tsl[0]:tsl[1]],
                                          PSA[2 + sl], f"ps{4 + sl}_free", 1.0, (tkm,))
                io[f"pa{sl}"] = tev
                q_toks.append(rope_tok)
            q_tok = tmax(*q_toks)

            att_toks = []
            nkb = (qc + 1) * 4
            for h in range(16):
                rc, hh = h // 2, h % 2
                base = hh * 64
                ksl = h % 2
                tkl = p.dma("sync", (lambda h=h, nkb=nkb, ksl=ksl: lambda e: e.dma_start(
                    out=sb.kbuf[ksl * 64:(ksl + 1) * 64, 0:nkb * 128],
                    in_=KTd[h * 64:(h + 1) * 64, 0:nkb * 128]))(),
                    waits=tmax(kbslot[ksl], ktd_tok if qc == 0 and h < 2 else None),
                    sem=f"kbl{ksl}")
                tsq = p.op("scalar", (lambda rc=rc, base=base: lambda e: e.activation(
                    sb.sqk[0:64, :], sb.QT[base:base + 64, rc, :], AF.Square))(),
                    waits=tmax(q_tok, io.get("sqk_free")), sem="S")
                tq2 = p.op("tensor", (lambda: lambda e: e.matmul(
                    PS6[0:1, :], sb.ones_col_bf[0:64, :], sb.sqk[0:64, :], start=True, stop=True))(),
                    waits=tmax(tsq, io.get("ps6_free")), sem="T")
                io["sqk_free"] = tq2
                tm1 = p.op("vector", (lambda h=h: lambda e: e.tensor_single_scalar(
                    sb.q2[:], PS6[0:1, :], sb.mk2[0:1, h:h + 1], ALU.mult))(),
                    waits=tmax(tq2, kt_tok), sem="V")
                io["ps6_free"] = tm1
                tm2 = p.op("scalar", (lambda: lambda e: e.activation(
                    sb.mrow[:], sb.q2[:], AF.Ln))(), waits=(tm1,), sem="S")
                tm3 = p.op("scalar", (lambda: lambda e: e.activation(
                    sb.mrow[:], sb.mrow[:], AF.Exp, scale=0.5))(), waits=(tm2,), sem="S")
                tm4 = p.op("vector", (lambda: lambda e: e.reduce_max(
                    sb.row_t1[0:1, 0:1], sb.mrow[:], axis=AX.X, negate=True))(),
                    waits=tmax(tm3, io.get("rt1_free")), sem="V")
                tm5 = p.op("tensor", (lambda: lambda e: e.matmul(
                    PS7[:, 0:1], sb.ones_row_f[:], sb.row_t1[0:1, 0:1], start=True, stop=True))(),
                    waits=tmax(tm4, io.get("ps7_free")), sem="T")
                io["rt1_free"] = tm5
                tm6 = p.op("vector", (lambda: lambda e: e.tensor_copy(sb.mcol[:], PS7[:, 0:1]))(),
                           waits=tmax(tm5, io.get("mcol_free")), sem="V")
                io["ps7_free"] = tm6

                ps_O = PSA[h % 2]
                tOm = None
                kread = []
                for kb in range(nkb):
                    u = kb % 2
                    ps_s = PS[2 + u]
                    ptk = 0
                    pt = sb.Pt0
                    tsm = p.op("tensor", (lambda kb=kb, rc=rc, base=base, ksl=ksl, ps_s=ps_s, u=u: lambda e: e.matmul(
                        ps_s[:], sb.kbuf[ksl * 64:ksl * 64 + 64, kb * 128:(kb + 1) * 128],
                        sb.QT[base:base + 64, rc, :], start=True, stop=True))(),
                        waits=tmax(io.get(f"psS{u}_free"), q_tok, tkl), sem="T")
                    kread.append(tsm)
                    texp = p.op("scalar", (lambda pt=pt, u=u, ps_s=ps_s: lambda e: e.activation(
                        pt[:, u * TC:(u + 1) * TC], ps_s[:], AF.Exp, bias=sb.mcol[:, 0:1]))(),
                        waits=tmax(tsm, tm6,
                                   io.get(f"pt_free{ptk}") if u == 0 else None), sem="S")
                    io[f"psS{u}_free"] = texp
                    tmask = texp
                    off = kb * 128 - qc * TC
                    if 0 <= off < TC:
                        oi = off // 128
                        tmask = p.op("gpsimd", (lambda pt=pt, u=u, oi=oi: lambda e: e.tensor_tensor(
                            pt[:, u * TC:(u + 1) * TC], pt[:, u * TC:(u + 1) * TC],
                            sb.mask01[:, oi * TC:(oi + 1) * TC], ALU.mult))(),
                            waits=(texp,), sem="G")
                    tOm = p.op("tensor", (lambda kb=kb, u=u, h=h, pt=pt, ps_O=ps_O, nkb=nkb: lambda e: e.matmul(
                        ps_O[0:65, :], sb.vtok[:, kb, h, :], pt[:, u * TC:(u + 1) * TC],
                        start=(kb == 0), stop=(kb == nkb - 1)))(),
                        waits=tmax(tmask, vtok_tok,
                                   (io.get(f"pa{h % 2}") if kb == 0 else None)),
                        sem="T")
                    if u == 1 or kb == nkb - 1:
                        io[f"pt_free{ptk}"] = tOm
                kbslot[ksl] = tmax(*kread)
                tli = p.op("vector", (lambda ps_O=ps_O: lambda e: e.reciprocal(
                    sb.linv[:], ps_O[64:65, :]))(), waits=tmax(tOm, io.get("linv_free")), sem="V")
                tlb = p.op("tensor", (lambda: lambda e: e.matmul(
                    PSA[2 + h % 2][0:64, :], sb.ones_row_f[0:1, 0:64], sb.linv[:],
                    start=True, stop=True))(),
                    waits=tmax(tli, io.get(f"ps{4 + h % 2}_free")), sem="T")
                tle = p.op("scalar", (lambda: lambda e: e.activation(
                    sb.lbc[:], PSA[2 + h % 2][0:64, :], AF.Copy))(),
                    waits=tmax(tlb, io.get("lbc_free")), sem="S")
                io[f"ps{4 + h % 2}_free"] = tle
                tat = p.op("vector", (lambda h=h, rc=rc, base=base, ps_O=ps_O: lambda e: e.tensor_mul(
                    sb.att[base:base + 64, rc, :], ps_O[0:64, :], sb.lbc[:]))(),
                    waits=(tle,), sem="V")
                io[f"pa{h % 2}"] = tat
                io["lbc_free"] = tat
                io["linv_free"] = tlb
                io["mcol_free"] = texp
                att_toks.append(tat)
            att_tok = tmax(*att_toks)

            res_toks = []
            for o in range(ND):
                wsl = o % 2
                wob = sb.wo0 if wsl == 0 else sb.wo1
                two = p.dma("gpsimd", (lambda o=o, wob=wob: lambda e: e.dma_start(
                    out=wob[:], in_=dr["wo"][:, o * 128:(o + 1) * 128]
                    .rearrange("(c p) o2 -> p c o2", p=128)))(),
                    waits=io.get(f"wo{wsl}_rd", ()), sem=f"wold{wsl}")
                ps_o = PS6 if o % 2 == 0 else PS7
                tpo = None
                for c in range(ND):
                    sp = (c == ND - 1)
                    tpo = p.op("tensor", (lambda o=o, c=c, sp=sp, ps_o=ps_o, wob=wob: lambda e: e.matmul(
                        ps_o[:], wob[:, c, :], sb.att[:, c, :],
                        start=(c == 0), stop=sp))(),
                        waits=tmax(att_tok, two, io.get(f"ps{6 + o % 2}_free")) if c == 0 else (),
                        sem="T" if sp else None)
                io[f"wo{wsl}_rd"] = (tpo,)
                tr = p.op("vector", (lambda o=o, ps_o=ps_o: lambda e: e.tensor_add(
                    sb.lnxb[:, o, :], sb.lnxb[:, o, :], ps_o[:]))(), waits=(tpo,), sem="V")
                io[f"ps{6 + o % 2}_free"] = tr
                res_toks.append(tr)
            tst_d = p.dma("sync", (lambda qc=qc: lambda e: e.dma_start(
                out=hd[:, qc * TC:(qc + 1) * TC].rearrange("(c p) t -> p c t", p=128),
                in_=sb.lnxb[:]))(), waits=tuple(res_toks), sem="xst")
            store_toks.append(tst_d)
            lnx_consumed = tmax(tst_d, *res_toks)
            xc_consumed = ()

        p.op("sync", lambda e: None, waits=tmax(*store_toks))
    return None



_D1_DINS = [("wq", [D, D], F32), ("wk", [D, D], F32), ("wv", [D, D], F32),
            ("wo", [D, D], F32), ("sc1", [128, ND], F32), ("cos_bc", [128, T], BF16),
            ("sin_sgn", [128, T], BF16), ("Rswap", [128, 128], BF16),
            ("mask01", [128, 4 * TC], BF16), ("ones_col_bf", [128, 1], BF16),
            ("ones_row_f", [1, 128], F32), ("cst", [128, 8], F32)]


def build_launchD1():
    nc = bass.Bass("TRN2", target_bir_lowering=False)
    dr = {"xin": nc.dram_tensor("xin", [D, T], F32, kind="ExternalInput").ap()}
    for (nm, shp, dt) in _D1_DINS:
        dr[nm] = nc.dram_tensor(nm, shp, dt, kind="ExternalInput").ap()
    dr["lnxd"] = nc.dram_tensor("lnxd", [D, T], BF16, kind="ExternalOutput").ap()
    dr["hd"] = nc.dram_tensor("hd", [D, T], BF16, kind="ExternalOutput").ap()
    dr["KTd"] = nc.dram_tensor("KTd", [D, T], BF16, kind="Internal").ap()
    with ExitStack() as ctx:
        p = Prog(nc)
        PS = [ctx.enter_context(nc.psum_tensor(f"PS{k}", [128, TC], F32)) for k in range(8)]
        allocd = {}
        def palloc(name, shape, dt=F32):
            if name not in allocd:
                allocd[name] = ctx.enter_context(nc.sbuf_tensor("s_" + name, shape, dt))
            return allocd[name]
        emit_D1(nc, p, palloc, PS, dr)
        p.build(ctx)
    return nc

def _prep_D1(inputs, b, xin):
    global _MASK01
    c = _CONSTS
    if _MASK01 is None:
        _MASK01 = _mask01()
    return {
        "xin": xin,
        "wq": np.ascontiguousarray(np.asarray(inputs["L1_ffn_wq"])),
        "wk": np.ascontiguousarray(np.asarray(inputs["L1_ffn_wk"])),
        "wv": np.ascontiguousarray(np.asarray(inputs["L1_ffn_wv"])),
        "wo": np.ascontiguousarray(np.asarray(inputs["L1_ffn_wo"])),
        "sc1": np.ascontiguousarray(np.asarray(inputs["L1_ffn_scale1"]).reshape(ND, 128).T),
        "cos_bc": c["cos_bc"], "sin_sgn": c["sin_sgn"], "Rswap": c["Rswap"],
        "mask01": _MASK01,
        "ones_col_bf": c["ones_col_bf"], "ones_row_f": c["ones_row_f"], "cst": c["cst"],
    }



D2SLAB = 1024
ND2SLAB = T // D2SLAB


def emit_D2(nc, p, palloc, PS, dr):
    xin = dr["xin"]
    hdin = dr["hdin"]
    xout = dr["xout"]

    with _nullctx():
        sb = _SB()
        def alloc(name, shape, dt=F32):
            t = palloc(name, shape, dt)
            setattr(sb, name, t)
            return t
        alloc("hsl", [128, ND, D2SLAB], BF16)
        alloc("xhatb", [128, ND, D2SLAB], BF16)
        alloc("out_acc", [128, ND, D2SLAB])
        alloc("x3s", [128, D2SLAB])
        alloc("xb", [128, ND, TC], BF16)
        alloc("sq", [128, ND, TC], BF16)
        alloc("ln_scr", [128, TC])
        for r in ("row_nm", "row_t1", "row_t2", "row_a", "row_b"):
            alloc(r, [1, TC])
        alloc("rw_bf", [128, ND, 8], BF16)
        alloc("noise_s", [128, 8, 8])
        alloc("rsc", [128, 48])
        alloc("w_T8", [8, D2SLAB])
        alloc("wbc", [128, D2SLAB])
        alloc("W1b", [128, ND, D], BF16)
        alloc("W2b", [128, ND, D], BF16)
        alloc("WOb", [128, ND, D], BF16)
        alloc("sg0", [128, TC], BF16); alloc("sg1", [128, TC], BF16)
        alloc("u2w0", [128, TC]); alloc("u2w1", [128, TC])
        alloc("h0", [128, ND, TC], BF16); alloc("h1", [128, ND, TC], BF16)
        alloc("tmpd", [128, TC])
        alloc("b1c", [128, ND, 8])
        alloc("b2c", [128, ND, 8])
        alloc("boutr", [8, D])
        alloc("sc2", [128, ND])
        alloc("identf", [128, 128])
        alloc("E8", [8, 1024])
        alloc("ones_col_bf", [128, 1], BF16)
        alloc("ones_row_f", [1, 128])
        alloc("cst", [128, 8])

        psl = {6: PS[6], 7: PS[7], 4: PS[4], 5: PS[5]}

        wtoks = []
        for nm_ in ("b1c", "b2c", "boutr", "sc2", "identf", "E8", "ones_col_bf",
                    "ones_row_f", "cst"):
            wtoks.append(p.dma("sync", (lambda nm_=nm_: lambda e: e.dma_start(
                out=getattr(sb, nm_)[:], in_=dr[nm_]))(), sem="wld"))
        wtoks.append(p.dma("gpsimd", lambda e: e.dma_start(
            out=sb.rw_bf[:], in_=dr["rwT"].rearrange("(c p) e -> p c e", p=128)), sem="wldg"))
        wtok = tmax(*wtoks)
        io = {}

        store_toks = []
        h_consumed = ()
        acc_consumed = ()
        w_consumed = ()
        for s in range(ND2SLAB):
            ldh = p.dma("sync", (lambda s=s: lambda e: e.dma_start(
                out=sb.hsl[:], in_=hdin[:, s * D2SLAB:(s + 1) * D2SLAB]
                .rearrange("(c p) t -> p c t", p=128)))(),
                waits=h_consumed, sem="hld")
            ldn = p.dma("sync", (lambda s=s: lambda e: e.dma_start(
                out=sb.noise_s[:], in_=dr["noise"][s * D2SLAB:(s + 1) * D2SLAB, :]
                .rearrange("(tt p) e -> p tt e", p=128)))(),
                waits=io.get("noise_free", ()), sem="nld")
            h_tok = tmax(ldh, wtok)

            ln_last = None
            for q in range(D2SLAB // TC):
                ln_last = emit_norm2(p, sb, io, psl,
                                     sb.hsl[:, :, q * TC:(q + 1) * TC], h_tok,
                                     sb.xhatb[:, :, q * TC:(q + 1) * TC],
                                     eps_col=4, rms=True, scale_col=sb.sc2)

            wT_toks = []
            for tt in range(D2SLAB // 128):
                tlg = None
                for c in range(ND):
                    sp = (c == ND - 1)
                    tlg = p.op("tensor", (lambda c=c, tt=tt, sp=sp: lambda e: e.matmul(
                        PS[4][:, 0:8], sb.xhatb[:, c, tt * 128:(tt + 1) * 128], sb.rw_bf[:, c, :],
                        start=(c == 0), stop=sp))(),
                        waits=tmax(ln_last, io.get("ps4_free")) if c == 0 else (),
                        sem="T" if sp else None)
                lg = sb.rsc[:, 0:8]; msk = sb.rsc[:, 8:16]; tmp = sb.rsc[:, 16:24]
                e_ = sb.rsc[:, 24:32]; wraw = sb.rsc[:, 32:40]
                m1 = sb.rsc[:, 40:41]; m1n = sb.rsc[:, 41:42]; m2 = sb.rsc[:, 42:43]
                den = sb.rsc[:, 43:44]; dinv = sb.rsc[:, 44:45]; wt = sb.rsc[:, 0:8]
                ta = p.op("vector", (lambda lg=lg, tt=tt: lambda e: e.tensor_add(
                    lg, PS[4][:, 0:8], sb.noise_s[:, tt, :]))(),
                    waits=tmax(tlg, ldn, io.get("rsc_free")), sem="V")
                io["ps4_free"] = ta
                tb = p.op("vector", (lambda lg=lg, m1=m1: lambda e: e.reduce_max(m1, lg, axis=AX.X))(),
                          waits=(ta,), sem="V")
                tc_ = p.op("vector", (lambda lg=lg, msk=msk, m1=m1: lambda e: e.tensor_single_scalar(
                    msk, lg, m1[:, 0:1], ALU.is_equal))(), waits=(tb,), sem="V")
                td = p.op("vector", (lambda m1=m1, m1n=m1n: lambda e: e.tensor_scalar_mul(
                    m1n, m1, -1.0))(), waits=(tb,), sem="V")
                te_ = p.op("vector", (lambda msk=msk, tmp=tmp, lg=lg: lambda e: e.scalar_tensor_tensor(
                    tmp, msk, -1e9, lg, ALU.mult, ALU.add))(), waits=(tc_,), sem="V")
                tf = p.op("vector", (lambda tmp=tmp, m2=m2: lambda e: e.reduce_max(m2, tmp, axis=AX.X))(),
                          waits=(te_,), sem="V")
                tg = p.op("scalar", (lambda lg=lg, e_=e_, m1n=m1n: lambda e: e.activation(
                    e_, lg, AF.Exp, bias=m1n[:, 0:1]))(), waits=(td, te_), sem="S")
                tden = p.op("vector", (lambda e_=e_, den=den: lambda e: e.reduce_sum(den, e_, axis=AX.X))(),
                            waits=(tg,), sem="V")
                tk = p.op("vector", (lambda den=den, dinv=dinv: lambda e: e.reciprocal(dinv, den))(),
                          waits=(tden,), sem="V")
                th = p.op("vector", (lambda lg=lg, msk=msk, m2=m2: lambda e: e.tensor_single_scalar(
                    msk, lg, m2[:, 0:1], ALU.is_ge))(), waits=(tf, tg), sem="V")
                ti = p.op("vector", (lambda e_=e_, msk=msk, wraw=wraw: lambda e: e.tensor_mul(
                    wraw, e_, msk))(), waits=(th,), sem="V")
                tl = p.op("vector", (lambda wraw=wraw, dinv=dinv, wt=wt: lambda e: e.tensor_single_scalar(
                    wt, wraw, dinv[:, 0:1], ALU.mult))(), waits=(tk, ti), sem="V")
                ttr = p.op("tensor", (lambda wt=wt: lambda e: e.transpose(
                    PS[5][0:8, 0:128], wt, sb.identf[:]))(),
                    waits=tmax(tl, io.get("ps5_free")), sem="T")
                tcp = p.op("vector", (lambda tt=tt: lambda e: e.tensor_copy(
                    sb.w_T8[:, tt * 128:(tt + 1) * 128], PS[5][0:8, 0:128]))(),
                    waits=(ttr,), sem="V")
                io["ps5_free"] = tcp
                io["rsc_free"] = ttr
                wT_toks.append(tcp)
            wT_tok = tmax(*wT_toks)
            io["noise_free"] = wT_tok

            init_toks = {}
            for o in range(ND):
                for q in range(D2SLAB // TC):
                    tbm = p.op("tensor", (lambda o=o, q=q: lambda e: e.matmul(
                        PS[6 + q % 2][:], sb.boutr[0:8, o * 128:(o + 1) * 128],
                        sb.w_T8[0:8, q * TC:(q + 1) * TC], start=True, stop=True))(),
                        waits=tmax(wT_tok, io.get(f"ps{6 + q % 2}_free")), sem="T")
                    tbc = p.op("vector", (lambda o=o, q=q: lambda e: e.tensor_copy(
                        sb.out_acc[:, o, q * TC:(q + 1) * TC], PS[6 + q % 2][:]))(),
                        waits=tmax(tbm, acc_consumed), sem="V")
                    io[f"ps{6 + q % 2}_free"] = tbc
                    init_toks[(o, q)] = tbc
            acc_init = tmax(*init_toks.values())

            for ei in range(8):
                wbc_toks = []
                for q in range(D2SLAB // TC):
                    tw1 = p.op("tensor", (lambda ei=ei, q=q: lambda e: e.matmul(
                        PS[6][:], sb.E8[0:8, ei * 128:(ei + 1) * 128],
                        sb.w_T8[0:8, q * TC:(q + 1) * TC], start=True, stop=True))(),
                        waits=tmax(wT_tok, io.get("ps6_free")), sem="T")
                    tw2 = p.op("vector", (lambda q=q: lambda e: e.tensor_copy(
                        sb.wbc[:, q * TC:(q + 1) * TC], PS[6][:]))(),
                        waits=tmax(tw1, io.get("wbc_free")), sem="V")
                    io["ps6_free"] = tw2
                    wbc_toks.append(tw2)
                twb = tmax(*wbc_toks)

                t_w1 = p.dma("gpsimd", (lambda ei=ei: lambda e: e.dma_start(
                    out=sb.W1b[:], in_=dr["W1"][ei].rearrange("(c p) h -> p c h", p=128)))(),
                    waits=io.get("W1b_rd", ()), sem="w1ld")
                t_w2 = p.dma("gpsimd", (lambda ei=ei: lambda e: e.dma_start(
                    out=sb.W2b[:], in_=dr["W2"][ei].rearrange("(c p) h -> p c h", p=128)))(),
                    waits=io.get("W2b_rd", ()), sem="w2ld")
                t_w3 = p.dma("gpsimd", (lambda ei=ei: lambda e: e.dma_start(
                    out=sb.WOb[:], in_=dr["WO"][ei].rearrange("(c p) o -> p c o", p=128)))(),
                    waits=io.get("WOb_rd", ()), sem="w3ld")
                w1rd = []; w2rd = []; w3rd = []
                for tcq in range(D2SLAB // TC):
                    h_sl = tcq % 2
                    hbuf = sb.h0 if h_sl == 0 else sb.h1
                    h_toks = []
                    for hc in range(ND):
                        gr = hc % 2
                        ps_g = PS[0 + gr]
                        ps_u = PS[2 + gr]
                        sg = sb.sg0 if gr == 0 else sb.sg1
                        u2w = sb.u2w0 if gr == 0 else sb.u2w1
                        tgm = None
                        for c in range(ND):
                            sp = (c == ND - 1)
                            tgm = p.op("tensor", (lambda c=c, hc=hc, tcq=tcq, sp=sp, ps_g=ps_g: lambda e: e.matmul(
                                ps_g[:], sb.W1b[:, c, hc * 128:(hc + 1) * 128],
                                sb.xhatb[:, c, tcq * TC:(tcq + 1) * TC], start=(c == 0), stop=sp))(),
                                waits=tmax(t_w1, ln_last, io.get(f"ps{0 + gr}_free")) if c == 0 else (),
                                sem="T" if sp else None)
                        w1rd.append(tgm)
                        tsl_ = p.op("scalar", (lambda sg=sg, ps_g=ps_g, hc=hc, ei=ei: lambda e: e.activation(
                            sg[:], ps_g[:], AF.Silu, bias=sb.b1c[:, hc, ei:ei + 1]))(),
                            waits=tmax(tgm, io.get(f"sg{gr}_free")), sem="S")
                        io[f"ps{0 + gr}_free"] = tsl_
                        tum = None
                        for c in range(ND):
                            sp = (c == ND - 1)
                            tum = p.op("tensor", (lambda c=c, hc=hc, tcq=tcq, sp=sp, ps_u=ps_u: lambda e: e.matmul(
                                ps_u[:], sb.W2b[:, c, hc * 128:(hc + 1) * 128],
                                sb.xhatb[:, c, tcq * TC:(tcq + 1) * TC], start=(c == 0), stop=sp))(),
                                waits=tmax(t_w2, io.get(f"ps{2 + gr}_free")) if c == 0 else (),
                                sem="T" if sp else None)
                        w2rd.append(tum)
                        tuw = p.op("vector", (lambda u2w=u2w, hc=hc, ei=ei, ps_u=ps_u: lambda e: e.tensor_scalar_add(
                            u2w[:], ps_u[:], sb.b2c[:, hc, ei:ei + 1]))(),
                            waits=tmax(tum, io.get(f"u2w{gr}_free")), sem="V")
                        io[f"ps{2 + gr}_free"] = tuw
                        thm = p.op("vector", (lambda hbuf=hbuf, hc=hc, sg=sg, u2w=u2w: lambda e: e.tensor_mul(
                            hbuf[:, hc, :], sg[:], u2w[:]))(),
                            waits=tmax(tsl_, tuw, io.get(f"h{h_sl}_free")), sem="V")
                        io[f"sg{gr}_free"] = thm
                        io[f"u2w{gr}_free"] = thm
                        h_toks.append(thm)
                    acc_toks = []
                    for o in range(ND):
                        dr_ = o % 2
                        ps_d = PS[4 + dr_]
                        tdm = None
                        for hc in range(ND):
                            sp = (hc == ND - 1)
                            tdm = p.op("tensor", (lambda o=o, hc=hc, sp=sp, ps_d=ps_d, hbuf=hbuf: lambda e: e.matmul(
                                ps_d[:], sb.WOb[:, hc, o * 128:(o + 1) * 128], hbuf[:, hc, :],
                                start=(hc == 0), stop=sp))(),
                                waits=tmax(t_w3, h_toks[-1], io.get(f"ps{4 + dr_}_free")) if hc == 0 else (),
                                sem="T" if sp else None)
                        w3rd.append(tdm)
                        tmu = p.op("vector", (lambda o=o, tcq=tcq, ps_d=ps_d: lambda e: e.tensor_mul(
                            sb.tmpd[:], ps_d[:], sb.wbc[:, tcq * TC:(tcq + 1) * TC]))(),
                            waits=tmax(tdm, io.get("tmpd_free")), sem="V")
                        io[f"ps{4 + dr_}_free"] = tmu
                        tac = p.op("vector", (lambda o=o, tcq=tcq: lambda e: e.tensor_add(
                            sb.out_acc[:, o, tcq * TC:(tcq + 1) * TC],
                            sb.out_acc[:, o, tcq * TC:(tcq + 1) * TC], sb.tmpd[:]))(),
                            waits=tmax(tmu, acc_init), sem="V")
                        io["tmpd_free"] = tac
                        acc_toks.append(tac)
                    io[f"h{h_sl}_free"] = acc_toks[-1]
                io["W1b_rd"] = tmax(*w1rd)
                io["W2b_rd"] = tmax(*w2rd)
                io["WOb_rd"] = tmax(*w3rd)
                io["wbc_free"] = tmax(*w2rd)
                io["wbc_free"] = io.get("tmpd_free")

            res_toks = []
            for o in range(ND):
                ldx = p.dma("sync", (lambda s=s, o=o: lambda e: e.dma_start(
                    out=sb.x3s[:], in_=xin[o * 128:(o + 1) * 128, s * D2SLAB:(s + 1) * D2SLAB]))(),
                    waits=tmax(io.get("x3s_free")), sem="x3ld")
                t1_ = p.op("vector", (lambda o=o: lambda e: e.tensor_add(
                    sb.out_acc[:, o, :], sb.out_acc[:, o, :], sb.x3s[:]))(),
                    waits=tmax(ldx, io.get("tmpd_free")), sem="V")
                io["x3s_free"] = t1_
                t2_ = p.op("vector", (lambda o=o: lambda e: e.tensor_add(
                    sb.out_acc[:, o, :], sb.out_acc[:, o, :], sb.hsl[:, o, :]))(),
                    waits=(t1_,), sem="V")
                res_toks.append(t2_)
            tst_d = p.dma("sync", (lambda s=s: lambda e: e.dma_start(
                out=xout[:, s * D2SLAB:(s + 1) * D2SLAB].rearrange("(c p) t -> p c t", p=128),
                in_=sb.out_acc[:]))(), waits=tuple(res_toks), sem="xst")
            store_toks.append(tst_d)
            h_consumed = tmax(tst_d, *res_toks)
            acc_consumed = tmax(tst_d,)

        p.op("sync", lambda e: None, waits=tmax(*store_toks))
    return None



_D2_DINS = [("rwT", [D, 8], F32), ("noise", [T, 8], F32), ("W1", [8, D, D], F32),
            ("W2", [8, D, D], F32), ("WO", [8, D, D], F32), ("b1c", [128, ND, 8], F32),
            ("b2c", [128, ND, 8], F32), ("boutr", [8, D], F32), ("sc2", [128, ND], F32),
            ("identf", [128, 128], F32), ("E8", [8, 1024], F32),
            ("ones_col_bf", [128, 1], BF16), ("ones_row_f", [1, 128], F32),
            ("cst", [128, 8], F32)]


def build_launchD2():
    return _standalone(emit_D2, _D2_DINS,
                       extra=[("hdin", [D, T], BF16, "ExternalInput")])

def _prep_D2(inputs, b, xin, hdin):
    c = _CONSTS
    E8 = np.zeros((8, 1024), np.float32)
    for ei in range(8):
        E8[ei, ei * 128:(ei + 1) * 128] = 1.0
    noise = _gumbel_noise()[b] * 0.05
    return {
        "xin": xin,
        "hdin": hdin,
        "rwT": np.ascontiguousarray(np.asarray(inputs["L1_ffn_router_w"])),
        "noise": np.ascontiguousarray(noise.astype(np.float32)),
        "W1": np.ascontiguousarray(np.asarray(inputs["L1_ffn_w1"])),
        "W2": np.ascontiguousarray(np.asarray(inputs["L1_ffn_w2"])),
        "WO": np.ascontiguousarray(np.asarray(inputs["L1_ffn_wout"])),
        "b1c": np.ascontiguousarray(np.asarray(inputs["L1_ffn_b1"]).T.reshape(ND, 128, 8).transpose(1, 0, 2)),
        "b2c": np.ascontiguousarray(np.asarray(inputs["L1_ffn_b2"]).T.reshape(ND, 128, 8).transpose(1, 0, 2)),
        "boutr": np.ascontiguousarray(np.asarray(inputs["L1_ffn_bout"])),
        "sc2": np.ascontiguousarray(np.asarray(inputs["L1_ffn_scale2"]).reshape(ND, 128).T),
        "identf": np.eye(128, dtype=np.float32),
        "E8": E8,
        "ones_col_bf": c["ones_col_bf"], "ones_row_f": c["ones_row_f"], "cst": c["cst"],
    }



_SHARED_SPECS = [("cst", [128, 8], F32), ("ones_col_bf", [128, 1], BF16),
                 ("ones_row_f", [1, 128], F32), ("identf", [128, 128], F32),
                 ("E8", [8, 1024], F32), ("SB16", [16, 128], BF16)]


def build_fused():
    nc = bass.Bass("TRN2", target_bir_lowering=False)
    shared_dr = {}
    def shared_din(name, shape, dt):
        if name not in shared_dr:
            shared_dr[name] = nc.dram_tensor(name, shape, dt, kind="ExternalInput").ap()
        return shared_dr[name]
    def mkdr(pref, dins):
        dr = {}
        for (nm, shp, dt) in dins:
            if nm in SHARED_SBUF:
                dr[nm] = shared_din(nm, shp, dt)
            else:
                dr[nm] = nc.dram_tensor(f"{pref}_{nm}", shp, dt, kind="ExternalInput").ap()
        return dr
    drA = mkdr("A", _A_DINS)
    drB = mkdr("B", _B_DINS)
    drC = mkdr("C", _C_DINS)
    drD1 = mkdr("D1", _D1_DINS)
    drD2 = mkdr("D2", _D2_DINS)
    xin = nc.dram_tensor("A_xin", [D, T], F32, kind="ExternalInput").ap()
    xout = nc.dram_tensor("xout", [D, T], F32, kind="ExternalOutput").ap()
    x1d = nc.dram_tensor("x1d", [D, T], F32, kind="Internal").ap()
    x2d = nc.dram_tensor("x2d", [D, T], F32, kind="Internal").ap()
    x3d = nc.dram_tensor("x3d", [D, T], F32, kind="Internal").ap()
    lnxd = nc.dram_tensor("lnxd", [D, T], BF16, kind="Internal").ap()
    hdd = nc.dram_tensor("hdd", [D, T], BF16, kind="Internal").ap()
    KTd = nc.dram_tensor("KTd", [D, T], BF16, kind="Internal").ap()
    drA["xin"] = xin; drA["xout"] = x1d
    drB["xin"] = x1d; drB["xout"] = x2d
    drC["xin"] = x2d; drC["xout"] = x3d; drC["KTd"] = KTd
    drD1["xin"] = x3d; drD1["lnxd"] = lnxd; drD1["hd"] = hdd; drD1["KTd"] = KTd
    drD2["xin"] = x3d; drD2["hdin"] = hdd; drD2["xout"] = xout

    with ExitStack() as ctx:
        p = Prog(nc)
        PS = [ctx.enter_context(nc.psum_tensor(f"PS{k}", [128, TC], F32)) for k in range(8)]
        palloc = PhaseAlloc(nc)
        palloc.prealloc_shared(_SHARED_SPECS)
        for pref, emit, dr in (("A", emit_A, drA), ("B", emit_B, drB), ("C", emit_C, drC),
                               ("D1", emit_D1, drD1), ("D2", emit_D2, drD2)):
            palloc.start(pref)
            emit(nc, p, palloc, PS, dr)
            p.fence()
        p.build(ctx)
    return nc


def kernel(**inputs):
    cores = list(range(NCORE))
    nc = build_fused()
    in_maps = []
    for b in cores:
        m = {}
        def put(pref, d):
            for k, v in d.items():
                if k in ("xin", "hdin"):
                    continue
                if k in SHARED_SBUF:
                    m[k] = v
                else:
                    m[f"{pref}_{k}"] = v
        pa = _prep_A(inputs, b)
        m["A_xin"] = pa.pop("xin")
        put("A", pa)
        put("B", _prep_B(inputs, b, None))
        put("C", _prep_C(inputs, b, None))
        put("D1", _prep_D1(inputs, b, None))
        put("D2", _prep_D2(inputs, b, None, None))
        in_maps.append(m)
    res = run_bass_kernel_spmd(nc, in_maps, core_ids=cores)
    out = np.stack([np.asarray(r["xout"]).T for r in res.results])
    return out.astype(np.float32)


# version 03
# speedup vs baseline: 1.1925x; 1.1925x over previous
import numpy as np
import ml_dtypes
import concourse.bass as bass
import concourse.mybir as mybir
from concourse.bass_utils import run_bass_kernel_spmd
from contextlib import ExitStack

BF = ml_dtypes.bfloat16
F32 = mybir.dt.float32
BF16 = mybir.dt.bfloat16
AF = mybir.ActivationFunctionType
ALU = mybir.AluOpType
AX = mybir.AxisListType

T = 4096
D = 1024
ND = D // 128
TC = 512
NTC = T // TC
NCORE = 8
DSTATE = 16


class Prog:
    ENGINES = ("sync", "gpsimd", "tensor", "vector", "scalar")

    def __init__(self, nc):
        self.nc = nc
        self.ops = {e: [] for e in self.ENGINES}
        self.counts = {}
        self.order = []

    def _sem(self, name):
        if name not in self.counts:
            self.counts[name] = 0
            self.order.append(name)

    def op(self, eng, fn, waits=(), sem=None, inc=1):
        ws = tuple((s, v) for (s, v) in waits if s is not None)
        tok = None
        if sem is not None:
            self._sem(sem)
            self.counts[sem] += inc
            tok = (sem, self.counts[sem])
        self.ops[eng].append((fn, ws, sem, inc))
        return tok

    def dma(self, eng, fn, waits=(), sem=None):
        return self.op(eng, fn, waits, sem, 16)

    def fence(self):
        toks = tuple((n, c) for n, c in self.counts.items())
        for eng in self.ENGINES:
            self.op(eng, lambda e: None, waits=toks)

    def build(self, ctx):
        sems = {n: ctx.enter_context(self.nc.semaphore(n)) for n in self.order}
        block = ctx.enter_context(self.nc.Block())
        for eng_name in self.ENGINES:
            oplist = self.ops[eng_name]
            if not oplist:
                continue

            def mk(oplist):
                def emit(e):
                    hw = {}
                    for (fn, ws, sem, inc) in oplist:
                        for (s, v) in ws:
                            if hw.get(s, -1) < v:
                                e.wait_ge(sems[s], v)
                                hw[s] = v
                        ins = fn(e)
                        if sem is not None:
                            ins.then_inc(sems[sem], inc)
                return emit

            getattr(block, eng_name)(mk(oplist))


def tmax(*toks):
    best = {}
    def add(t):
        if t is None:
            return
        if isinstance(t, list) or (isinstance(t, tuple) and t and isinstance(t[0], (tuple, list))):
            for u in t:
                add(u)
        elif isinstance(t, tuple) and len(t) == 2 and isinstance(t[0], str):
            s, v = t
            if best.get(s, -1) < v:
                best[s] = v
        else:
            for u in t:
                add(u)
    for t in toks:
        add(t)
    return tuple(best.items())



import contextlib
_nullctx = contextlib.nullcontext


class _SB:
    pass


SHARED_SBUF = {"cst", "ones_col_bf", "ones_row_f", "identf", "E8", "SB16"}
SBUF_TOP = 229344


class PhaseAlloc:
    def __init__(self, nc):
        self.nc = nc
        self.base = None
        self.bump = 0
        self.peak = 0
        self.shared = {}
        self.pref = "P"

    def start(self, pref):
        if self.base is None:
            self.base = self.nc._sbuf_addr_for_side("left")
        self.bump = 0
        self.pref = pref

    def __call__(self, name, shape, dt=F32):
        if name in SHARED_SBUF:
            if name not in self.shared:
                assert self.base is None, f"shared alloc {name} after phase start"
                self.shared[name] = self.nc.alloc_sbuf_tensor("sh_" + name, list(shape), dt)
            return self.shared[name]
        esz = 2 if dt == BF16 else 4
        elems = 1
        for d_ in shape[1:]:
            elems *= d_
        off = (self.base + self.bump + 31) // 32 * 32
        t = self.nc.alloc_sbuf_tensor_at(f"{self.pref}_{name}", list(shape), dt, offset=off)
        self.bump = off - self.base + elems * esz
        self.peak = max(self.peak, self.base + self.bump)
        assert self.base + self.bump <= SBUF_TOP, \
            f"phase {self.pref} SBUF overflow: {self.base + self.bump} > {SBUF_TOP} at {name}"
        return t

    def prealloc_shared(self, specs):
        for (name, shape, dt) in specs:
            if name not in self.shared:
                self.shared[name] = self.nc.alloc_sbuf_tensor("sh_" + name, list(shape), dt)


def _consts():
    c = {}
    c["ident"] = np.eye(128, dtype=BF)
    LR = np.zeros((128, 16 * 128), dtype=np.float32)
    LS = np.zeros((128, 16 * 128), dtype=np.float32)
    for j in range(16):
        for p in range(128):
            k = j * 8 + p // 16
            LR[k, j * 128 + p] = 1.0
            LS[p, j * 128 + k] = 1.0
    c["LR"] = LR.astype(BF)
    c["LS"] = LS.astype(BF)
    SB16 = np.zeros((16, 128), dtype=np.float32)
    for m in range(128):
        SB16[m % 16, m] = 1.0
    c["SB16"] = SB16.astype(BF)
    cp = np.zeros((128, 8), np.float32)
    cp[:, 0] = 1.0; cp[:, 1] = -1.0; cp[:, 2] = -0.5; cp[:, 3] = 1e-5
    cp[:, 4] = 1e-6; cp[:, 5] = 0.125; cp[:, 6] = 0.0
    c["cst"] = cp
    c["ones_col_bf"] = np.ones((128, 1), dtype=BF)
    c["ones_row_f"] = np.ones((1, 128), dtype=np.float32)
    c["ones_row_bf"] = np.ones((1, 128), dtype=BF)
    c["neg_row_bf"] = -np.ones((1, 128), dtype=BF)
    half = 32
    inv = 1.0 / 10000.0 ** (np.arange(half, dtype=np.float64) / half)
    ang = np.arange(T, dtype=np.float64)[:, None] * inv[None, :]
    cos = np.cos(ang).astype(np.float32)
    sin = np.sin(ang).astype(np.float32)
    cos_bc = np.empty((128, T), np.float32)
    sin_sgn = np.empty((128, T), np.float32)
    for p in range(128):
        cos_bc[p] = cos[:, p % 32]
        sin_sgn[p] = -sin[:, p % 32] if (p % 64) < 32 else sin[:, p % 32]
    c["cos_bc"] = cos_bc.astype(BF)
    c["sin_sgn"] = sin_sgn.astype(BF)
    RS = np.zeros((128, 128), np.float32)
    for p in range(128):
        q = p + 32 if (p % 64) < 32 else p - 32
        RS[q, p] = 1.0
    c["Rswap"] = RS.astype(BF)
    return c

_CONSTS = _consts()
_NOISE = None


def _gumbel_noise():
    global _NOISE
    if _NOISE is None:
        import jax
        cpu = jax.devices("cpu")[0]
        with jax.default_device(cpu):
            g = jax.random.gumbel(jax.random.key(42), (8, 4096, 8), dtype=jax.numpy.float32)
            _NOISE = np.asarray(g)
    return _NOISE



def emit_layernorm(p, sb, io, xc, xc_tok, kA=6, kB=7, kS1=4, kS2=5, PS=None, eps_col=3):
    psA, psB, psS1, psS2 = PS[kA], PS[kB], PS[kS1], PS[kS2]
    xcw = xc[:].rearrange("p c t -> p (c t)")
    t_xb = p.op("vector", lambda e: e.tensor_copy(sb.xb[:].rearrange("p c t -> p (c t)"), xcw),
                waits=xc_tok, sem="V")
    t_sq = p.op("scalar", lambda e: e.activation(sb.sq[:].rearrange("p c t -> p (c t)"), xcw, AF.Square),
                waits=xc_tok, sem="S")
    tsx = tsq = None
    for c in range(ND):
        sp = (c == ND - 1)
        tsx = p.op("tensor", (lambda c=c, sp=sp: lambda e: e.matmul(
            psS1[0:1, :], sb.ones_col_bf[:], sb.xb[:, c, :], start=(c == 0), stop=sp))(),
            waits=tmax(t_xb, io.get(f"ps{kS1}_free")) if c == 0 else (), sem="T" if sp else None)
    for c in range(ND):
        sp = (c == ND - 1)
        tsq = p.op("tensor", (lambda c=c, sp=sp: lambda e: e.matmul(
            psS2[0:1, :], sb.ones_col_bf[:], sb.sq[:, c, :], start=(c == 0), stop=sp))(),
            waits=tmax(t_sq, io.get(f"ps{kS2}_free")) if c == 0 else (), sem="T" if sp else None)
    t1 = p.op("vector", lambda e: e.tensor_scalar_mul(sb.row_nm[:], psS1[0:1, :], -1.0 / D),
              waits=(tsx,), sem="V")
    t2 = p.op("vector", lambda e: e.tensor_scalar_mul(sb.row_t1[:], psS2[0:1, :], 1.0 / D),
              waits=(tsq,), sem="V")
    t3 = p.op("vector", lambda e: e.tensor_mul(sb.row_t2[:], sb.row_nm[:], sb.row_nm[:]),
              waits=(t1,), sem="V")
    t4 = p.op("vector", lambda e: e.tensor_sub(sb.row_t1[:], sb.row_t1[:], sb.row_t2[:]),
              waits=(t2, t3), sem="V")
    t5 = p.op("scalar", lambda e: e.activation(sb.row_t2[:], sb.row_t1[:], AF.Ln,
                                               bias=sb.cst[0:1, eps_col:eps_col + 1]),
              waits=(t4,), sem="S")
    t6 = p.op("scalar", lambda e: e.activation(sb.row_a[:], sb.row_t2[:], AF.Exp, scale=-0.5),
              waits=(t5,), sem="S")
    t7 = p.op("vector", lambda e: e.tensor_mul(sb.row_b[:], sb.row_nm[:], sb.row_a[:]),
              waits=(t6,), sem="V")
    tA = p.op("tensor", lambda e: e.matmul(psA[:], sb.ones_row_f[:], sb.row_a[:], start=True, stop=True),
              waits=tmax(t6, io.get(f"ps{kA}_free")), sem="T")
    tB = p.op("tensor", lambda e: e.matmul(psB[:], sb.ones_row_f[:], sb.row_b[:], start=True, stop=True),
              waits=tmax(t7, io.get(f"ps{kB}_free")), sem="T")
    last = None
    for c in range(ND):
        tm_ = p.op("vector", (lambda c=c: lambda e: e.tensor_mul(sb.xhat[:, c, :], xc[:, c, :], psA[:]))(),
                   waits=tmax(tA, xc_tok) if c == 0 else (), sem="V")
        last = p.op("vector", (lambda c=c: lambda e: e.tensor_add(
            sb.xhat[:, c, :], sb.xhat[:, c, :], psB[:]))(),
            waits=tmax(tm_, (tB,) if c == 0 else None), sem="V")
    t_cast = p.op("vector", lambda e: e.tensor_copy(
        sb.xhatb[:].rearrange("p c t -> p (c t)"), sb.xhat[:].rearrange("p c t -> p (c t)")),
        waits=(last,), sem="V")
    io[f"ps{kS1}_free"] = t1
    io[f"ps{kS2}_free"] = t2
    io[f"ps{kA}_free"] = last
    io[f"ps{kB}_free"] = last
    return last, t_cast



def emit_A(nc, p, palloc, PS, dr):
    xin = dr["xin"]
    xout = dr["xout"]

    with _nullctx():
        sb = _SB()
        def alloc(name, shape, dt=F32):
            t = palloc(name, shape, dt)
            setattr(sb, name, t)
            return t
        alloc("xc0", [128, ND, TC]); alloc("xc1", [128, ND, TC])
        alloc("xb", [128, ND, TC], BF16)
        alloc("sq", [128, ND, TC], BF16)
        alloc("xhat", [128, ND, TC])
        alloc("xhatb", [128, ND, TC], BF16)
        alloc("dt_scr", [128, TC])
        alloc("dth", [128, ND, TC], BF16); alloc("dtl", [128, ND, TC], BF16)
        alloc("uh", [128, ND, TC], BF16); alloc("ul", [128, ND, TC], BF16)
        alloc("u_tmp", [128, TC])
        alloc("zb", [128, TC]); alloc("azb", [128, TC])
        alloc("ebuf", [128, TC]); alloc("lbuf", [128, TC])
        alloc("rowB", [16, TC]); alloc("rowC", [16, TC])
        alloc("Bh", [16, TC], BF16); alloc("Bl", [16, TC], BF16)
        alloc("Ch", [16, TC], BF16); alloc("Cl", [16, TC], BF16)
        alloc("Bbc", [128, TC]); alloc("Cbc", [128, TC])
        alloc("dA0", [128, TC]); alloc("dA1", [128, TC])
        alloc("dB0", [128, TC]); alloc("dB1", [128, TC])
        alloc("st0", [128, TC]); alloc("st1", [128, TC])
        alloc("tmp0", [128, TC], BF16); alloc("tmp1", [128, TC], BF16)
        alloc("state_cols", [128, 128])
        alloc("ybf", [128, ND, TC], BF16)
        for r in ("row_nm", "row_t1", "row_t2", "row_a", "row_b"):
            alloc(r, [1, TC])
        alloc("wdtT", [1, D])
        alloc("bdt", [128, ND])
        alloc("wxpT", [128, ND, 32], BF16)
        alloc("bxp", [16, 2])
        alloc("wou0", [128, ND, 128], BF16)
        alloc("wou1", [128, ND, 128], BF16)
        alloc("bout", [128, ND])
        alloc("Acl", [128, 128])
        alloc("negA", [128, 128])
        alloc("LR", [128, 2048], BF16)
        alloc("LS", [128, 2048], BF16)
        alloc("SB16", [16, 128], BF16)
        alloc("ones_col_bf", [128, 1], BF16)
        alloc("ones_row_f", [1, 128])
        alloc("cst", [128, 8])


        wtoks = []
        for nm_ in ("wdtT", "bdt", "bxp", "bout", "Acl", "LR", "LS", "SB16",
                    "ones_col_bf", "ones_row_f", "cst"):
            wtoks.append(p.dma("sync", (lambda nm_=nm_: lambda e: e.dma_start(
                out=getattr(sb, nm_)[:], in_=dr[nm_]))(), sem="wld"))
        wtoks.append(p.dma("gpsimd", lambda e: e.dma_start(
            out=sb.wxpT[:], in_=dr["wxpT"].rearrange("(c p) e -> p c e", p=128)), sem="wldg"))
        wtok = tmax(*wtoks)
        t_ea = p.op("scalar", lambda e: e.activation(sb.negA[:], sb.Acl[:], AF.Exp),
                    waits=wtok, sem="S")
        t_na = p.op("vector", lambda e: e.tensor_scalar_mul(sb.negA[:], sb.negA[:], -1.0),
                    waits=(t_ea,), sem="V")
        wtok = tmax(wtok, t_na)
        t_sc = p.op("vector", lambda e: e.memset(sb.state_cols[:], 0.0), sem="V")

        io = {}
        store_toks = []
        xc_consumed = {0: (), 1: ()}
        scan_prev = {}
        scan_prev_all = t_sc
        for i in range(NTC):
            s = i % 2
            xc = sb.xc0 if s == 0 else sb.xc1
            ldx = p.dma("sync", (lambda i=i, xc=xc: lambda e: e.dma_start(
                out=xc[:], in_=xin[:, i * TC:(i + 1) * TC].rearrange("(c p) t -> p c t", p=128)))(),
                waits=xc_consumed[s], sem=f"xld{s}")
            xc_tok = tmax(ldx, wtok)

            t_apply, t_cast = emit_layernorm(p, sb, io, xc, xc_tok, kA=6, kB=7, kS1=4, kS2=5, PS=PS)

            dthl_toks = []
            u_toks = []
            prev_dt_readers = ()
            for c in range(ND):
                tz = p.op("tensor", (lambda c=c: lambda e: e.matmul(
                    PS[4][:], sb.wdtT[0:1, c * 128:(c + 1) * 128], sb.xhat[0:1, 0, :],
                    start=True, stop=True))(),
                    waits=tmax(t_apply, io.get("ps4_free")), sem="T")
                tzb = p.op("vector", (lambda c=c: lambda e: e.tensor_scalar_add(
                    sb.zb[:], PS[4][:], sb.bdt[:, c:c + 1]))(), waits=(tz,), sem="V")
                io["ps4_free"] = tzb
                tazb = p.op("scalar", lambda e: e.activation(sb.azb[:], sb.zb[:], AF.Abs),
                            waits=(tzb,), sem="S")
                te_ = p.op("scalar", lambda e: e.activation(sb.ebuf[:], sb.azb[:], AF.Exp, scale=-1.0),
                           waits=(tazb,), sem="S")
                tl_ = p.op("scalar", lambda e: e.activation(sb.lbuf[:], sb.ebuf[:], AF.Ln,
                                                            bias=sb.cst[:, 0:1]),
                           waits=(te_,), sem="S")
                tdt = p.op("vector", (lambda c=c: lambda e: e.scalar_tensor_tensor(
                    sb.dt_scr[:], sb.zb[:], 0.0, sb.lbuf[:], ALU.max, ALU.add))(),
                    waits=tmax(tl_, tzb, prev_dt_readers), sem="V")
                tdh = p.op("vector", (lambda c=c: lambda e: e.tensor_copy(
                    sb.dth[:, c, :], sb.dt_scr[:]))(), waits=(tdt,), sem="V")
                tdl = p.op("vector", (lambda c=c: lambda e: e.tensor_sub(
                    sb.dtl[:, c, :], sb.dt_scr[:], sb.dth[:, c, :]))(), waits=(tdh,), sem="V")
                tu = p.op("vector", (lambda c=c: lambda e: e.tensor_mul(
                    sb.u_tmp[:], sb.dt_scr[:], sb.xhat[:, c, :]))(), waits=(tdt,), sem="V")
                th_ = p.op("vector", (lambda c=c: lambda e: e.tensor_copy(
                    sb.uh[:, c, :], sb.u_tmp[:]))(), waits=(tu,), sem="V")
                tl2 = p.op("vector", (lambda c=c: lambda e: e.tensor_sub(
                    sb.ul[:, c, :], sb.u_tmp[:], sb.uh[:, c, :]))(), waits=(th_,), sem="V")
                dthl_toks.append(tmax(tdh, tdl))
                u_toks.append(tl2)
                prev_dt_readers = tmax(tdl, tu)

            tBp = tCp = None
            for c in range(ND):
                sp_ = (c == ND - 1)
                tBp = p.op("tensor", (lambda c=c, sp_=sp_: lambda e: e.matmul(
                    PS[4][0:16, :], sb.wxpT[:, c, 0:16], sb.xhatb[:, c, :], start=(c == 0), stop=sp_))(),
                    waits=tmax(t_cast, io.get("ps4_free")) if c == 0 else (), sem="T" if sp_ else None)
            for c in range(ND):
                sp_ = (c == ND - 1)
                tCp = p.op("tensor", (lambda c=c, sp_=sp_: lambda e: e.matmul(
                    PS[5][0:16, :], sb.wxpT[:, c, 16:32], sb.xhatb[:, c, :], start=(c == 0), stop=sp_))(),
                    waits=tmax(io.get("ps5_free")) if c == 0 else (), sem="T" if sp_ else None)
            tBr = p.op("vector", lambda e: e.tensor_scalar_add(sb.rowB[:], PS[4][0:16, :], sb.bxp[:, 0:1]),
                       waits=(tBp,), sem="V")
            tCr = p.op("vector", lambda e: e.tensor_scalar_add(sb.rowC[:], PS[5][0:16, :], sb.bxp[:, 1:2]),
                       waits=(tCp,), sem="V")
            io["ps4_free"] = tBr
            io["ps5_free"] = tCr
            tBh = p.op("vector", lambda e: e.tensor_copy(sb.Bh[:], sb.rowB[:]), waits=(tBr,), sem="V")
            tBl = p.op("vector", lambda e: e.tensor_sub(sb.Bl[:], sb.rowB[:], sb.Bh[:]), waits=(tBh,), sem="V")
            tCh = p.op("vector", lambda e: e.tensor_copy(sb.Ch[:], sb.rowC[:]), waits=(tCr,), sem="V")
            tCl = p.op("vector", lambda e: e.tensor_sub(sb.Cl[:], sb.rowC[:], sb.Ch[:]), waits=(tCh,), sem="V")
            p.op("tensor", lambda e: e.matmul(PS[6][:], sb.SB16[:], sb.Bh[:], start=True, stop=False),
                 waits=tmax(tBh, io.get("ps6_free")), sem=None)
            tb2 = p.op("tensor", lambda e: e.matmul(PS[6][:], sb.SB16[:], sb.Bl[:], start=False, stop=True),
                       waits=(tBl,), sem="T")
            p.op("tensor", lambda e: e.matmul(PS[7][:], sb.SB16[:], sb.Ch[:], start=True, stop=False),
                 waits=tmax(tCh, io.get("ps7_free")), sem=None)
            tc2 = p.op("tensor", lambda e: e.matmul(PS[7][:], sb.SB16[:], sb.Cl[:], start=False, stop=True),
                       waits=(tCl,), sem="T")
            tBbc = p.op("vector", lambda e: e.tensor_copy(sb.Bbc[:], PS[6][:]), waits=(tb2,), sem="V")
            tCbc = p.op("vector", lambda e: e.tensor_copy(sb.Cbc[:], PS[7][:]), waits=(tc2,), sem="V")
            io["ps6_free"] = tBbc
            io["ps7_free"] = tCbc

            y_done = {}
            rep_free = {0: None, 1: None}
            ubc_free = {0: None, 1: None}
            buf_free = {0: None, 1: None}
            pend_y = None

            def emit_pend(pend):
                c_, j_, tmb_, ttm_ = pend
                ps_y = PS[4 + (c_ % 2)]
                ty = p.op("tensor", (lambda j_=j_, tmb_=tmb_, ps_y=ps_y: lambda e: e.matmul(
                    ps_y[:], sb.LS[:, j_ * 128:(j_ + 1) * 128], tmb_[:],
                    start=(j_ == 0), stop=(j_ == 15)))(),
                    waits=tmax(ttm_, (y_done.get(c_ % 2) if j_ == 0 else None),
                               (io.get(f"ps{4 + c_ % 2}_free") if j_ == 0 else None)),
                    sem="T" if j_ == 15 else None)
                if j_ == 15:
                    tev = p.op("scalar", (lambda c_=c_, ps_y=ps_y: lambda e: e.activation(
                        sb.ybf[:, c_, :], ps_y[:], AF.Copy))(), waits=(ty,), sem="S")
                    y_done[c_ % 2] = tev

            for j2 in range(128):
                c, j = j2 // 16, j2 % 16
                r = j2 % 2
                ps_rep = PS[0 + r]
                ps_ubc = PS[2 + r]
                dAb = sb.dA0 if r == 0 else sb.dA1
                dBb = sb.dB0 if r == 0 else sb.dB1
                stb = sb.st0 if r == 0 else sb.st1
                tmb = sb.tmp0 if r == 0 else sb.tmp1

                p.op("tensor", (lambda c=c, j=j, ps_rep=ps_rep: lambda e: e.matmul(
                    ps_rep[:], sb.LR[:, j * 128:(j + 1) * 128], sb.dth[:, c, :], start=True, stop=False))(),
                    waits=tmax(dthl_toks[c], rep_free[r]), sem=None)
                t2_ = p.op("tensor", (lambda c=c, j=j, ps_rep=ps_rep: lambda e: e.matmul(
                    ps_rep[:], sb.LR[:, j * 128:(j + 1) * 128], sb.dtl[:, c, :], start=False, stop=True))(),
                    sem="T")
                tdA = p.op("scalar", (lambda dAb=dAb, ps_rep=ps_rep, j2=j2: lambda e: e.activation(
                    dAb[:], ps_rep[:], AF.Exp, scale=sb.negA[:, j2:j2 + 1]))(),
                    waits=tmax(t2_, buf_free[r]), sem="S")
                rep_free[r] = tdA
                p.op("tensor", (lambda c=c, j=j, ps_ubc=ps_ubc: lambda e: e.matmul(
                    ps_ubc[:], sb.LR[:, j * 128:(j + 1) * 128], sb.uh[:, c, :], start=True, stop=False))(),
                    waits=tmax(u_toks[c], ubc_free[r]), sem=None)
                t4_ = p.op("tensor", (lambda c=c, j=j, ps_ubc=ps_ubc: lambda e: e.matmul(
                    ps_ubc[:], sb.LR[:, j * 128:(j + 1) * 128], sb.ul[:, c, :], start=False, stop=True))(),
                    sem="T")
                if pend_y is not None:
                    emit_pend(pend_y)
                tdB = p.op("vector", (lambda dBb=dBb, ps_ubc=ps_ubc: lambda e: e.tensor_mul(
                    dBb[:], ps_ubc[:], sb.Bbc[:]))(), waits=tmax(t4_, tBbc, buf_free[r]), sem="V")
                ubc_free[r] = tdB
                tsc = p.op("vector", (lambda dAb=dAb, dBb=dBb, stb=stb, j2=j2: lambda e: e.tensor_tensor_scan(
                    stb[:], dAb[:], dBb[:], sb.state_cols[:, j2:j2 + 1], ALU.mult, ALU.add))(),
                    waits=tmax(tdA, tdB, scan_prev.get(j2, scan_prev_all)), sem="V")
                tsv = p.op("vector", (lambda stb=stb, j2=j2: lambda e: e.tensor_copy(
                    sb.state_cols[:, j2:j2 + 1], stb[:, TC - 1:TC]))(), waits=(tsc,), sem="V")
                scan_prev[j2] = tsv
                ttm = p.op("gpsimd", (lambda stb=stb, tmb=tmb: lambda e: e.tensor_tensor(
                    tmb[:], stb[:], sb.Cbc[:], ALU.mult))(), waits=tmax(tsv, tCbc), sem="G")
                buf_free[r] = ttm
                pend_y = (c, j, tmb, ttm)
            emit_pend(pend_y)
            pend_y = None
            io["ps4_free"] = y_done[0]
            io["ps5_free"] = y_done[1]
            y_all = tmax(y_done[0], y_done[1])

            proj_toks = []
            for o in range(ND):
                wsl = o % 2
                wob = sb.wou0 if wsl == 0 else sb.wou1
                two = p.dma("gpsimd", (lambda o=o, wob=wob: lambda e: e.dma_start(
                    out=wob[:], in_=dr["woutT"][:, o * 128:(o + 1) * 128]
                    .rearrange("(c p) o2 -> p c o2", p=128)))(),
                    waits=tmax(io.get(f"wou{wsl}_rd")), sem=f"would{wsl}")
                ko = 6 + (o % 2)
                ps_o = PS[ko]
                tpo = None
                for c in range(ND):
                    sp_ = (c == ND - 1)
                    w0 = tmax(y_all, two, io.get(f"ps{ko}_free")) if c == 0 else ()
                    tpo = p.op("tensor", (lambda o=o, c=c, sp_=sp_, ps_o=ps_o, wob=wob: lambda e: e.matmul(
                        ps_o[:], wob[:, c, :], sb.ybf[:, c, :],
                        start=(c == 0), stop=sp_))(),
                        waits=w0, sem="T" if sp_ else None)
                io[f"wou{wsl}_rd"] = (tpo,)
                tst = p.op("vector", (lambda o=o, ps_o=ps_o, xc=xc: lambda e: e.scalar_tensor_tensor(
                    xc[:, o, :], ps_o[:], sb.bout[:, o:o + 1], xc[:, o, :], ALU.add, ALU.add))(),
                    waits=tmax(tpo, wtok), sem="V")
                io[f"ps{ko}_free"] = tst
                proj_toks.append(tst)

            tst_d = p.dma("sync", (lambda i=i, xc=xc: lambda e: e.dma_start(
                out=xout[:, i * TC:(i + 1) * TC].rearrange("(c p) t -> p c t", p=128), in_=xc[:]))(),
                waits=tuple(proj_toks), sem=f"xst{s}")
            store_toks.append(tst_d)
            xc_consumed[s] = tmax(tst_d, *proj_toks)
            scan_prev_all = scan_prev[127]

        p.op("sync", lambda e: None, waits=tmax(*store_toks))
    return None




_A_DINS = [("wdtT", [1, D], F32), ("bdt", [128, ND], F32), ("wxpT", [D, 32], F32),
           ("bxp", [16, 2], F32), ("woutT", [D, D], F32), ("bout", [128, ND], F32),
           ("Acl", [128, 128], F32), ("LR", [128, 2048], BF16), ("LS", [128, 2048], BF16),
           ("SB16", [16, 128], BF16), ("ones_col_bf", [128, 1], BF16),
           ("ones_row_f", [1, 128], F32), ("cst", [128, 8], F32)]


def _standalone(emit, dins, extra=()):
    nc = bass.Bass("TRN2", target_bir_lowering=False)
    dr = {"xin": nc.dram_tensor("xin", [D, T], F32, kind="ExternalInput").ap()}
    for (nm, shp, dt) in dins:
        dr[nm] = nc.dram_tensor(nm, shp, dt, kind="ExternalInput").ap()
    dr["xout"] = nc.dram_tensor("xout", [D, T], F32, kind="ExternalOutput").ap()
    for (nm, shp, dt, kind) in extra:
        dr[nm] = nc.dram_tensor(nm, shp, dt, kind=kind).ap()
    with ExitStack() as ctx:
        p = Prog(nc)
        PS = [ctx.enter_context(nc.psum_tensor(f"PS{k}", [128, TC], F32)) for k in range(8)]
        allocd = {}
        def palloc(name, shape, dt=F32):
            if name not in allocd:
                allocd[name] = ctx.enter_context(nc.sbuf_tensor("s_" + name, shape, dt))
            return allocd[name]
        emit(nc, p, palloc, PS, dr)
        p.build(ctx)
    return nc


def build_launchA():
    return _standalone(emit_A, _A_DINS)

def _acl(A_log):
    out = np.empty((128, 128), np.float32)
    for j2 in range(128):
        cc, j = j2 // 16, j2 % 16
        for pp in range(128):
            out[pp, j2] = A_log[cc * 128 + j * 8 + pp // 16, pp % 16]
    return out


def _prep_A(inputs, b):
    c = _CONSTS
    x = np.ascontiguousarray(np.asarray(inputs["L0_attn_x"][b]).T)
    return {
        "xin": x,
        "wdtT": np.ascontiguousarray(np.asarray(inputs["L0_attn_W_dt"]).reshape(1, D)),
        "bdt": np.ascontiguousarray(np.asarray(inputs["L0_attn_b_dt"]).reshape(ND, 128).T),
        "wxpT": np.ascontiguousarray(np.asarray(inputs["L0_attn_W_xp"]).T),
        "bxp": np.ascontiguousarray(np.asarray(inputs["L0_attn_b_xp"]).reshape(2, 16).T),
        "woutT": np.ascontiguousarray(np.asarray(inputs["L0_attn_W_out"]).T),
        "bout": np.ascontiguousarray(np.asarray(inputs["L0_attn_b_out"]).reshape(ND, 128).T),
        "Acl": _acl(np.asarray(inputs["L0_attn_A_log"])),
        "LR": c["LR"], "LS": c["LS"], "SB16": c["SB16"],
        "ones_col_bf": c["ones_col_bf"], "ones_row_f": c["ones_row_f"], "cst": c["cst"],
    }


def kernel_unfused(**inputs):
    cores = list(range(NCORE))
    resA = run_bass_kernel_spmd(build_launchA(), [_prep_A(inputs, b) for b in cores], core_ids=cores)
    x1 = [np.asarray(r["xout"]) for r in resA.results]
    resB = run_bass_kernel_spmd(build_launchB(), [_prep_B(inputs, b, x1[b]) for b in cores], core_ids=cores)
    x2 = [np.asarray(r["xout"]) for r in resB.results]
    resC = run_bass_kernel_spmd(build_launchC(), [_prep_C(inputs, b, x2[b]) for b in cores], core_ids=cores)
    x3 = [np.asarray(r["xout"]) for r in resC.results]
    resD1 = run_bass_kernel_spmd(build_launchD1(), [_prep_D1(inputs, b, x3[b]) for b in cores], core_ids=cores)
    hds = [np.asarray(r["hd"]) for r in resD1.results]
    resD2 = run_bass_kernel_spmd(build_launchD2(), [_prep_D2(inputs, b, x3[b], hds[b]) for b in cores], core_ids=cores)
    out = np.stack([np.asarray(r["xout"]).T for r in resD2.results])
    return out.astype(np.float32)



SLAB = 1024
NSLAB = T // SLAB
HB = 512
NHB = 4096 // HB
NHC = HB // 128


def emit_norm2(p, sb, io, PS, xcv, xc_tok, xhatb_v, banks=(6, 7, 4, 5), eps_col=3,
               rms=False, scale_col=None):
    kA, kB, kS1, kS2 = banks
    psA, psB, psS1, psS2 = PS[kA], PS[kB], PS[kS1], PS[kS2]
    t_sq = t_xb = None
    tsx = tsq = None
    if not rms:
        t_xb = p.op("vector", lambda e: e.tensor_copy(sb.xb[:], xcv), waits=xc_tok, sem="V")
    t_sq = p.op("scalar", lambda e: e.activation(sb.sq[:], xcv, AF.Square),
                waits=xc_tok, sem="S")
    if not rms:
        for c in range(ND):
            sp = (c == ND - 1)
            tsx = p.op("tensor", (lambda c=c, sp=sp: lambda e: e.matmul(
                psS1[0:1, :], sb.ones_col_bf[:], sb.xb[:, c, :], start=(c == 0), stop=sp))(),
                waits=tmax(t_xb, io.get(f"ps{kS1}_free")) if c == 0 else (), sem="T" if sp else None)
    for c in range(ND):
        sp = (c == ND - 1)
        tsq = p.op("tensor", (lambda c=c, sp=sp: lambda e: e.matmul(
            psS2[0:1, :], sb.ones_col_bf[:], sb.sq[:, c, :], start=(c == 0), stop=sp))(),
            waits=tmax(t_sq, io.get(f"ps{kS2}_free")) if c == 0 else (), sem="T" if sp else None)
    if rms:
        t2 = p.op("vector", lambda e: e.tensor_scalar_mul(sb.row_t1[:], psS2[0:1, :], 1.0 / D),
                  waits=(tsq,), sem="V")
        t5 = p.op("scalar", lambda e: e.activation(sb.row_t2[:], sb.row_t1[:], AF.Ln,
                                                   bias=sb.cst[0:1, eps_col:eps_col + 1]),
                  waits=(t2,), sem="S")
        t6 = p.op("scalar", lambda e: e.activation(sb.row_a[:], sb.row_t2[:], AF.Exp, scale=-0.5),
                  waits=(t5,), sem="S")
        io[f"ps{kS2}_free"] = t2
        tA = p.op("tensor", lambda e: e.matmul(psA[:], sb.ones_row_f[:], sb.row_a[:],
                                               start=True, stop=True),
                  waits=tmax(t6, io.get(f"ps{kA}_free")), sem="T")
        last = None
        for c in range(ND):
            if scale_col is not None:
                last = p.op("vector", (lambda c=c: lambda e: e.scalar_tensor_tensor(
                    xhatb_v[:, c, :], xcv[:, c, :], scale_col[:, c:c + 1], psA[:],
                    ALU.mult, ALU.mult))(),
                    waits=tmax(tA, xc_tok) if c == 0 else (), sem="V")
            else:
                last = p.op("vector", (lambda c=c: lambda e: e.tensor_mul(
                    xhatb_v[:, c, :], xcv[:, c, :], psA[:]))(),
                    waits=tmax(tA, xc_tok) if c == 0 else (), sem="V")
        io[f"ps{kA}_free"] = last
        return last
    t1 = p.op("vector", lambda e: e.tensor_scalar_mul(sb.row_nm[:], psS1[0:1, :], -1.0 / D),
              waits=(tsx,), sem="V")
    t2 = p.op("vector", lambda e: e.tensor_scalar_mul(sb.row_t1[:], psS2[0:1, :], 1.0 / D),
              waits=(tsq,), sem="V")
    t3 = p.op("vector", lambda e: e.tensor_mul(sb.row_t2[:], sb.row_nm[:], sb.row_nm[:]),
              waits=(t1,), sem="V")
    t4 = p.op("vector", lambda e: e.tensor_sub(sb.row_t1[:], sb.row_t1[:], sb.row_t2[:]),
              waits=(t2, t3), sem="V")
    t5 = p.op("scalar", lambda e: e.activation(sb.row_t2[:], sb.row_t1[:], AF.Ln,
                                               bias=sb.cst[0:1, eps_col:eps_col + 1]),
              waits=(t4,), sem="S")
    t6 = p.op("scalar", lambda e: e.activation(sb.row_a[:], sb.row_t2[:], AF.Exp, scale=-0.5),
              waits=(t5,), sem="S")
    t7 = p.op("vector", lambda e: e.tensor_mul(sb.row_b[:], sb.row_nm[:], sb.row_a[:]),
              waits=(t6,), sem="V")
    tA = p.op("tensor", lambda e: e.matmul(psA[:], sb.ones_row_f[:], sb.row_a[:], start=True, stop=True),
              waits=tmax(t6, io.get(f"ps{kA}_free")), sem="T")
    tB = p.op("tensor", lambda e: e.matmul(psB[:], sb.ones_row_f[:], sb.row_b[:], start=True, stop=True),
              waits=tmax(t7, io.get(f"ps{kB}_free")), sem="T")
    last = None
    for c in range(ND):
        tm_ = p.op("vector", (lambda c=c: lambda e: e.tensor_mul(sb.ln_scr[:], xcv[:, c, :], psA[:]))(),
                   waits=tmax(tA, xc_tok, io.get("ln_scr_free")) if c == 0 else (io.get("ln_scr_free"),),
                   sem="V")
        last = p.op("vector", (lambda c=c: lambda e: e.tensor_add(
            xhatb_v[:, c, :], sb.ln_scr[:], psB[:]))(),
            waits=tmax(tm_, (tB,) if c == 0 else None), sem="V")
        io["ln_scr_free"] = last
    io[f"ps{kS1}_free"] = t1
    io[f"ps{kS2}_free"] = t2
    io[f"ps{kA}_free"] = last
    io[f"ps{kB}_free"] = last
    return last


def emit_B(nc, p, palloc, PS, dr):
    xin = dr["xin"]
    xout = dr["xout"]

    with _nullctx():
        sb = _SB()
        def alloc(name, shape, dt=F32):
            t = palloc(name, shape, dt)
            setattr(sb, name, t)
            return t
        alloc("xc", [128, ND, SLAB])
        alloc("xhatb", [128, ND, SLAB], BF16)
        alloc("xhatw", [128, ND, SLAB], BF16)
        alloc("out_acc", [128, ND, SLAB])
        alloc("xb", [128, ND, TC], BF16)
        alloc("sq", [128, ND, TC], BF16)
        alloc("ln_scr", [128, TC])
        for r in ("row_nm", "row_t1", "row_t2", "row_a", "row_b"):
            alloc(r, [1, TC])
        alloc("wg_bf", [128, ND, 8], BF16)
        alloc("bg_bc", [128, 8])
        alloc("rsc", [128, 48])
        alloc("w_T8", [8, SLAB])
        alloc("wbc", [128, SLAB])
        alloc("W1b0", [128, ND, HB], BF16); alloc("W1b1", [128, ND, HB], BF16)
        alloc("W2b0", [128, ND, HB], BF16); alloc("W2b1", [128, ND, HB], BF16)
        alloc("W3b0", [128, NHC, D], BF16); alloc("W3b1", [128, NHC, D], BF16)
        alloc("sg0", [128, TC], BF16); alloc("sg1", [128, TC], BF16)
        alloc("h0", [128, NHC, TC], BF16); alloc("h1", [128, NHC, TC], BF16)
        alloc("identf", [128, 128])
        alloc("E8", [8, 1024])
        alloc("ones_col_bf", [128, 1], BF16)
        alloc("ones_row_f", [1, 128])
        alloc("cst", [128, 8])


        wtoks = []
        for nm_ in ("identf", "E8", "ones_col_bf", "ones_row_f", "cst", "bgrow_"):
            pass
        for nm_ in ("identf", "E8", "ones_col_bf", "ones_row_f", "cst"):
            wtoks.append(p.dma("sync", (lambda nm_=nm_: lambda e: e.dma_start(
                out=getattr(sb, nm_)[:], in_=dr[nm_]))(), sem="wld"))
        wtoks.append(p.dma("gpsimd", lambda e: e.dma_start(
            out=sb.wg_bf[:], in_=dr["wgT"].rearrange("(c p) e -> p c e", p=128)), sem="wldg"))
        wtok = tmax(*wtoks)
        tbg0 = p.dma("sync", lambda e: e.dma_start(out=sb.rsc[0:1, 0:8], in_=dr["bgrow"]), sem="wld")
        tbgm = p.op("tensor", lambda e: e.matmul(PS[6][:, 0:8], sb.ones_row_f[:], sb.rsc[0:1, 0:8],
                                                 start=True, stop=True),
                    waits=tmax(wtok, tbg0), sem="T")
        tbg = p.op("vector", lambda e: e.tensor_copy(sb.bg_bc[:], PS[6][:, 0:8]), waits=(tbgm,), sem="V")
        wtok = tmax(wtok, tbg)
        io = {"ps6_free": tbg}

        store_toks = []
        xc_consumed = ()
        acc_consumed = ()
        wslot_free = {0: (), 1: ()}
        for s in range(NSLAB):
            ldx = p.dma("sync", (lambda s=s: lambda e: e.dma_start(
                out=sb.xc[:], in_=xin[:, s * SLAB:(s + 1) * SLAB].rearrange("(c p) t -> p c t", p=128)))(),
                waits=xc_consumed, sem="xld")
            xc_tok = tmax(ldx, wtok)
            t_ms = p.op("vector", lambda e: e.memset(sb.out_acc[:].rearrange("p c t -> p (c t)"), 0.0),
                        waits=acc_consumed, sem="V")

            ln_last = None
            for q in range(SLAB // TC):
                ln_last = emit_norm2(p, sb, io, PS,
                                     sb.xc[:, :, q * TC:(q + 1) * TC], xc_tok,
                                     sb.xhatb[:, :, q * TC:(q + 1) * TC])

            wT_toks = []
            for tt in range(SLAB // 128):
                tlg = None
                for c in range(ND):
                    sp = (c == ND - 1)
                    tlg = p.op("tensor", (lambda c=c, tt=tt, sp=sp: lambda e: e.matmul(
                        PS[4][:, 0:8], sb.xhatb[:, c, tt * 128:(tt + 1) * 128], sb.wg_bf[:, c, :],
                        start=(c == 0), stop=sp))(),
                        waits=tmax(ln_last, io.get("ps4_free")) if c == 0 else (),
                        sem="T" if sp else None)
                lg = sb.rsc[:, 0:8]; msk = sb.rsc[:, 8:16]; tmp = sb.rsc[:, 16:24]
                e_ = sb.rsc[:, 24:32]; wraw = sb.rsc[:, 32:40]
                m1 = sb.rsc[:, 40:41]; m1n = sb.rsc[:, 41:42]; m2 = sb.rsc[:, 42:43]
                den = sb.rsc[:, 43:44]; dinv = sb.rsc[:, 44:45]; wt = sb.rsc[:, 0:8]
                ta = p.op("vector", (lambda lg=lg: lambda e: e.tensor_add(lg, PS[4][:, 0:8], sb.bg_bc[:]))(),
                          waits=tmax(tlg, io.get("rsc_free")), sem="V")
                io["ps4_free"] = ta
                tb = p.op("vector", (lambda: lambda e: e.reduce_max(m1, lg, axis=AX.X))(),
                          waits=(ta,), sem="V")
                tc_ = p.op("vector", (lambda: lambda e: e.tensor_single_scalar(msk, lg, m1[:, 0:1], ALU.is_equal))(),
                           waits=(tb,), sem="V")
                td = p.op("vector", (lambda: lambda e: e.tensor_scalar_mul(m1n, m1, -1.0))(),
                          waits=(tb,), sem="V")
                te_ = p.op("vector", (lambda: lambda e: e.scalar_tensor_tensor(
                    tmp, msk, -1e9, lg, ALU.mult, ALU.add))(), waits=(tc_,), sem="V")
                tf = p.op("vector", (lambda: lambda e: e.reduce_max(m2, tmp, axis=AX.X))(),
                          waits=(te_,), sem="V")
                tg = p.op("scalar", (lambda: lambda e: e.activation(e_, lg, AF.Exp, bias=m1n[:, 0:1]))(),
                          waits=(td, te_), sem="S")
                th = p.op("vector", (lambda: lambda e: e.tensor_single_scalar(msk, lg, m2[:, 0:1], ALU.is_ge))(),
                          waits=(tf, tg), sem="V")
                ti = p.op("vector", (lambda: lambda e: e.tensor_mul(wraw, e_, msk))(),
                          waits=(th, tg), sem="V")
                tj = p.op("vector", (lambda: lambda e: e.reduce_sum(den, wraw, axis=AX.X))(),
                          waits=(ti,), sem="V")
                tk = p.op("vector", (lambda: lambda e: e.reciprocal(dinv, den))(),
                          waits=(tj,), sem="V")
                tl = p.op("vector", (lambda: lambda e: e.tensor_single_scalar(wt, wraw, dinv[:, 0:1], ALU.mult))(),
                          waits=(tk,), sem="V")
                ttr = p.op("tensor", (lambda wt=wt: lambda e: e.transpose(
                    PS[5][0:8, 0:128], wt, sb.identf[:]))(),
                    waits=tmax(tl, io.get("ps5_free")), sem="T")
                tcp = p.op("vector", (lambda tt=tt: lambda e: e.tensor_copy(
                    sb.w_T8[:, tt * 128:(tt + 1) * 128], PS[5][0:8, 0:128]))(),
                    waits=(ttr,), sem="V")
                io["ps5_free"] = tcp
                io["rsc_free"] = ttr
                wT_toks.append(tcp)
            wT_tok = tmax(*wT_toks)

            for ei in range(8):
                wbc_toks = []
                for q in range(SLAB // TC):
                    tw1 = p.op("tensor", (lambda ei=ei, q=q: lambda e: e.matmul(
                        PS[6][:], sb.E8[0:8, ei * 128:(ei + 1) * 128],
                        sb.w_T8[0:8, q * TC:(q + 1) * TC], start=True, stop=True))(),
                        waits=tmax(wT_tok, io.get("ps6_free")), sem="T")
                    tw2 = p.op("vector", (lambda q=q: lambda e: e.tensor_copy(
                        sb.wbc[:, q * TC:(q + 1) * TC], PS[6][:]))(),
                        waits=tmax(tw1, io.get("wbc_free")), sem="V")
                    io["ps6_free"] = tw2
                    wbc_toks.append(tw2)
                twb = tmax(*wbc_toks)
                txw_toks = []
                for c in range(ND):
                    txw = p.op("vector", (lambda c=c: lambda e: e.tensor_mul(
                        sb.xhatw[:, c, :], sb.xhatb[:, c, :], sb.wbc[:]))(),
                        waits=tmax(twb, io.get("xhatw_free")) if c == 0 else (), sem="V")
                    txw_toks.append(txw)
                txw_tok = txw_toks[-1]

                for hb in range(NHB):
                    sl = hb % 2
                    W1b = sb.W1b0 if sl == 0 else sb.W1b1
                    W2b = sb.W2b0 if sl == 0 else sb.W2b1
                    W3b = sb.W3b0 if sl == 0 else sb.W3b1
                    t_w1 = p.dma("gpsimd", (lambda ei=ei, hb=hb, W1b=W1b: lambda e: e.dma_start(
                        out=W1b[:], in_=dr["W1T"][ei, :, hb * HB:(hb + 1) * HB]
                        .rearrange("(c p) h -> p c h", p=128)))(),
                        waits=wslot_free[sl], sem=f"w1ld{sl}")
                    t_w2 = p.dma("gpsimd", (lambda ei=ei, hb=hb, W2b=W2b: lambda e: e.dma_start(
                        out=W2b[:], in_=dr["W2T"][ei, :, hb * HB:(hb + 1) * HB]
                        .rearrange("(c p) h -> p c h", p=128)))(),
                        sem=f"w2ld{sl}")
                    t_w3 = p.dma("gpsimd", (lambda ei=ei, hb=hb, W3b=W3b: lambda e: e.dma_start(
                        out=W3b[:], in_=dr["W3T"][ei, hb * HB:(hb + 1) * HB, :]
                        .rearrange("(c p) o -> p c o", p=128)))(),
                        sem=f"w3ld{sl}")
                    wrd = []
                    for tcq in range(SLAB // TC):
                        h_sl = tcq % 2
                        hbuf = sb.h0 if h_sl == 0 else sb.h1
                        h_toks = []
                        for hc in range(NHC):
                            gr = hc % 2
                            ps_g = PS[0 + gr]
                            ps_u = PS[2 + gr]
                            sg = sb.sg0 if gr == 0 else sb.sg1
                            tgm = None
                            for c in range(ND):
                                sp = (c == ND - 1)
                                tgm = p.op("tensor", (lambda c=c, hc=hc, tcq=tcq, sp=sp, ps_g=ps_g, W1b=W1b: lambda e: e.matmul(
                                    ps_g[:], W1b[:, c, hc * 128:(hc + 1) * 128],
                                    sb.xhatb[:, c, tcq * TC:(tcq + 1) * TC], start=(c == 0), stop=sp))(),
                                    waits=tmax(t_w1, ln_last, io.get(f"ps{0 + gr}_free")) if c == 0 else (),
                                    sem="T" if sp else None)
                            tsl = p.op("scalar", (lambda sg=sg, ps_g=ps_g: lambda e: e.activation(
                                sg[:], ps_g[:], AF.Silu))(),
                                waits=tmax(tgm, io.get(f"sg{gr}_free")), sem="S")
                            io[f"ps{0 + gr}_free"] = tsl
                            tum = None
                            for c in range(ND):
                                sp = (c == ND - 1)
                                tum = p.op("tensor", (lambda c=c, hc=hc, tcq=tcq, sp=sp, ps_u=ps_u, W2b=W2b: lambda e: e.matmul(
                                    ps_u[:], W2b[:, c, hc * 128:(hc + 1) * 128],
                                    sb.xhatw[:, c, tcq * TC:(tcq + 1) * TC], start=(c == 0), stop=sp))(),
                                    waits=tmax(t_w2, txw_tok, io.get(f"ps{2 + gr}_free")) if c == 0 else (),
                                    sem="T" if sp else None)
                            wrd.append(tgm); wrd.append(tum)
                            thm = p.op("vector", (lambda hbuf=hbuf, hc=hc, sg=sg, ps_u=ps_u: lambda e: e.tensor_mul(
                                hbuf[:, hc, :], sg[:], ps_u[:]))(),
                                waits=tmax(tsl, tum, io.get(f"h{h_sl}_free")), sem="V")
                            io[f"ps{2 + gr}_free"] = thm
                            io[f"sg{gr}_free"] = thm
                            h_toks.append(thm)
                        acc_toks = []
                        for o in range(ND):
                            dr_ = o % 2
                            ps_d = PS[4 + dr_]
                            tdm = None
                            for hc in range(NHC):
                                sp = (hc == NHC - 1)
                                tdm = p.op("tensor", (lambda o=o, hc=hc, sp=sp, ps_d=ps_d, W3b=W3b, hbuf=hbuf: lambda e: e.matmul(
                                    ps_d[:], W3b[:, hc, o * 128:(o + 1) * 128], hbuf[:, hc, :],
                                    start=(hc == 0), stop=sp))(),
                                    waits=tmax(t_w3, h_toks[-1], io.get(f"ps{4 + dr_}_free")) if hc == 0 else (),
                                    sem="T" if sp else None)
                            tac = p.op("vector", (lambda o=o, tcq=tcq, ps_d=ps_d: lambda e: e.tensor_add(
                                sb.out_acc[:, o, tcq * TC:(tcq + 1) * TC],
                                sb.out_acc[:, o, tcq * TC:(tcq + 1) * TC], ps_d[:]))(),
                                waits=tmax(tdm, t_ms), sem="V")
                            io[f"ps{4 + dr_}_free"] = tac
                            wrd.append(tdm)
                            acc_toks.append(tac)
                        io[f"h{h_sl}_free"] = acc_toks[-1]
                    wslot_free[sl] = tmax(*wrd)
                io["xhatw_free"] = tmax(wslot_free[0], wslot_free[1])
                io["wbc_free"] = tmax(wslot_free[0], wslot_free[1])

            res_toks = []
            last_acc = io.get("ps4_free"), io.get("ps5_free")
            for o in range(ND):
                tr = p.op("vector", (lambda o=o: lambda e: e.tensor_add(
                    sb.xc[:, o, :], sb.xc[:, o, :], sb.out_acc[:, o, :]))(),
                    waits=tmax(*last_acc) if o == 0 else (), sem="V")
                res_toks.append(tr)
            tst_d = p.dma("sync", (lambda s=s: lambda e: e.dma_start(
                out=xout[:, s * SLAB:(s + 1) * SLAB].rearrange("(c p) t -> p c t", p=128),
                in_=sb.xc[:]))(), waits=tuple(res_toks), sem="xst")
            store_toks.append(tst_d)
            xc_consumed = tmax(tst_d, *res_toks)
            acc_consumed = tmax(*res_toks)

        p.op("sync", lambda e: None, waits=tmax(*store_toks))
    return None



_B_DINS = [("wgT", [D, 8], F32), ("bgrow", [1, 8], F32), ("W1T", [8, D, 4096], F32),
           ("W2T", [8, D, 4096], F32), ("W3T", [8, 4096, D], F32),
           ("identf", [128, 128], F32), ("E8", [8, 1024], F32),
           ("ones_col_bf", [128, 1], BF16), ("ones_row_f", [1, 128], F32),
           ("cst", [128, 8], F32)]


def build_launchB():
    return _standalone(emit_B, _B_DINS)

def _prep_B(inputs, b, xin, cache={}):
    c = _CONSTS
    key = id(inputs.get("L0_ffn_Wgate"))
    if cache.get("key") != key:
        E8 = np.zeros((8, 1024), np.float32)
        for ei in range(8):
            E8[ei, ei * 128:(ei + 1) * 128] = 1.0
        cache.clear()
        cache.update(
            key=key,
            wgT=np.ascontiguousarray(np.asarray(inputs["L0_ffn_wg"]).T),
            bgrow=np.ascontiguousarray(np.asarray(inputs["L0_ffn_bg"]).reshape(1, 8)),
            W1T=np.ascontiguousarray(np.asarray(inputs["L0_ffn_Wgate"]).transpose(0, 2, 1)),
            W2T=np.ascontiguousarray(np.asarray(inputs["L0_ffn_Wup"]).transpose(0, 2, 1)),
            W3T=np.ascontiguousarray(np.asarray(inputs["L0_ffn_Wdown"]).transpose(0, 2, 1)),
            identf=np.eye(128, dtype=np.float32), E8=E8)
    return {
        "xin": xin,
        "wgT": cache["wgT"], "bgrow": cache["bgrow"],
        "W1T": cache["W1T"], "W2T": cache["W2T"], "W3T": cache["W3T"],
        "identf": cache["identf"], "E8": cache["E8"],
        "ones_col_bf": c["ones_col_bf"], "ones_row_f": c["ones_row_f"], "cst": c["cst"],
    }



NKBMAX = T // 128


def emit_C(nc, p, palloc, PS, dr):
    xin = dr["xin"]
    xout = dr["xout"]
    KTd = dr["KTd"]

    with _nullctx():
        sb = _SB()
        def alloc(name, shape, dt=F32):
            t = palloc(name, shape, dt)
            setattr(sb, name, t)
            return t
        alloc("xc", [128, ND, TC])
        alloc("xb", [128, ND, TC], BF16)
        alloc("sq", [128, ND, TC], BF16)
        alloc("ln_scr", [128, TC])
        for r in ("row_nm", "row_t1", "row_t2", "row_a", "row_b"):
            alloc(r, [1, TC])
        alloc("xhatb", [128, ND, TC], BF16)
        alloc("kt0", [128, TC], BF16); alloc("kt1", [128, TC], BF16)
        alloc("kbuf", [128, T], BF16)
        alloc("vtok", [128, NKBMAX, 16, 65], BF16)
        alloc("QT", [128, ND, TC], BF16)
        alloc("att", [128, ND, TC], BF16)
        alloc("qsb", [128, TC], BF16)
        alloc("t1s", [128, TC], BF16)
        alloc("t2s", [128, TC], BF16)
        alloc("sqk", [128, TC], BF16)
        alloc("mk2", [1, 16])
        alloc("q2", [1, TC]); alloc("mrow", [1, TC])
        alloc("mcol", [128, 16])
        alloc("linv", [1, TC]); alloc("lbc", [64, TC])
        alloc("Pt0", [128, 2 * TC], BF16); alloc("Pt1", [128, 2 * TC], BF16)
        alloc("wk0", [128, ND, 128], BF16); alloc("wk1", [128, ND, 128], BF16)
        alloc("wv0", [128, ND, TC], BF16)
        alloc("wo0", [128, ND, 128], BF16); alloc("wo1", [128, ND, 128], BF16)
        alloc("cos_bc", [128, T], BF16)
        alloc("sin_sgn", [128, T], BF16)
        alloc("Rswap", [128, 128], BF16)
        alloc("mask01", [128, 4 * TC], BF16)
        alloc("ones_col_bf", [128, 1], BF16)
        alloc("ones_row_f", [1, 128])
        alloc("cst", [128, 8])

        PSA = [PS[0], PS[1], PS[4], PS[5]]
        PS6 = PS[6]
        PS7 = PS[7]
        psl = {6: PS6, 7: PS7, 4: PS[4], 5: PS[5]}

        wtoks = []
        for nm_ in ("cos_bc", "sin_sgn", "Rswap", "mask01", "ones_col_bf", "ones_row_f", "cst"):
            wtoks.append(p.dma("sync", (lambda nm_=nm_: lambda e: e.dma_start(
                out=getattr(sb, nm_)[:], in_=dr[nm_]))(), sem="wld"))
        wtok = tmax(*wtoks)
        t_mk = p.op("vector", lambda e: e.memset(sb.mk2[:], 0.0), sem="V")
        t_vo = p.op("vector", lambda e: e.memset(sb.vtok[:, :, :, 64:65], 1.0), sem="V")
        io = {}

        def emit_rope(dst_view, src_ps, cos_v, sin_v, rot_ps, rot_key, scale, waits_in,
                      dst_war=()):
            tev = p.op("scalar", (lambda: lambda e: e.activation(
                sb.qsb[:], src_ps[:], AF.Copy, scale=scale))(),
                waits=tmax(waits_in, io.get("qsb_free")), sem="S")
            trm = p.op("tensor", (lambda: lambda e: e.matmul(
                rot_ps[:], sb.Rswap[:], sb.qsb[:], start=True, stop=True))(),
                waits=tmax(tev, io.get(rot_key)), sem="T")
            tp1 = p.op("gpsimd", (lambda: lambda e: e.tensor_tensor(
                sb.t1s[:], sb.qsb[:], cos_v, ALU.mult))(),
                waits=tmax(tev, io.get("t1s_free")), sem="G")
            tp2 = p.op("vector", (lambda: lambda e: e.tensor_tensor(
                sb.t2s[:], rot_ps[:], sin_v, ALU.mult))(),
                waits=tmax(trm, io.get("t2s_free")), sem="V")
            io[rot_key] = tp2
            tp3 = p.op("vector", (lambda dst_view=dst_view: lambda e: e.tensor_add(
                dst_view, sb.t1s[:], sb.t2s[:]))(), waits=tmax(tp1, tp2, dst_war), sem="V")
            io["qsb_free"] = tp1
            io["t1s_free"] = tp3
            io["t2s_free"] = tp3
            return tp3, tev

        wkslot = {0: (), 1: ()}
        wvslot = {0: (), 1: ()}
        ktslot = {0: (), 1: ()}
        xc_consumed = ()
        vtok_done = []
        for i in range(NTC):
            tsl = (i * TC, (i + 1) * TC)
            ldx = p.dma("sync", (lambda i=i: lambda e: e.dma_start(
                out=sb.xc[:], in_=xin[:, i * TC:(i + 1) * TC].rearrange("(c p) t -> p c t", p=128)))(),
                waits=xc_consumed, sem="xld")
            xc_tok = tmax(ldx, wtok)
            ln_last = emit_norm2(p, sb, io, psl, sb.xc[:], xc_tok, sb.xhatb[:])

            for rc in range(ND):
                sl = rc % 2
                wk = sb.wk0 if sl == 0 else sb.wk1
                kt = sb.kt0 if sl == 0 else sb.kt1
                twk = p.dma("gpsimd", (lambda rc=rc, wk=wk: lambda e: e.dma_start(
                    out=wk[:], in_=dr["qkvT"][:, D + rc * 128:D + (rc + 1) * 128]
                    .rearrange("(c p) o -> p c o", p=128)))(),
                    waits=wkslot[sl], sem=f"wkld{sl}")
                ps_k = PSA[sl]
                tkm = None
                for c in range(ND):
                    sp = (c == ND - 1)
                    tkm = p.op("tensor", (lambda c=c, sp=sp, wk=wk, ps_k=ps_k: lambda e: e.matmul(
                        ps_k[:], wk[:, c, :], sb.xhatb[:, c, :], start=(c == 0), stop=sp))(),
                        waits=tmax(twk, ln_last, io.get(f"pa{sl}")) if c == 0 else (),
                        sem="T" if sp else None)
                wkslot[sl] = (tkm,)
                rope_tok, tev = emit_rope(kt[:], ps_k,
                                          sb.cos_bc[:, tsl[0]:tsl[1]], sb.sin_sgn[:, tsl[0]:tsl[1]],
                                          PSA[2 + sl], f"ps{4 + sl}_free", 0.125,
                                          (tkm,), dst_war=ktslot[sl])
                io[f"pa{sl}"] = tev
                tkst = p.dma("sync", (lambda rc=rc, i=i, kt=kt: lambda e: e.dma_start(
                    out=KTd[rc * 128:(rc + 1) * 128, i * TC:(i + 1) * TC], in_=kt[:]))(),
                    waits=(rope_tok,), sem=f"ktst{sl}")
                ktd_last = tkst
                tsq = p.op("scalar", (lambda kt=kt: lambda e: e.activation(
                    sb.sqk[:], kt[:], AF.Square))(),
                    waits=tmax(rope_tok, io.get("sqk_free")), sem="S")
                ktslot[sl] = tmax(tkst, tsq)
                for hh in range(2):
                    base = hh * 64
                    psn = PS6 if hh == 0 else PS7
                    rowt = sb.row_t1 if hh == 0 else sb.row_t2
                    tnm = p.op("tensor", (lambda base=base, psn=psn: lambda e: e.matmul(
                        psn[0:1, :], sb.ones_col_bf[base:base + 64, :], sb.sqk[base:base + 64, :],
                        start=True, stop=True))(),
                        waits=tmax(tsq, io.get(f"ps{6 + hh}_free")), sem="T")
                    trm_ = p.op("vector", (lambda psn=psn, rowt=rowt: lambda e: e.reduce_max(
                        rowt[0:1, 0:1], psn[0:1, :], axis=AX.X))(), waits=(tnm,), sem="V")
                    io[f"ps{6 + hh}_free"] = trm_
                    h = 2 * rc + hh
                    tmk = p.op("vector", (lambda h=h, rowt=rowt: lambda e: e.tensor_tensor(
                        sb.mk2[0:1, h:h + 1], sb.mk2[0:1, h:h + 1], rowt[0:1, 0:1], ALU.max))(),
                        waits=(trm_, t_mk), sem="V")
                    io["sqk_free"] = tmk

            for vc in range(2):
                wv = sb.wv0
                twv = p.dma("gpsimd", (lambda vc=vc, wv=wv: lambda e: e.dma_start(
                    out=wv[:], in_=dr["qkvT"][:, 2 * D + vc * TC:2 * D + (vc + 1) * TC]
                    .rearrange("(c p) o -> p c o", p=128)))(),
                    waits=tmax(wvslot[0], wvslot[1]), sem="wvld")
                for ttt in range(TC // 128):
                    ps_v = PSA[ttt % 2]
                    tvm = None
                    for c in range(ND):
                        sp = (c == ND - 1)
                        tvm = p.op("tensor", (lambda c=c, ttt=ttt, sp=sp, wv=wv, ps_v=ps_v: lambda e: e.matmul(
                            ps_v[:], sb.xhatb[:, c, ttt * 128:(ttt + 1) * 128], wv[:, c, :],
                            start=(c == 0), stop=sp))(),
                            waits=tmax(twv, ln_last, io.get(f"pa{ttt % 2}")) if c == 0 else (),
                            sem="T" if sp else None)
                    wvslot[ttt % 2] = (tvm,)
                    kb = i * 4 + ttt
                    tev2 = None
                    for hh in range(8):
                        h = vc * 8 + hh
                        tev2 = p.op("scalar", (lambda kb=kb, h=h, hh=hh, ps_v=ps_v: lambda e: e.activation(
                            sb.vtok[:, kb, h, 0:64], ps_v[:, hh * 64:(hh + 1) * 64], AF.Copy))(),
                            waits=(tvm,) if hh == 0 else (), sem="S")
                    io[f"pa{ttt % 2}"] = tev2
                    vtok_done.append(tev2)
            xc_consumed = (ln_last,)
        vtok_tok = tmax(*vtok_done, t_vo)
        kt_tok = io.get("sqk_free")
        ktd_tok = tmax(("ktst0", p.counts.get("ktst0", 0)), ("ktst1", p.counts.get("ktst1", 0)))

        store_toks = []
        kbslot = {0: (), 1: ()}
        for qc in range(NTC):
            tsl = (qc * TC, (qc + 1) * TC)
            ldx = p.dma("sync", (lambda qc=qc: lambda e: e.dma_start(
                out=sb.xc[:], in_=xin[:, qc * TC:(qc + 1) * TC].rearrange("(c p) t -> p c t", p=128)))(),
                waits=xc_consumed, sem="xld")
            xc_tok = tmax(ldx, wtok)
            ln_last = emit_norm2(p, sb, io, psl, sb.xc[:], xc_tok, sb.xhatb[:])

            q_toks = []
            for rc in range(ND):
                sl = rc % 2
                wk = sb.wk0 if sl == 0 else sb.wk1
                twk = p.dma("gpsimd", (lambda rc=rc, wk=wk: lambda e: e.dma_start(
                    out=wk[:], in_=dr["qkvT"][:, rc * 128:(rc + 1) * 128]
                    .rearrange("(c p) o -> p c o", p=128)))(),
                    waits=wkslot[sl], sem=f"wkld{sl}")
                ps_k = PSA[sl]
                tkm = None
                for c in range(ND):
                    sp = (c == ND - 1)
                    tkm = p.op("tensor", (lambda c=c, sp=sp, wk=wk, ps_k=ps_k: lambda e: e.matmul(
                        ps_k[:], wk[:, c, :], sb.xhatb[:, c, :], start=(c == 0), stop=sp))(),
                        waits=tmax(twk, ln_last, io.get(f"pa{sl}")) if c == 0 else (),
                        sem="T" if sp else None)
                wkslot[sl] = (tkm,)
                rope_tok, tev = emit_rope(sb.QT[:, rc, :], ps_k,
                                          sb.cos_bc[:, tsl[0]:tsl[1]], sb.sin_sgn[:, tsl[0]:tsl[1]],
                                          PSA[2 + sl], f"ps{4 + sl}_free", 1.0, (tkm,))
                io[f"pa{sl}"] = tev
                q_toks.append(rope_tok)
            q_tok = tmax(*q_toks)

            att_toks = []
            nkb = (qc + 1) * 4
            mtoks = {}
            for h in range(16):
                rc, hh = h // 2, h % 2
                base = hh * 64
                tsq = p.op("scalar", (lambda rc=rc, base=base: lambda e: e.activation(
                    sb.sqk[0:64, :], sb.QT[base:base + 64, rc, :], AF.Square))(),
                    waits=tmax(q_tok, io.get("sqk_free")), sem="S")
                tq2 = p.op("tensor", (lambda: lambda e: e.matmul(
                    PS6[0:1, :], sb.ones_col_bf[0:64, :], sb.sqk[0:64, :], start=True, stop=True))(),
                    waits=tmax(tsq, io.get("ps6_free")), sem="T")
                io["sqk_free"] = tq2
                tm1 = p.op("vector", (lambda h=h: lambda e: e.tensor_single_scalar(
                    sb.q2[:], PS6[0:1, :], sb.mk2[0:1, h:h + 1], ALU.mult))(),
                    waits=tmax(tq2, kt_tok), sem="V")
                io["ps6_free"] = tm1
                tm2 = p.op("scalar", (lambda: lambda e: e.activation(
                    sb.mrow[:], sb.q2[:], AF.Ln))(), waits=(tm1,), sem="S")
                tm3 = p.op("scalar", (lambda: lambda e: e.activation(
                    sb.mrow[:], sb.mrow[:], AF.Exp, scale=0.5))(), waits=(tm2,), sem="S")
                tm4 = p.op("vector", (lambda: lambda e: e.reduce_max(
                    sb.row_t1[0:1, 0:1], sb.mrow[:], axis=AX.X, negate=True))(),
                    waits=tmax(tm3, io.get("rt1_free")), sem="V")
                tm5 = p.op("tensor", (lambda: lambda e: e.matmul(
                    PS7[:, 0:1], sb.ones_row_f[:], sb.row_t1[0:1, 0:1], start=True, stop=True))(),
                    waits=tmax(tm4, io.get("ps7_free")), sem="T")
                io["rt1_free"] = tm5
                tm6 = p.op("vector", (lambda h=h: lambda e: e.tensor_copy(
                    sb.mcol[:, h:h + 1], PS7[:, 0:1]))(),
                    waits=tmax(tm5, io.get("mcol_free")), sem="V")
                io["ps7_free"] = tm6
                mtoks[h] = tm6
            for h in range(16):
                rc, hh = h // 2, h % 2
                base = hh * 64
                ksl = h % 2
                tkl = p.dma("sync", (lambda h=h, nkb=nkb, ksl=ksl: lambda e: e.dma_start(
                    out=sb.kbuf[ksl * 64:(ksl + 1) * 64, 0:nkb * 128],
                    in_=KTd[h * 64:(h + 1) * 64, 0:nkb * 128]))(),
                    waits=tmax(kbslot[ksl], ktd_tok if qc == 0 and h < 2 else None),
                    sem=f"kbl{ksl}")

                ps_O = PSA[h % 2]
                tOm = None
                kread = []

                def emit_O(pend_):
                    kb_, u_, pt_, ptk_, tmask_ = pend_
                    t_ = p.op("tensor", (lambda kb_=kb_, u_=u_, h=h, pt_=pt_, ps_O=ps_O, nkb=nkb: lambda e: e.matmul(
                        ps_O[0:65, :], sb.vtok[:, kb_, h, :], pt_[:, u_ * TC:(u_ + 1) * TC],
                        start=(kb_ == 0), stop=(kb_ == nkb - 1)))(),
                        waits=tmax(tmask_, vtok_tok,
                                   (io.get(f"pa{h % 2}") if kb_ == 0 else None)),
                        sem="T")
                    io[f"pth{ptk_}{u_}"] = t_
                    return t_

                pend_o = None
                for kb in range(nkb):
                    u = kb % 2
                    ps_s = PS[2 + u]
                    ptk = (kb // 2) % 2
                    pt = sb.Pt0 if ptk == 0 else sb.Pt1
                    tsm = p.op("tensor", (lambda kb=kb, rc=rc, base=base, ksl=ksl, ps_s=ps_s, u=u: lambda e: e.matmul(
                        ps_s[:], sb.kbuf[ksl * 64:ksl * 64 + 64, kb * 128:(kb + 1) * 128],
                        sb.QT[base:base + 64, rc, :], start=True, stop=True))(),
                        waits=tmax(io.get(f"psS{u}_free"), q_tok, tkl), sem="T")
                    kread.append(tsm)
                    texp = p.op("scalar", (lambda pt=pt, u=u, ps_s=ps_s, h=h: lambda e: e.activation(
                        pt[:, u * TC:(u + 1) * TC], ps_s[:], AF.Exp, bias=sb.mcol[:, h:h + 1]))(),
                        waits=tmax(tsm, mtoks[h], io.get(f"pth{ptk}{u}")), sem="S")
                    io[f"psS{u}_free"] = texp
                    tmask = texp
                    off = kb * 128 - qc * TC
                    if 0 <= off < TC:
                        oi = off // 128
                        tmask = p.op("gpsimd", (lambda pt=pt, u=u, oi=oi: lambda e: e.tensor_tensor(
                            pt[:, u * TC:(u + 1) * TC], pt[:, u * TC:(u + 1) * TC],
                            sb.mask01[:, oi * TC:(oi + 1) * TC], ALU.mult))(),
                            waits=(texp,), sem="G")
                    if pend_o is not None:
                        emit_O(pend_o)
                    pend_o = (kb, u, pt, ptk, tmask)
                tOm = emit_O(pend_o)
                kbslot[ksl] = tmax(*kread)
                tli = p.op("vector", (lambda ps_O=ps_O: lambda e: e.reciprocal(
                    sb.linv[:], ps_O[64:65, :]))(), waits=tmax(tOm, io.get("linv_free")), sem="V")
                tlb = p.op("tensor", (lambda: lambda e: e.matmul(
                    PSA[2 + h % 2][0:64, :], sb.ones_row_f[0:1, 0:64], sb.linv[:],
                    start=True, stop=True))(),
                    waits=tmax(tli, io.get(f"ps{4 + h % 2}_free")), sem="T")
                tle = p.op("scalar", (lambda: lambda e: e.activation(
                    sb.lbc[:], PSA[2 + h % 2][0:64, :], AF.Copy))(),
                    waits=tmax(tlb, io.get("lbc_free")), sem="S")
                io[f"ps{4 + h % 2}_free"] = tle
                tat = p.op("vector", (lambda h=h, rc=rc, base=base, ps_O=ps_O: lambda e: e.tensor_mul(
                    sb.att[base:base + 64, rc, :], ps_O[0:64, :], sb.lbc[:]))(),
                    waits=(tle,), sem="V")
                io[f"pa{h % 2}"] = tat
                io["lbc_free"] = tat
                io["linv_free"] = tlb
                att_toks.append(tat)
            att_tok = tmax(*att_toks)
            io["mcol_free"] = att_tok

            res_toks = []
            for o in range(ND):
                wsl = o % 2
                wob = sb.wo0 if wsl == 0 else sb.wo1
                two = p.dma("gpsimd", (lambda o=o, wob=wob: lambda e: e.dma_start(
                    out=wob[:], in_=dr["woT"][:, o * 128:(o + 1) * 128]
                    .rearrange("(c p) o2 -> p c o2", p=128)))(),
                    waits=io.get(f"wo{wsl}_rd", ()), sem=f"wold{wsl}")
                ps_o = PS6 if o % 2 == 0 else PS7
                tpo = None
                for c in range(ND):
                    sp = (c == ND - 1)
                    tpo = p.op("tensor", (lambda o=o, c=c, sp=sp, ps_o=ps_o, wob=wob: lambda e: e.matmul(
                        ps_o[:], wob[:, c, :], sb.att[:, c, :],
                        start=(c == 0), stop=sp))(),
                        waits=tmax(att_tok, two, io.get(f"ps{6 + o % 2}_free")) if c == 0 else (),
                        sem="T" if sp else None)
                io[f"wo{wsl}_rd"] = (tpo,)
                tr = p.op("vector", (lambda o=o, ps_o=ps_o: lambda e: e.tensor_add(
                    sb.xc[:, o, :], sb.xc[:, o, :], ps_o[:]))(), waits=(tpo,), sem="V")
                io[f"ps{6 + o % 2}_free"] = tr
                res_toks.append(tr)
            tst_d = p.dma("sync", (lambda qc=qc: lambda e: e.dma_start(
                out=xout[:, qc * TC:(qc + 1) * TC].rearrange("(c p) t -> p c t", p=128),
                in_=sb.xc[:]))(), waits=tuple(res_toks), sem="xst")
            store_toks.append(tst_d)
            xc_consumed = tmax(tst_d, *res_toks)

        p.op("sync", lambda e: None, waits=tmax(*store_toks))
    return None



_C_DINS = [("qkvT", [D, 3 * D], F32), ("woT", [D, D], F32), ("cos_bc", [128, T], BF16),
           ("sin_sgn", [128, T], BF16), ("Rswap", [128, 128], BF16),
           ("mask01", [128, 4 * TC], BF16), ("ones_col_bf", [128, 1], BF16),
           ("ones_row_f", [1, 128], F32), ("cst", [128, 8], F32)]


def build_launchC():
    return _standalone(emit_C, _C_DINS, extra=[("KTd", [D, T], BF16, "Internal")])

def _mask01():
    m = np.zeros((128, 4 * TC), np.float32)
    for oi in range(4):
        off = oi * 128
        for pp in range(128):
            for f in range(TC):
                if off + pp <= f:
                    m[pp, oi * TC + f] = 1.0
    return m.astype(BF)

_MASK01 = None


def _prep_C(inputs, b, xin):
    global _MASK01
    c = _CONSTS
    if _MASK01 is None:
        _MASK01 = _mask01()
    return {
        "xin": xin,
        "qkvT": np.ascontiguousarray(np.asarray(inputs["L1_attn_qkv"]).T),
        "woT": np.ascontiguousarray(np.asarray(inputs["L1_attn_wo"]).T),
        "cos_bc": c["cos_bc"], "sin_sgn": c["sin_sgn"], "Rswap": c["Rswap"],
        "mask01": _MASK01,
        "ones_col_bf": c["ones_col_bf"], "ones_row_f": c["ones_row_f"], "cst": c["cst"],
    }




def emit_D1(nc, p, palloc, PS, dr):
    xin = dr["xin"]
    lnxd = dr["lnxd"]
    hd = dr["hd"]
    KTd = dr["KTd"]

    with _nullctx():
        sb = _SB()
        def alloc(name, shape, dt=F32):
            t = palloc(name, shape, dt)
            setattr(sb, name, t)
            return t
        alloc("xc", [128, ND, TC])
        alloc("lnxb", [128, ND, TC], BF16)
        alloc("xb", [128, ND, TC], BF16)
        alloc("sq", [128, ND, TC], BF16)
        alloc("ln_scr", [128, TC])
        for r in ("row_nm", "row_t1", "row_t2", "row_a", "row_b"):
            alloc(r, [1, TC])
        alloc("xhatb", [128, ND, TC], BF16)
        alloc("kt0", [128, TC], BF16); alloc("kt1", [128, TC], BF16)
        alloc("kbuf", [128, T], BF16)
        alloc("vtok", [128, NKBMAX, 16, 65], BF16)
        alloc("QT", [128, ND, TC], BF16)
        alloc("att", [128, ND, TC], BF16)
        alloc("qsb", [128, TC], BF16)
        alloc("t1s", [128, TC], BF16)
        alloc("t2s", [128, TC], BF16)
        alloc("sqk", [128, TC], BF16)
        alloc("mk2", [1, 16])
        alloc("q2", [1, TC]); alloc("mrow", [1, TC])
        alloc("mcol", [128, 16])
        alloc("linv", [1, TC]); alloc("lbc", [64, TC])
        alloc("Pt0", [128, 2 * TC], BF16)
        alloc("wk0", [128, ND, 128], BF16); alloc("wk1", [128, ND, 128], BF16)
        alloc("wv0", [128, ND, TC], BF16)
        alloc("wo0", [128, ND, 128], BF16); alloc("wo1", [128, ND, 128], BF16)
        alloc("sc1", [128, ND])
        alloc("cos_bc", [128, T], BF16)
        alloc("sin_sgn", [128, T], BF16)
        alloc("Rswap", [128, 128], BF16)
        alloc("mask01", [128, 4 * TC], BF16)
        alloc("ones_col_bf", [128, 1], BF16)
        alloc("ones_row_f", [1, 128])
        alloc("cst", [128, 8])

        PSA = [PS[0], PS[1], PS[4], PS[5]]
        PS6 = PS[6]
        PS7 = PS[7]
        psl = {6: PS6, 7: PS7, 4: PS[4], 5: PS[5]}

        wtoks = []
        for nm_ in ("cos_bc", "sin_sgn", "Rswap", "mask01", "ones_col_bf", "ones_row_f",
                    "cst", "sc1"):
            wtoks.append(p.dma("sync", (lambda nm_=nm_: lambda e: e.dma_start(
                out=getattr(sb, nm_)[:], in_=dr[nm_]))(), sem="wld"))
        wtok = tmax(*wtoks)
        t_mk = p.op("vector", lambda e: e.memset(sb.mk2[:], 0.0), sem="V")
        t_vo = p.op("vector", lambda e: e.memset(sb.vtok[:, :, :, 64:65], 1.0), sem="V")
        io = {}

        def emit_rope(dst_view, src_ps, cos_v, sin_v, rot_ps, rot_key, scale, waits_in,
                      dst_war=()):
            tev = p.op("scalar", (lambda: lambda e: e.activation(
                sb.qsb[:], src_ps[:], AF.Copy, scale=scale))(),
                waits=tmax(waits_in, io.get("qsb_free")), sem="S")
            trm = p.op("tensor", (lambda: lambda e: e.matmul(
                rot_ps[:], sb.Rswap[:], sb.qsb[:], start=True, stop=True))(),
                waits=tmax(tev, io.get(rot_key)), sem="T")
            tp1 = p.op("gpsimd", (lambda: lambda e: e.tensor_tensor(
                sb.t1s[:], sb.qsb[:], cos_v, ALU.mult))(),
                waits=tmax(tev, io.get("t1s_free")), sem="G")
            tp2 = p.op("vector", (lambda: lambda e: e.tensor_tensor(
                sb.t2s[:], rot_ps[:], sin_v, ALU.mult))(),
                waits=tmax(trm, io.get("t2s_free")), sem="V")
            io[rot_key] = tp2
            tp3 = p.op("vector", (lambda dst_view=dst_view: lambda e: e.tensor_add(
                dst_view, sb.t1s[:], sb.t2s[:]))(), waits=tmax(tp1, tp2, dst_war), sem="V")
            io["qsb_free"] = tp1
            io["t1s_free"] = tp3
            io["t2s_free"] = tp3
            return tp3, tev

        wkslot = {0: (), 1: ()}
        wvslot = {0: (), 1: ()}
        ktslot = {0: (), 1: ()}
        xc_consumed = ()
        vtok_done = []
        ktd_last = None
        for i in range(NTC):
            tsl = (i * TC, (i + 1) * TC)
            ldx = p.dma("sync", (lambda i=i: lambda e: e.dma_start(
                out=sb.xc[:], in_=xin[:, i * TC:(i + 1) * TC].rearrange("(c p) t -> p c t", p=128)))(),
                waits=xc_consumed, sem="xld")
            xc_tok = tmax(ldx, wtok)
            ln_last = emit_norm2(p, sb, io, psl, sb.xc[:], xc_tok, sb.lnxb[:])
            tlnst = p.dma("sync", (lambda i=i: lambda e: e.dma_start(
                out=lnxd[:, i * TC:(i + 1) * TC].rearrange("(c p) t -> p c t", p=128),
                in_=sb.lnxb[:]))(), waits=(ln_last,), sem="lnst")
            rms_last = emit_norm2(p, sb, io, psl, sb.lnxb[:], (ln_last,), sb.xhatb[:],
                                  eps_col=4, rms=True, scale_col=sb.sc1)
            ln_last = rms_last

            for rc in range(ND):
                sl = rc % 2
                wk = sb.wk0 if sl == 0 else sb.wk1
                kt = sb.kt0 if sl == 0 else sb.kt1
                twk = p.dma("gpsimd", (lambda rc=rc, wk=wk: lambda e: e.dma_start(
                    out=wk[:], in_=dr["wk"][:, rc * 128:(rc + 1) * 128]
                    .rearrange("(c p) o -> p c o", p=128)))(),
                    waits=wkslot[sl], sem=f"wkld{sl}")
                ps_k = PSA[sl]
                tkm = None
                for c in range(ND):
                    sp = (c == ND - 1)
                    tkm = p.op("tensor", (lambda c=c, sp=sp, wk=wk, ps_k=ps_k: lambda e: e.matmul(
                        ps_k[:], wk[:, c, :], sb.xhatb[:, c, :], start=(c == 0), stop=sp))(),
                        waits=tmax(twk, ln_last, io.get(f"pa{sl}")) if c == 0 else (),
                        sem="T" if sp else None)
                wkslot[sl] = (tkm,)
                rope_tok, tev = emit_rope(kt[:], ps_k,
                                          sb.cos_bc[:, tsl[0]:tsl[1]], sb.sin_sgn[:, tsl[0]:tsl[1]],
                                          PSA[2 + sl], f"ps{4 + sl}_free", 0.125,
                                          (tkm,), dst_war=ktslot[sl])
                io[f"pa{sl}"] = tev
                tkst = p.dma("sync", (lambda rc=rc, i=i, kt=kt: lambda e: e.dma_start(
                    out=KTd[rc * 128:(rc + 1) * 128, i * TC:(i + 1) * TC], in_=kt[:]))(),
                    waits=(rope_tok,), sem=f"ktst{sl}")
                tsq = p.op("scalar", (lambda kt=kt: lambda e: e.activation(
                    sb.sqk[:], kt[:], AF.Square))(),
                    waits=tmax(rope_tok, io.get("sqk_free")), sem="S")
                ktslot[sl] = tmax(tkst, tsq)
                for hh in range(2):
                    base = hh * 64
                    psn = PS6 if hh == 0 else PS7
                    rowt = sb.row_t1 if hh == 0 else sb.row_t2
                    tnm = p.op("tensor", (lambda base=base, psn=psn: lambda e: e.matmul(
                        psn[0:1, :], sb.ones_col_bf[base:base + 64, :], sb.sqk[base:base + 64, :],
                        start=True, stop=True))(),
                        waits=tmax(tsq, io.get(f"ps{6 + hh}_free")), sem="T")
                    trm_ = p.op("vector", (lambda psn=psn, rowt=rowt: lambda e: e.reduce_max(
                        rowt[0:1, 0:1], psn[0:1, :], axis=AX.X))(), waits=(tnm,), sem="V")
                    io[f"ps{6 + hh}_free"] = trm_
                    h = 2 * rc + hh
                    tmk = p.op("vector", (lambda h=h, rowt=rowt: lambda e: e.tensor_tensor(
                        sb.mk2[0:1, h:h + 1], sb.mk2[0:1, h:h + 1], rowt[0:1, 0:1], ALU.max))(),
                        waits=(trm_, t_mk), sem="V")
                    io["sqk_free"] = tmk

            for vc in range(2):
                wv = sb.wv0
                twv = p.dma("gpsimd", (lambda vc=vc, wv=wv: lambda e: e.dma_start(
                    out=wv[:], in_=dr["wv"][:, vc * TC:(vc + 1) * TC]
                    .rearrange("(c p) o -> p c o", p=128)))(),
                    waits=tmax(wvslot[0], wvslot[1]), sem="wvld")
                for ttt in range(TC // 128):
                    ps_v = PSA[ttt % 2]
                    tvm = None
                    for c in range(ND):
                        sp = (c == ND - 1)
                        tvm = p.op("tensor", (lambda c=c, ttt=ttt, sp=sp, wv=wv, ps_v=ps_v: lambda e: e.matmul(
                            ps_v[:], sb.xhatb[:, c, ttt * 128:(ttt + 1) * 128], wv[:, c, :],
                            start=(c == 0), stop=sp))(),
                            waits=tmax(twv, ln_last, io.get(f"pa{ttt % 2}")) if c == 0 else (),
                            sem="T" if sp else None)
                    wvslot[ttt % 2] = (tvm,)
                    kb = i * 4 + ttt
                    tev2 = None
                    for hh in range(8):
                        h = vc * 8 + hh
                        tev2 = p.op("scalar", (lambda kb=kb, h=h, hh=hh, ps_v=ps_v: lambda e: e.activation(
                            sb.vtok[:, kb, h, 0:64], ps_v[:, hh * 64:(hh + 1) * 64], AF.Copy))(),
                            waits=(tvm,) if hh == 0 else (), sem="S")
                    io[f"pa{ttt % 2}"] = tev2
                    vtok_done.append(tev2)
            xc_consumed = tmax(ln_last, tlnst)
        vtok_tok = tmax(*vtok_done, t_vo)
        kt_tok = io.get("sqk_free")
        ktd_tok = tmax(("ktst0", p.counts.get("ktst0", 0)), ("ktst1", p.counts.get("ktst1", 0)))
        lnx_tok = ("lnst", p.counts.get("lnst", 0))

        store_toks = []
        kbslot = {0: (), 1: ()}
        lnx_consumed = ()
        for qc in range(NTC):
            tsl = (qc * TC, (qc + 1) * TC)
            ldl = p.dma("sync", (lambda qc=qc: lambda e: e.dma_start(
                out=sb.lnxb[:], in_=lnxd[:, qc * TC:(qc + 1) * TC]
                .rearrange("(c p) t -> p c t", p=128)))(),
                waits=tmax(lnx_consumed, lnx_tok, xc_consumed), sem="lnld")
            lnx_tok2 = tmax(ldl, wtok)
            ln_last = emit_norm2(p, sb, io, psl, sb.lnxb[:], lnx_tok2, sb.xhatb[:],
                                 eps_col=4, rms=True, scale_col=sb.sc1)

            q_toks = []
            for rc in range(ND):
                sl = rc % 2
                wk = sb.wk0 if sl == 0 else sb.wk1
                twk = p.dma("gpsimd", (lambda rc=rc, wk=wk: lambda e: e.dma_start(
                    out=wk[:], in_=dr["wq"][:, rc * 128:(rc + 1) * 128]
                    .rearrange("(c p) o -> p c o", p=128)))(),
                    waits=wkslot[sl], sem=f"wkld{sl}")
                ps_k = PSA[sl]
                tkm = None
                for c in range(ND):
                    sp = (c == ND - 1)
                    tkm = p.op("tensor", (lambda c=c, sp=sp, wk=wk, ps_k=ps_k: lambda e: e.matmul(
                        ps_k[:], wk[:, c, :], sb.xhatb[:, c, :], start=(c == 0), stop=sp))(),
                        waits=tmax(twk, ln_last, io.get(f"pa{sl}")) if c == 0 else (),
                        sem="T" if sp else None)
                wkslot[sl] = (tkm,)
                rope_tok, tev = emit_rope(sb.QT[:, rc, :], ps_k,
                                          sb.cos_bc[:, tsl[0]:tsl[1]], sb.sin_sgn[:, tsl[0]:tsl[1]],
                                          PSA[2 + sl], f"ps{4 + sl}_free", 1.0, (tkm,))
                io[f"pa{sl}"] = tev
                q_toks.append(rope_tok)
            q_tok = tmax(*q_toks)

            att_toks = []
            nkb = (qc + 1) * 4
            mtoks = {}
            for h in range(16):
                rc, hh = h // 2, h % 2
                base = hh * 64
                tsq = p.op("scalar", (lambda rc=rc, base=base: lambda e: e.activation(
                    sb.sqk[0:64, :], sb.QT[base:base + 64, rc, :], AF.Square))(),
                    waits=tmax(q_tok, io.get("sqk_free")), sem="S")
                tq2 = p.op("tensor", (lambda: lambda e: e.matmul(
                    PS6[0:1, :], sb.ones_col_bf[0:64, :], sb.sqk[0:64, :], start=True, stop=True))(),
                    waits=tmax(tsq, io.get("ps6_free")), sem="T")
                io["sqk_free"] = tq2
                tm1 = p.op("vector", (lambda h=h: lambda e: e.tensor_single_scalar(
                    sb.q2[:], PS6[0:1, :], sb.mk2[0:1, h:h + 1], ALU.mult))(),
                    waits=tmax(tq2, kt_tok), sem="V")
                io["ps6_free"] = tm1
                tm2 = p.op("scalar", (lambda: lambda e: e.activation(
                    sb.mrow[:], sb.q2[:], AF.Ln))(), waits=(tm1,), sem="S")
                tm3 = p.op("scalar", (lambda: lambda e: e.activation(
                    sb.mrow[:], sb.mrow[:], AF.Exp, scale=0.5))(), waits=(tm2,), sem="S")
                tm4 = p.op("vector", (lambda: lambda e: e.reduce_max(
                    sb.row_t1[0:1, 0:1], sb.mrow[:], axis=AX.X, negate=True))(),
                    waits=tmax(tm3, io.get("rt1_free")), sem="V")
                tm5 = p.op("tensor", (lambda: lambda e: e.matmul(
                    PS7[:, 0:1], sb.ones_row_f[:], sb.row_t1[0:1, 0:1], start=True, stop=True))(),
                    waits=tmax(tm4, io.get("ps7_free")), sem="T")
                io["rt1_free"] = tm5
                tm6 = p.op("vector", (lambda h=h: lambda e: e.tensor_copy(
                    sb.mcol[:, h:h + 1], PS7[:, 0:1]))(),
                    waits=tmax(tm5, io.get("mcol_free")), sem="V")
                io["ps7_free"] = tm6
                mtoks[h] = tm6
            for h in range(16):
                rc, hh = h // 2, h % 2
                base = hh * 64
                ksl = h % 2
                tkl = p.dma("sync", (lambda h=h, nkb=nkb, ksl=ksl: lambda e: e.dma_start(
                    out=sb.kbuf[ksl * 64:(ksl + 1) * 64, 0:nkb * 128],
                    in_=KTd[h * 64:(h + 1) * 64, 0:nkb * 128]))(),
                    waits=tmax(kbslot[ksl], ktd_tok if qc == 0 and h < 2 else None),
                    sem=f"kbl{ksl}")

                ps_O = PSA[h % 2]
                tOm = None
                kread = []

                def emit_O(pend_):
                    kb_, u_, pt_, ptk_, tmask_ = pend_
                    t_ = p.op("tensor", (lambda kb_=kb_, u_=u_, h=h, pt_=pt_, ps_O=ps_O, nkb=nkb: lambda e: e.matmul(
                        ps_O[0:65, :], sb.vtok[:, kb_, h, :], pt_[:, u_ * TC:(u_ + 1) * TC],
                        start=(kb_ == 0), stop=(kb_ == nkb - 1)))(),
                        waits=tmax(tmask_, vtok_tok,
                                   (io.get(f"pa{h % 2}") if kb_ == 0 else None)),
                        sem="T")
                    io[f"pth{ptk_}{u_}"] = t_
                    return t_

                pend_o = None
                for kb in range(nkb):
                    u = kb % 2
                    ps_s = PS[2 + u]
                    ptk = 0
                    pt = sb.Pt0
                    tsm = p.op("tensor", (lambda kb=kb, rc=rc, base=base, ksl=ksl, ps_s=ps_s, u=u: lambda e: e.matmul(
                        ps_s[:], sb.kbuf[ksl * 64:ksl * 64 + 64, kb * 128:(kb + 1) * 128],
                        sb.QT[base:base + 64, rc, :], start=True, stop=True))(),
                        waits=tmax(io.get(f"psS{u}_free"), q_tok, tkl), sem="T")
                    kread.append(tsm)
                    texp = p.op("scalar", (lambda pt=pt, u=u, ps_s=ps_s, h=h: lambda e: e.activation(
                        pt[:, u * TC:(u + 1) * TC], ps_s[:], AF.Exp, bias=sb.mcol[:, h:h + 1]))(),
                        waits=tmax(tsm, mtoks[h], io.get(f"pth{ptk}{u}")), sem="S")
                    io[f"psS{u}_free"] = texp
                    tmask = texp
                    off = kb * 128 - qc * TC
                    if 0 <= off < TC:
                        oi = off // 128
                        tmask = p.op("gpsimd", (lambda pt=pt, u=u, oi=oi: lambda e: e.tensor_tensor(
                            pt[:, u * TC:(u + 1) * TC], pt[:, u * TC:(u + 1) * TC],
                            sb.mask01[:, oi * TC:(oi + 1) * TC], ALU.mult))(),
                            waits=(texp,), sem="G")
                    if pend_o is not None:
                        emit_O(pend_o)
                    pend_o = (kb, u, pt, ptk, tmask)
                tOm = emit_O(pend_o)
                kbslot[ksl] = tmax(*kread)
                tli = p.op("vector", (lambda ps_O=ps_O: lambda e: e.reciprocal(
                    sb.linv[:], ps_O[64:65, :]))(), waits=tmax(tOm, io.get("linv_free")), sem="V")
                tlb = p.op("tensor", (lambda: lambda e: e.matmul(
                    PSA[2 + h % 2][0:64, :], sb.ones_row_f[0:1, 0:64], sb.linv[:],
                    start=True, stop=True))(),
                    waits=tmax(tli, io.get(f"ps{4 + h % 2}_free")), sem="T")
                tle = p.op("scalar", (lambda: lambda e: e.activation(
                    sb.lbc[:], PSA[2 + h % 2][0:64, :], AF.Copy))(),
                    waits=tmax(tlb, io.get("lbc_free")), sem="S")
                io[f"ps{4 + h % 2}_free"] = tle
                tat = p.op("vector", (lambda h=h, rc=rc, base=base, ps_O=ps_O: lambda e: e.tensor_mul(
                    sb.att[base:base + 64, rc, :], ps_O[0:64, :], sb.lbc[:]))(),
                    waits=(tle,), sem="V")
                io[f"pa{h % 2}"] = tat
                io["lbc_free"] = tat
                io["linv_free"] = tlb
                att_toks.append(tat)
            att_tok = tmax(*att_toks)
            io["mcol_free"] = att_tok

            res_toks = []
            for o in range(ND):
                wsl = o % 2
                wob = sb.wo0 if wsl == 0 else sb.wo1
                two = p.dma("gpsimd", (lambda o=o, wob=wob: lambda e: e.dma_start(
                    out=wob[:], in_=dr["wo"][:, o * 128:(o + 1) * 128]
                    .rearrange("(c p) o2 -> p c o2", p=128)))(),
                    waits=io.get(f"wo{wsl}_rd", ()), sem=f"wold{wsl}")
                ps_o = PS6 if o % 2 == 0 else PS7
                tpo = None
                for c in range(ND):
                    sp = (c == ND - 1)
                    tpo = p.op("tensor", (lambda o=o, c=c, sp=sp, ps_o=ps_o, wob=wob: lambda e: e.matmul(
                        ps_o[:], wob[:, c, :], sb.att[:, c, :],
                        start=(c == 0), stop=sp))(),
                        waits=tmax(att_tok, two, io.get(f"ps{6 + o % 2}_free")) if c == 0 else (),
                        sem="T" if sp else None)
                io[f"wo{wsl}_rd"] = (tpo,)
                tr = p.op("vector", (lambda o=o, ps_o=ps_o: lambda e: e.tensor_add(
                    sb.lnxb[:, o, :], sb.lnxb[:, o, :], ps_o[:]))(), waits=(tpo,), sem="V")
                io[f"ps{6 + o % 2}_free"] = tr
                res_toks.append(tr)
            tst_d = p.dma("sync", (lambda qc=qc: lambda e: e.dma_start(
                out=hd[:, qc * TC:(qc + 1) * TC].rearrange("(c p) t -> p c t", p=128),
                in_=sb.lnxb[:]))(), waits=tuple(res_toks), sem="xst")
            store_toks.append(tst_d)
            lnx_consumed = tmax(tst_d, *res_toks)
            xc_consumed = ()

        p.op("sync", lambda e: None, waits=tmax(*store_toks))
    return None



_D1_DINS = [("wq", [D, D], F32), ("wk", [D, D], F32), ("wv", [D, D], F32),
            ("wo", [D, D], F32), ("sc1", [128, ND], F32), ("cos_bc", [128, T], BF16),
            ("sin_sgn", [128, T], BF16), ("Rswap", [128, 128], BF16),
            ("mask01", [128, 4 * TC], BF16), ("ones_col_bf", [128, 1], BF16),
            ("ones_row_f", [1, 128], F32), ("cst", [128, 8], F32)]


def build_launchD1():
    nc = bass.Bass("TRN2", target_bir_lowering=False)
    dr = {"xin": nc.dram_tensor("xin", [D, T], F32, kind="ExternalInput").ap()}
    for (nm, shp, dt) in _D1_DINS:
        dr[nm] = nc.dram_tensor(nm, shp, dt, kind="ExternalInput").ap()
    dr["lnxd"] = nc.dram_tensor("lnxd", [D, T], BF16, kind="ExternalOutput").ap()
    dr["hd"] = nc.dram_tensor("hd", [D, T], BF16, kind="ExternalOutput").ap()
    dr["KTd"] = nc.dram_tensor("KTd", [D, T], BF16, kind="Internal").ap()
    with ExitStack() as ctx:
        p = Prog(nc)
        PS = [ctx.enter_context(nc.psum_tensor(f"PS{k}", [128, TC], F32)) for k in range(8)]
        allocd = {}
        def palloc(name, shape, dt=F32):
            if name not in allocd:
                allocd[name] = ctx.enter_context(nc.sbuf_tensor("s_" + name, shape, dt))
            return allocd[name]
        emit_D1(nc, p, palloc, PS, dr)
        p.build(ctx)
    return nc

def _prep_D1(inputs, b, xin):
    global _MASK01
    c = _CONSTS
    if _MASK01 is None:
        _MASK01 = _mask01()
    return {
        "xin": xin,
        "wq": np.ascontiguousarray(np.asarray(inputs["L1_ffn_wq"])),
        "wk": np.ascontiguousarray(np.asarray(inputs["L1_ffn_wk"])),
        "wv": np.ascontiguousarray(np.asarray(inputs["L1_ffn_wv"])),
        "wo": np.ascontiguousarray(np.asarray(inputs["L1_ffn_wo"])),
        "sc1": np.ascontiguousarray(np.asarray(inputs["L1_ffn_scale1"]).reshape(ND, 128).T),
        "cos_bc": c["cos_bc"], "sin_sgn": c["sin_sgn"], "Rswap": c["Rswap"],
        "mask01": _MASK01,
        "ones_col_bf": c["ones_col_bf"], "ones_row_f": c["ones_row_f"], "cst": c["cst"],
    }



D2SLAB = 1024
ND2SLAB = T // D2SLAB


def emit_D2(nc, p, palloc, PS, dr):
    xin = dr["xin"]
    hdin = dr["hdin"]
    xout = dr["xout"]

    with _nullctx():
        sb = _SB()
        def alloc(name, shape, dt=F32):
            t = palloc(name, shape, dt)
            setattr(sb, name, t)
            return t
        alloc("hsl", [128, ND, D2SLAB], BF16)
        alloc("xhatb", [128, ND, D2SLAB], BF16)
        alloc("out_acc", [128, ND, D2SLAB])
        alloc("x3s", [128, D2SLAB])
        alloc("xb", [128, ND, TC], BF16)
        alloc("sq", [128, ND, TC], BF16)
        alloc("ln_scr", [128, TC])
        for r in ("row_nm", "row_t1", "row_t2", "row_a", "row_b"):
            alloc(r, [1, TC])
        alloc("rw_bf", [128, ND, 8], BF16)
        alloc("noise_s", [128, 8, 8])
        alloc("rsc", [128, 48])
        alloc("w_T8", [8, D2SLAB])
        alloc("wbc", [128, D2SLAB])
        alloc("W1b", [128, ND, D], BF16)
        alloc("W2b", [128, ND, D], BF16)
        alloc("WOb", [128, ND, D], BF16)
        alloc("sg0", [128, TC], BF16); alloc("sg1", [128, TC], BF16)
        alloc("u2w0", [128, TC]); alloc("u2w1", [128, TC])
        alloc("h0", [128, ND, TC], BF16); alloc("h1", [128, ND, TC], BF16)
        alloc("tmpd", [128, TC])
        alloc("b1c", [128, ND, 8])
        alloc("b2c", [128, ND, 8])
        alloc("boutr", [8, D])
        alloc("sc2", [128, ND])
        alloc("identf", [128, 128])
        alloc("E8", [8, 1024])
        alloc("ones_col_bf", [128, 1], BF16)
        alloc("ones_row_f", [1, 128])
        alloc("cst", [128, 8])

        psl = {6: PS[6], 7: PS[7], 4: PS[4], 5: PS[5]}

        wtoks = []
        for nm_ in ("b1c", "b2c", "boutr", "sc2", "identf", "E8", "ones_col_bf",
                    "ones_row_f", "cst"):
            wtoks.append(p.dma("sync", (lambda nm_=nm_: lambda e: e.dma_start(
                out=getattr(sb, nm_)[:], in_=dr[nm_]))(), sem="wld"))
        wtoks.append(p.dma("gpsimd", lambda e: e.dma_start(
            out=sb.rw_bf[:], in_=dr["rwT"].rearrange("(c p) e -> p c e", p=128)), sem="wldg"))
        wtok = tmax(*wtoks)
        io = {}

        store_toks = []
        h_consumed = ()
        acc_consumed = ()
        w_consumed = ()
        for s in range(ND2SLAB):
            ldh = p.dma("sync", (lambda s=s: lambda e: e.dma_start(
                out=sb.hsl[:], in_=hdin[:, s * D2SLAB:(s + 1) * D2SLAB]
                .rearrange("(c p) t -> p c t", p=128)))(),
                waits=h_consumed, sem="hld")
            ldn = p.dma("sync", (lambda s=s: lambda e: e.dma_start(
                out=sb.noise_s[:], in_=dr["noise"][s * D2SLAB:(s + 1) * D2SLAB, :]
                .rearrange("(tt p) e -> p tt e", p=128)))(),
                waits=io.get("noise_free", ()), sem="nld")
            h_tok = tmax(ldh, wtok)

            ln_last = None
            for q in range(D2SLAB // TC):
                ln_last = emit_norm2(p, sb, io, psl,
                                     sb.hsl[:, :, q * TC:(q + 1) * TC], h_tok,
                                     sb.xhatb[:, :, q * TC:(q + 1) * TC],
                                     eps_col=4, rms=True, scale_col=sb.sc2)

            wT_toks = []
            for tt in range(D2SLAB // 128):
                tlg = None
                for c in range(ND):
                    sp = (c == ND - 1)
                    tlg = p.op("tensor", (lambda c=c, tt=tt, sp=sp: lambda e: e.matmul(
                        PS[4][:, 0:8], sb.xhatb[:, c, tt * 128:(tt + 1) * 128], sb.rw_bf[:, c, :],
                        start=(c == 0), stop=sp))(),
                        waits=tmax(ln_last, io.get("ps4_free")) if c == 0 else (),
                        sem="T" if sp else None)
                lg = sb.rsc[:, 0:8]; msk = sb.rsc[:, 8:16]; tmp = sb.rsc[:, 16:24]
                e_ = sb.rsc[:, 24:32]; wraw = sb.rsc[:, 32:40]
                m1 = sb.rsc[:, 40:41]; m1n = sb.rsc[:, 41:42]; m2 = sb.rsc[:, 42:43]
                den = sb.rsc[:, 43:44]; dinv = sb.rsc[:, 44:45]; wt = sb.rsc[:, 0:8]
                ta = p.op("vector", (lambda lg=lg, tt=tt: lambda e: e.tensor_add(
                    lg, PS[4][:, 0:8], sb.noise_s[:, tt, :]))(),
                    waits=tmax(tlg, ldn, io.get("rsc_free")), sem="V")
                io["ps4_free"] = ta
                tb = p.op("vector", (lambda lg=lg, m1=m1: lambda e: e.reduce_max(m1, lg, axis=AX.X))(),
                          waits=(ta,), sem="V")
                tc_ = p.op("vector", (lambda lg=lg, msk=msk, m1=m1: lambda e: e.tensor_single_scalar(
                    msk, lg, m1[:, 0:1], ALU.is_equal))(), waits=(tb,), sem="V")
                td = p.op("vector", (lambda m1=m1, m1n=m1n: lambda e: e.tensor_scalar_mul(
                    m1n, m1, -1.0))(), waits=(tb,), sem="V")
                te_ = p.op("vector", (lambda msk=msk, tmp=tmp, lg=lg: lambda e: e.scalar_tensor_tensor(
                    tmp, msk, -1e9, lg, ALU.mult, ALU.add))(), waits=(tc_,), sem="V")
                tf = p.op("vector", (lambda tmp=tmp, m2=m2: lambda e: e.reduce_max(m2, tmp, axis=AX.X))(),
                          waits=(te_,), sem="V")
                tg = p.op("scalar", (lambda lg=lg, e_=e_, m1n=m1n: lambda e: e.activation(
                    e_, lg, AF.Exp, bias=m1n[:, 0:1]))(), waits=(td, te_), sem="S")
                tden = p.op("vector", (lambda e_=e_, den=den: lambda e: e.reduce_sum(den, e_, axis=AX.X))(),
                            waits=(tg,), sem="V")
                tk = p.op("vector", (lambda den=den, dinv=dinv: lambda e: e.reciprocal(dinv, den))(),
                          waits=(tden,), sem="V")
                th = p.op("vector", (lambda lg=lg, msk=msk, m2=m2: lambda e: e.tensor_single_scalar(
                    msk, lg, m2[:, 0:1], ALU.is_ge))(), waits=(tf, tg), sem="V")
                ti = p.op("vector", (lambda e_=e_, msk=msk, wraw=wraw: lambda e: e.tensor_mul(
                    wraw, e_, msk))(), waits=(th,), sem="V")
                tl = p.op("vector", (lambda wraw=wraw, dinv=dinv, wt=wt: lambda e: e.tensor_single_scalar(
                    wt, wraw, dinv[:, 0:1], ALU.mult))(), waits=(tk, ti), sem="V")
                ttr = p.op("tensor", (lambda wt=wt: lambda e: e.transpose(
                    PS[5][0:8, 0:128], wt, sb.identf[:]))(),
                    waits=tmax(tl, io.get("ps5_free")), sem="T")
                tcp = p.op("vector", (lambda tt=tt: lambda e: e.tensor_copy(
                    sb.w_T8[:, tt * 128:(tt + 1) * 128], PS[5][0:8, 0:128]))(),
                    waits=(ttr,), sem="V")
                io["ps5_free"] = tcp
                io["rsc_free"] = ttr
                wT_toks.append(tcp)
            wT_tok = tmax(*wT_toks)
            io["noise_free"] = wT_tok

            init_toks = {}
            for o in range(ND):
                for q in range(D2SLAB // TC):
                    tbm = p.op("tensor", (lambda o=o, q=q: lambda e: e.matmul(
                        PS[6 + q % 2][:], sb.boutr[0:8, o * 128:(o + 1) * 128],
                        sb.w_T8[0:8, q * TC:(q + 1) * TC], start=True, stop=True))(),
                        waits=tmax(wT_tok, io.get(f"ps{6 + q % 2}_free")), sem="T")
                    tbc = p.op("vector", (lambda o=o, q=q: lambda e: e.tensor_copy(
                        sb.out_acc[:, o, q * TC:(q + 1) * TC], PS[6 + q % 2][:]))(),
                        waits=tmax(tbm, acc_consumed), sem="V")
                    io[f"ps{6 + q % 2}_free"] = tbc
                    init_toks[(o, q)] = tbc
            acc_init = tmax(*init_toks.values())

            for ei in range(8):
                wbc_toks = []
                for q in range(D2SLAB // TC):
                    tw1 = p.op("tensor", (lambda ei=ei, q=q: lambda e: e.matmul(
                        PS[6][:], sb.E8[0:8, ei * 128:(ei + 1) * 128],
                        sb.w_T8[0:8, q * TC:(q + 1) * TC], start=True, stop=True))(),
                        waits=tmax(wT_tok, io.get("ps6_free")), sem="T")
                    tw2 = p.op("vector", (lambda q=q: lambda e: e.tensor_copy(
                        sb.wbc[:, q * TC:(q + 1) * TC], PS[6][:]))(),
                        waits=tmax(tw1, io.get("wbc_free")), sem="V")
                    io["ps6_free"] = tw2
                    wbc_toks.append(tw2)
                twb = tmax(*wbc_toks)

                t_w1 = p.dma("gpsimd", (lambda ei=ei: lambda e: e.dma_start(
                    out=sb.W1b[:], in_=dr["W1"][ei].rearrange("(c p) h -> p c h", p=128)))(),
                    waits=io.get("W1b_rd", ()), sem="w1ld")
                t_w2 = p.dma("gpsimd", (lambda ei=ei: lambda e: e.dma_start(
                    out=sb.W2b[:], in_=dr["W2"][ei].rearrange("(c p) h -> p c h", p=128)))(),
                    waits=io.get("W2b_rd", ()), sem="w2ld")
                t_w3 = p.dma("gpsimd", (lambda ei=ei: lambda e: e.dma_start(
                    out=sb.WOb[:], in_=dr["WO"][ei].rearrange("(c p) o -> p c o", p=128)))(),
                    waits=io.get("WOb_rd", ()), sem="w3ld")
                w1rd = []; w2rd = []; w3rd = []
                for tcq in range(D2SLAB // TC):
                    h_sl = tcq % 2
                    hbuf = sb.h0 if h_sl == 0 else sb.h1
                    h_toks = []
                    for hc in range(ND):
                        gr = hc % 2
                        ps_g = PS[0 + gr]
                        ps_u = PS[2 + gr]
                        sg = sb.sg0 if gr == 0 else sb.sg1
                        u2w = sb.u2w0 if gr == 0 else sb.u2w1
                        tgm = None
                        for c in range(ND):
                            sp = (c == ND - 1)
                            tgm = p.op("tensor", (lambda c=c, hc=hc, tcq=tcq, sp=sp, ps_g=ps_g: lambda e: e.matmul(
                                ps_g[:], sb.W1b[:, c, hc * 128:(hc + 1) * 128],
                                sb.xhatb[:, c, tcq * TC:(tcq + 1) * TC], start=(c == 0), stop=sp))(),
                                waits=tmax(t_w1, ln_last, io.get(f"ps{0 + gr}_free")) if c == 0 else (),
                                sem="T" if sp else None)
                        w1rd.append(tgm)
                        tsl_ = p.op("scalar", (lambda sg=sg, ps_g=ps_g, hc=hc, ei=ei: lambda e: e.activation(
                            sg[:], ps_g[:], AF.Silu, bias=sb.b1c[:, hc, ei:ei + 1]))(),
                            waits=tmax(tgm, io.get(f"sg{gr}_free")), sem="S")
                        io[f"ps{0 + gr}_free"] = tsl_
                        tum = None
                        for c in range(ND):
                            sp = (c == ND - 1)
                            tum = p.op("tensor", (lambda c=c, hc=hc, tcq=tcq, sp=sp, ps_u=ps_u: lambda e: e.matmul(
                                ps_u[:], sb.W2b[:, c, hc * 128:(hc + 1) * 128],
                                sb.xhatb[:, c, tcq * TC:(tcq + 1) * TC], start=(c == 0), stop=sp))(),
                                waits=tmax(t_w2, io.get(f"ps{2 + gr}_free")) if c == 0 else (),
                                sem="T" if sp else None)
                        w2rd.append(tum)
                        tuw = p.op("vector", (lambda u2w=u2w, hc=hc, ei=ei, ps_u=ps_u: lambda e: e.tensor_scalar_add(
                            u2w[:], ps_u[:], sb.b2c[:, hc, ei:ei + 1]))(),
                            waits=tmax(tum, io.get(f"u2w{gr}_free")), sem="V")
                        io[f"ps{2 + gr}_free"] = tuw
                        thm = p.op("vector", (lambda hbuf=hbuf, hc=hc, sg=sg, u2w=u2w: lambda e: e.tensor_mul(
                            hbuf[:, hc, :], sg[:], u2w[:]))(),
                            waits=tmax(tsl_, tuw, io.get(f"h{h_sl}_free")), sem="V")
                        io[f"sg{gr}_free"] = thm
                        io[f"u2w{gr}_free"] = thm
                        h_toks.append(thm)
                    acc_toks = []
                    for o in range(ND):
                        dr_ = o % 2
                        ps_d = PS[4 + dr_]
                        tdm = None
                        for hc in range(ND):
                            sp = (hc == ND - 1)
                            tdm = p.op("tensor", (lambda o=o, hc=hc, sp=sp, ps_d=ps_d, hbuf=hbuf: lambda e: e.matmul(
                                ps_d[:], sb.WOb[:, hc, o * 128:(o + 1) * 128], hbuf[:, hc, :],
                                start=(hc == 0), stop=sp))(),
                                waits=tmax(t_w3, h_toks[-1], io.get(f"ps{4 + dr_}_free")) if hc == 0 else (),
                                sem="T" if sp else None)
                        w3rd.append(tdm)
                        tmu = p.op("vector", (lambda o=o, tcq=tcq, ps_d=ps_d: lambda e: e.tensor_mul(
                            sb.tmpd[:], ps_d[:], sb.wbc[:, tcq * TC:(tcq + 1) * TC]))(),
                            waits=tmax(tdm, io.get("tmpd_free")), sem="V")
                        io[f"ps{4 + dr_}_free"] = tmu
                        tac = p.op("vector", (lambda o=o, tcq=tcq: lambda e: e.tensor_add(
                            sb.out_acc[:, o, tcq * TC:(tcq + 1) * TC],
                            sb.out_acc[:, o, tcq * TC:(tcq + 1) * TC], sb.tmpd[:]))(),
                            waits=tmax(tmu, acc_init), sem="V")
                        io["tmpd_free"] = tac
                        acc_toks.append(tac)
                    io[f"h{h_sl}_free"] = acc_toks[-1]
                io["W1b_rd"] = tmax(*w1rd)
                io["W2b_rd"] = tmax(*w2rd)
                io["WOb_rd"] = tmax(*w3rd)
                io["wbc_free"] = tmax(*w2rd)
                io["wbc_free"] = io.get("tmpd_free")

            res_toks = []
            for o in range(ND):
                ldx = p.dma("sync", (lambda s=s, o=o: lambda e: e.dma_start(
                    out=sb.x3s[:], in_=xin[o * 128:(o + 1) * 128, s * D2SLAB:(s + 1) * D2SLAB]))(),
                    waits=tmax(io.get("x3s_free")), sem="x3ld")
                t1_ = p.op("vector", (lambda o=o: lambda e: e.tensor_add(
                    sb.out_acc[:, o, :], sb.out_acc[:, o, :], sb.x3s[:]))(),
                    waits=tmax(ldx, io.get("tmpd_free")), sem="V")
                io["x3s_free"] = t1_
                t2_ = p.op("vector", (lambda o=o: lambda e: e.tensor_add(
                    sb.out_acc[:, o, :], sb.out_acc[:, o, :], sb.hsl[:, o, :]))(),
                    waits=(t1_,), sem="V")
                res_toks.append(t2_)
            tst_d = p.dma("sync", (lambda s=s: lambda e: e.dma_start(
                out=xout[:, s * D2SLAB:(s + 1) * D2SLAB].rearrange("(c p) t -> p c t", p=128),
                in_=sb.out_acc[:]))(), waits=tuple(res_toks), sem="xst")
            store_toks.append(tst_d)
            h_consumed = tmax(tst_d, *res_toks)
            acc_consumed = tmax(tst_d,)

        p.op("sync", lambda e: None, waits=tmax(*store_toks))
    return None



_D2_DINS = [("rwT", [D, 8], F32), ("noise", [T, 8], F32), ("W1", [8, D, D], F32),
            ("W2", [8, D, D], F32), ("WO", [8, D, D], F32), ("b1c", [128, ND, 8], F32),
            ("b2c", [128, ND, 8], F32), ("boutr", [8, D], F32), ("sc2", [128, ND], F32),
            ("identf", [128, 128], F32), ("E8", [8, 1024], F32),
            ("ones_col_bf", [128, 1], BF16), ("ones_row_f", [1, 128], F32),
            ("cst", [128, 8], F32)]


def build_launchD2():
    return _standalone(emit_D2, _D2_DINS,
                       extra=[("hdin", [D, T], BF16, "ExternalInput")])

def _prep_D2(inputs, b, xin, hdin):
    c = _CONSTS
    E8 = np.zeros((8, 1024), np.float32)
    for ei in range(8):
        E8[ei, ei * 128:(ei + 1) * 128] = 1.0
    noise = _gumbel_noise()[b] * 0.05
    return {
        "xin": xin,
        "hdin": hdin,
        "rwT": np.ascontiguousarray(np.asarray(inputs["L1_ffn_router_w"])),
        "noise": np.ascontiguousarray(noise.astype(np.float32)),
        "W1": np.ascontiguousarray(np.asarray(inputs["L1_ffn_w1"])),
        "W2": np.ascontiguousarray(np.asarray(inputs["L1_ffn_w2"])),
        "WO": np.ascontiguousarray(np.asarray(inputs["L1_ffn_wout"])),
        "b1c": np.ascontiguousarray(np.asarray(inputs["L1_ffn_b1"]).T.reshape(ND, 128, 8).transpose(1, 0, 2)),
        "b2c": np.ascontiguousarray(np.asarray(inputs["L1_ffn_b2"]).T.reshape(ND, 128, 8).transpose(1, 0, 2)),
        "boutr": np.ascontiguousarray(np.asarray(inputs["L1_ffn_bout"])),
        "sc2": np.ascontiguousarray(np.asarray(inputs["L1_ffn_scale2"]).reshape(ND, 128).T),
        "identf": np.eye(128, dtype=np.float32),
        "E8": E8,
        "ones_col_bf": c["ones_col_bf"], "ones_row_f": c["ones_row_f"], "cst": c["cst"],
    }



_SHARED_SPECS = [("cst", [128, 8], F32), ("ones_col_bf", [128, 1], BF16),
                 ("ones_row_f", [1, 128], F32), ("identf", [128, 128], F32),
                 ("E8", [8, 1024], F32), ("SB16", [16, 128], BF16)]


def build_fused():
    nc = bass.Bass("TRN2", target_bir_lowering=False)
    shared_dr = {}
    def shared_din(name, shape, dt):
        if name not in shared_dr:
            shared_dr[name] = nc.dram_tensor(name, shape, dt, kind="ExternalInput").ap()
        return shared_dr[name]
    def mkdr(pref, dins):
        dr = {}
        for (nm, shp, dt) in dins:
            if nm in SHARED_SBUF:
                dr[nm] = shared_din(nm, shp, dt)
            else:
                dr[nm] = nc.dram_tensor(f"{pref}_{nm}", shp, dt, kind="ExternalInput").ap()
        return dr
    drA = mkdr("A", _A_DINS)
    drB = mkdr("B", _B_DINS)
    drC = mkdr("C", _C_DINS)
    drD1 = mkdr("D1", _D1_DINS)
    drD2 = mkdr("D2", _D2_DINS)
    xin = nc.dram_tensor("A_xin", [D, T], F32, kind="ExternalInput").ap()
    xout = nc.dram_tensor("xout", [D, T], F32, kind="ExternalOutput").ap()
    x1d = nc.dram_tensor("x1d", [D, T], F32, kind="Internal").ap()
    x2d = nc.dram_tensor("x2d", [D, T], F32, kind="Internal").ap()
    x3d = nc.dram_tensor("x3d", [D, T], F32, kind="Internal").ap()
    lnxd = nc.dram_tensor("lnxd", [D, T], BF16, kind="Internal").ap()
    hdd = nc.dram_tensor("hdd", [D, T], BF16, kind="Internal").ap()
    KTd = nc.dram_tensor("KTd", [D, T], BF16, kind="Internal").ap()
    drA["xin"] = xin; drA["xout"] = x1d
    drB["xin"] = x1d; drB["xout"] = x2d
    drC["xin"] = x2d; drC["xout"] = x3d; drC["KTd"] = KTd
    drD1["xin"] = x3d; drD1["lnxd"] = lnxd; drD1["hd"] = hdd; drD1["KTd"] = KTd
    drD2["xin"] = x3d; drD2["hdin"] = hdd; drD2["xout"] = xout

    with ExitStack() as ctx:
        p = Prog(nc)
        PS = [ctx.enter_context(nc.psum_tensor(f"PS{k}", [128, TC], F32)) for k in range(8)]
        palloc = PhaseAlloc(nc)
        palloc.prealloc_shared(_SHARED_SPECS)
        for pref, emit, dr in (("A", emit_A, drA), ("B", emit_B, drB), ("C", emit_C, drC),
                               ("D1", emit_D1, drD1), ("D2", emit_D2, drD2)):
            palloc.start(pref)
            emit(nc, p, palloc, PS, dr)
            p.fence()
        p.build(ctx)
    return nc


def kernel(**inputs):
    cores = list(range(NCORE))
    nc = build_fused()
    in_maps = []
    for b in cores:
        m = {}
        def put(pref, d):
            for k, v in d.items():
                if k in ("xin", "hdin"):
                    continue
                if k in SHARED_SBUF:
                    m[k] = v
                else:
                    m[f"{pref}_{k}"] = v
        pa = _prep_A(inputs, b)
        m["A_xin"] = pa.pop("xin")
        put("A", pa)
        put("B", _prep_B(inputs, b, None))
        put("C", _prep_C(inputs, b, None))
        put("D1", _prep_D1(inputs, b, None))
        put("D2", _prep_D2(inputs, b, None, None))
        in_maps.append(m)
    res = run_bass_kernel_spmd(nc, in_maps, core_ids=cores)
    out = np.stack([np.asarray(r["xout"]).T for r in res.results])
    return out.astype(np.float32)


# version 04
# speedup vs baseline: 1.3116x; 1.0999x over previous
import numpy as np
import ml_dtypes
import concourse.bass as bass
import concourse.mybir as mybir
from concourse.bass_utils import run_bass_kernel_spmd
from contextlib import ExitStack

BF = ml_dtypes.bfloat16
F32 = mybir.dt.float32
BF16 = mybir.dt.bfloat16
AF = mybir.ActivationFunctionType
ALU = mybir.AluOpType
AX = mybir.AxisListType

T = 4096
D = 1024
ND = D // 128
TC = 512
NTC = T // TC
NCORE = 8
DSTATE = 16


class Prog:
    ENGINES = ("sync", "gpsimd", "tensor", "vector", "scalar")

    def __init__(self, nc):
        self.nc = nc
        self.ops = {e: [] for e in self.ENGINES}
        self.counts = {}
        self.order = []

    def _sem(self, name):
        if name not in self.counts:
            self.counts[name] = 0
            self.order.append(name)

    def op(self, eng, fn, waits=(), sem=None, inc=1):
        ws = tuple((s, v) for (s, v) in waits if s is not None)
        tok = None
        if sem is not None:
            self._sem(sem)
            self.counts[sem] += inc
            tok = (sem, self.counts[sem])
        self.ops[eng].append((fn, ws, sem, inc))
        return tok

    def dma(self, eng, fn, waits=(), sem=None):
        return self.op(eng, fn, waits, sem, 16)

    def fence(self):
        toks = tuple((n, c) for n, c in self.counts.items())
        for eng in self.ENGINES:
            self.op(eng, lambda e: None, waits=toks)

    def build(self, ctx):
        sems = {n: ctx.enter_context(self.nc.semaphore(n)) for n in self.order}
        block = ctx.enter_context(self.nc.Block())
        for eng_name in self.ENGINES:
            oplist = self.ops[eng_name]
            if not oplist:
                continue

            def mk(oplist):
                def emit(e):
                    hw = {}
                    for (fn, ws, sem, inc) in oplist:
                        for (s, v) in ws:
                            if hw.get(s, -1) < v:
                                e.wait_ge(sems[s], v)
                                hw[s] = v
                        ins = fn(e)
                        if sem is not None:
                            ins.then_inc(sems[sem], inc)
                return emit

            getattr(block, eng_name)(mk(oplist))


def tmax(*toks):
    best = {}
    def add(t):
        if t is None:
            return
        if isinstance(t, list) or (isinstance(t, tuple) and t and isinstance(t[0], (tuple, list))):
            for u in t:
                add(u)
        elif isinstance(t, tuple) and len(t) == 2 and isinstance(t[0], str):
            s, v = t
            if best.get(s, -1) < v:
                best[s] = v
        else:
            for u in t:
                add(u)
    for t in toks:
        add(t)
    return tuple(best.items())



import contextlib
_nullctx = contextlib.nullcontext


class _SB:
    pass


SHARED_SBUF = {"cst", "ones_col_bf", "ones_row_f", "identf", "E8", "SB16"}
SBUF_TOP = 229344


class PhaseAlloc:
    def __init__(self, nc):
        self.nc = nc
        self.base = None
        self.bump = 0
        self.peak = 0
        self.shared = {}
        self.pref = "P"

    def start(self, pref):
        if self.base is None:
            self.base = self.nc._sbuf_addr_for_side("left")
        self.bump = 0
        self.pref = pref

    def __call__(self, name, shape, dt=F32):
        if name in SHARED_SBUF:
            if name not in self.shared:
                assert self.base is None, f"shared alloc {name} after phase start"
                self.shared[name] = self.nc.alloc_sbuf_tensor("sh_" + name, list(shape), dt)
            return self.shared[name]
        esz = 2 if dt == BF16 else 4
        elems = 1
        for d_ in shape[1:]:
            elems *= d_
        off = (self.base + self.bump + 31) // 32 * 32
        t = self.nc.alloc_sbuf_tensor_at(f"{self.pref}_{name}", list(shape), dt, offset=off)
        self.bump = off - self.base + elems * esz
        self.peak = max(self.peak, self.base + self.bump)
        assert self.base + self.bump <= SBUF_TOP, \
            f"phase {self.pref} SBUF overflow: {self.base + self.bump} > {SBUF_TOP} at {name}"
        return t

    def prealloc_shared(self, specs):
        for (name, shape, dt) in specs:
            if name not in self.shared:
                self.shared[name] = self.nc.alloc_sbuf_tensor("sh_" + name, list(shape), dt)


def _consts():
    c = {}
    c["ident"] = np.eye(128, dtype=BF)
    LR = np.zeros((128, 16 * 128), dtype=np.float32)
    LS = np.zeros((128, 16 * 128), dtype=np.float32)
    for j in range(16):
        for p in range(128):
            k = j * 8 + p // 16
            LR[k, j * 128 + p] = 1.0
            LS[p, j * 128 + k] = 1.0
    c["LR"] = LR.astype(BF)
    c["LS"] = LS.astype(BF)
    SB16 = np.zeros((16, 128), dtype=np.float32)
    for m in range(128):
        SB16[m % 16, m] = 1.0
    c["SB16"] = SB16.astype(BF)
    cp = np.zeros((128, 8), np.float32)
    cp[:, 0] = 1.0; cp[:, 1] = -1.0; cp[:, 2] = -0.5; cp[:, 3] = 1e-5
    cp[:, 4] = 1e-6; cp[:, 5] = 0.125; cp[:, 6] = 0.0
    c["cst"] = cp
    c["ones_col_bf"] = np.ones((128, 1), dtype=BF)
    c["ones_row_f"] = np.ones((1, 128), dtype=np.float32)
    c["ones_row_bf"] = np.ones((1, 128), dtype=BF)
    c["neg_row_bf"] = -np.ones((1, 128), dtype=BF)
    half = 32
    inv = 1.0 / 10000.0 ** (np.arange(half, dtype=np.float64) / half)
    ang = np.arange(T, dtype=np.float64)[:, None] * inv[None, :]
    cos = np.cos(ang).astype(np.float32)
    sin = np.sin(ang).astype(np.float32)
    cos_bc = np.empty((128, T), np.float32)
    sin_sgn = np.empty((128, T), np.float32)
    for p in range(128):
        cos_bc[p] = cos[:, p % 32]
        sin_sgn[p] = -sin[:, p % 32] if (p % 64) < 32 else sin[:, p % 32]
    c["cos_bc"] = cos_bc.astype(BF)
    c["sin_sgn"] = sin_sgn.astype(BF)
    RS = np.zeros((128, 128), np.float32)
    for p in range(128):
        q = p + 32 if (p % 64) < 32 else p - 32
        RS[q, p] = 1.0
    c["Rswap"] = RS.astype(BF)
    return c

_CONSTS = _consts()
_NOISE = None


def _gumbel_noise():
    global _NOISE
    if _NOISE is None:
        import jax
        cpu = jax.devices("cpu")[0]
        with jax.default_device(cpu):
            g = jax.random.gumbel(jax.random.key(42), (8, 4096, 8), dtype=jax.numpy.float32)
            _NOISE = np.asarray(g)
    return _NOISE



def emit_layernorm(p, sb, io, xc, xc_tok, kA=6, kB=7, kS1=4, kS2=5, PS=None, eps_col=3):
    psA, psB, psS1, psS2 = PS[kA], PS[kB], PS[kS1], PS[kS2]
    xcw = xc[:].rearrange("p c t -> p (c t)")
    t_xb = p.op("vector", lambda e: e.tensor_copy(sb.xb[:].rearrange("p c t -> p (c t)"), xcw),
                waits=xc_tok, sem="V")
    t_sq = p.op("scalar", lambda e: e.activation(sb.sq[:].rearrange("p c t -> p (c t)"), xcw, AF.Square),
                waits=xc_tok, sem="S")
    tsx = tsq = None
    for c in range(ND):
        sp = (c == ND - 1)
        tsx = p.op("tensor", (lambda c=c, sp=sp: lambda e: e.matmul(
            psS1[0:1, :], sb.ones_col_bf[:], sb.xb[:, c, :], start=(c == 0), stop=sp))(),
            waits=tmax(t_xb, io.get(f"ps{kS1}_free")) if c == 0 else (), sem="T" if sp else None)
    for c in range(ND):
        sp = (c == ND - 1)
        tsq = p.op("tensor", (lambda c=c, sp=sp: lambda e: e.matmul(
            psS2[0:1, :], sb.ones_col_bf[:], sb.sq[:, c, :], start=(c == 0), stop=sp))(),
            waits=tmax(t_sq, io.get(f"ps{kS2}_free")) if c == 0 else (), sem="T" if sp else None)
    t1 = p.op("vector", lambda e: e.tensor_scalar_mul(sb.row_nm[:], psS1[0:1, :], -1.0 / D),
              waits=(tsx,), sem="V")
    t2 = p.op("vector", lambda e: e.tensor_scalar_mul(sb.row_t1[:], psS2[0:1, :], 1.0 / D),
              waits=(tsq,), sem="V")
    t3 = p.op("vector", lambda e: e.tensor_mul(sb.row_t2[:], sb.row_nm[:], sb.row_nm[:]),
              waits=(t1,), sem="V")
    t4 = p.op("vector", lambda e: e.tensor_sub(sb.row_t1[:], sb.row_t1[:], sb.row_t2[:]),
              waits=(t2, t3), sem="V")
    t5 = p.op("scalar", lambda e: e.activation(sb.row_t2[:], sb.row_t1[:], AF.Ln,
                                               bias=sb.cst[0:1, eps_col:eps_col + 1]),
              waits=(t4,), sem="S")
    t6 = p.op("scalar", lambda e: e.activation(sb.row_a[:], sb.row_t2[:], AF.Exp, scale=-0.5),
              waits=(t5,), sem="S")
    t7 = p.op("vector", lambda e: e.tensor_mul(sb.row_b[:], sb.row_nm[:], sb.row_a[:]),
              waits=(t6,), sem="V")
    tA = p.op("tensor", lambda e: e.matmul(psA[:], sb.ones_row_f[:], sb.row_a[:], start=True, stop=True),
              waits=tmax(t6, io.get(f"ps{kA}_free")), sem="T")
    tB = p.op("tensor", lambda e: e.matmul(psB[:], sb.ones_row_f[:], sb.row_b[:], start=True, stop=True),
              waits=tmax(t7, io.get(f"ps{kB}_free")), sem="T")
    last = None
    for c in range(ND):
        tm_ = p.op("vector", (lambda c=c: lambda e: e.tensor_mul(sb.xhat[:, c, :], xc[:, c, :], psA[:]))(),
                   waits=tmax(tA, xc_tok) if c == 0 else (), sem="V")
        last = p.op("vector", (lambda c=c: lambda e: e.tensor_add(
            sb.xhat[:, c, :], sb.xhat[:, c, :], psB[:]))(),
            waits=tmax(tm_, (tB,) if c == 0 else None), sem="V")
    t_cast = p.op("vector", lambda e: e.tensor_copy(
        sb.xhatb[:].rearrange("p c t -> p (c t)"), sb.xhat[:].rearrange("p c t -> p (c t)")),
        waits=(last,), sem="V")
    io[f"ps{kS1}_free"] = t1
    io[f"ps{kS2}_free"] = t2
    io[f"ps{kA}_free"] = last
    io[f"ps{kB}_free"] = last
    return last, t_cast



def emit_A(nc, p, palloc, PS, dr):
    xin = dr["xin"]
    xout = dr["xout"]

    with _nullctx():
        sb = _SB()
        def alloc(name, shape, dt=F32):
            t = palloc(name, shape, dt)
            setattr(sb, name, t)
            return t
        alloc("xc0", [128, ND, TC]); alloc("xc1", [128, ND, TC])
        alloc("xb", [128, ND, TC], BF16)
        alloc("sq", [128, ND, TC], BF16)
        alloc("xhat", [128, ND, TC])
        alloc("xhatb", [128, ND, TC], BF16)
        alloc("dt_scr", [128, TC])
        alloc("dth", [128, ND, TC], BF16); alloc("dtl", [128, ND, TC], BF16)
        alloc("uh", [128, ND, TC], BF16); alloc("ul", [128, ND, TC], BF16)
        alloc("u_tmp", [128, TC])
        alloc("zb", [128, TC]); alloc("azb", [128, TC])
        alloc("ebuf", [128, TC]); alloc("lbuf", [128, TC])
        alloc("rowB", [16, TC]); alloc("rowC", [16, TC])
        alloc("Bh", [16, TC], BF16); alloc("Bl", [16, TC], BF16)
        alloc("Ch", [16, TC], BF16); alloc("Cl", [16, TC], BF16)
        alloc("Bbc", [128, TC]); alloc("Cbc", [128, TC])
        alloc("dA0", [128, TC]); alloc("dA1", [128, TC])
        alloc("dB0", [128, TC]); alloc("dB1", [128, TC])
        alloc("st0", [128, TC]); alloc("st1", [128, TC])
        alloc("tmp0", [128, TC], BF16); alloc("tmp1", [128, TC], BF16)
        alloc("state_cols", [128, 128])
        alloc("ybf", [128, ND, TC], BF16)
        for r in ("row_nm", "row_t1", "row_t2", "row_a", "row_b"):
            alloc(r, [1, TC])
        alloc("wdtT", [1, D])
        alloc("bdt", [128, ND])
        alloc("wxpT", [128, ND, 32], BF16)
        alloc("bxp", [16, 2])
        alloc("wou0", [128, ND, 128], BF16)
        alloc("wou1", [128, ND, 128], BF16)
        alloc("bout", [128, ND])
        alloc("Acl", [128, 128])
        alloc("negA", [128, 128])
        alloc("LR", [128, 2048], BF16)
        alloc("LS", [128, 2048], BF16)
        alloc("SB16", [16, 128], BF16)
        alloc("ones_col_bf", [128, 1], BF16)
        alloc("ones_row_f", [1, 128])
        alloc("cst", [128, 8])


        wtoks = []
        for nm_ in ("wdtT", "bdt", "bxp", "bout", "Acl", "LR", "LS", "SB16",
                    "ones_col_bf", "ones_row_f", "cst"):
            wtoks.append(p.dma("sync", (lambda nm_=nm_: lambda e: e.dma_start(
                out=getattr(sb, nm_)[:], in_=dr[nm_]))(), sem="wld"))
        wtoks.append(p.dma("gpsimd", lambda e: e.dma_start(
            out=sb.wxpT[:], in_=dr["wxpT"].rearrange("(c p) e -> p c e", p=128)), sem="wldg"))
        wtok = tmax(*wtoks)
        t_ea = p.op("scalar", lambda e: e.activation(sb.negA[:], sb.Acl[:], AF.Exp),
                    waits=wtok, sem="S")
        t_na = p.op("vector", lambda e: e.tensor_scalar_mul(sb.negA[:], sb.negA[:], -1.0),
                    waits=(t_ea,), sem="V")
        wtok = tmax(wtok, t_na)
        t_sc = p.op("vector", lambda e: e.memset(sb.state_cols[:], 0.0), sem="V")

        io = {}
        store_toks = []
        xc_consumed = {0: (), 1: ()}
        scan_prev = {}
        scan_prev_all = t_sc
        for i in range(NTC):
            s = i % 2
            xc = sb.xc0 if s == 0 else sb.xc1
            ldx = p.dma("sync", (lambda i=i, xc=xc: lambda e: e.dma_start(
                out=xc[:], in_=xin[:, i * TC:(i + 1) * TC].rearrange("(c p) t -> p c t", p=128)))(),
                waits=xc_consumed[s], sem=f"xld{s}")
            xc_tok = tmax(ldx, wtok)

            t_apply, t_cast = emit_layernorm(p, sb, io, xc, xc_tok, kA=6, kB=7, kS1=4, kS2=5, PS=PS)

            dthl_toks = []
            u_toks = []
            prev_dt_readers = ()
            for c in range(ND):
                tz = p.op("tensor", (lambda c=c: lambda e: e.matmul(
                    PS[4][:], sb.wdtT[0:1, c * 128:(c + 1) * 128], sb.xhat[0:1, 0, :],
                    start=True, stop=True))(),
                    waits=tmax(t_apply, io.get("ps4_free")), sem="T")
                tzb = p.op("vector", (lambda c=c: lambda e: e.tensor_scalar_add(
                    sb.zb[:], PS[4][:], sb.bdt[:, c:c + 1]))(), waits=(tz,), sem="V")
                io["ps4_free"] = tzb
                tazb = p.op("scalar", lambda e: e.activation(sb.azb[:], sb.zb[:], AF.Abs),
                            waits=(tzb,), sem="S")
                te_ = p.op("scalar", lambda e: e.activation(sb.ebuf[:], sb.azb[:], AF.Exp, scale=-1.0),
                           waits=(tazb,), sem="S")
                tl_ = p.op("scalar", lambda e: e.activation(sb.lbuf[:], sb.ebuf[:], AF.Ln,
                                                            bias=sb.cst[:, 0:1]),
                           waits=(te_,), sem="S")
                tdt = p.op("vector", (lambda c=c: lambda e: e.scalar_tensor_tensor(
                    sb.dt_scr[:], sb.zb[:], 0.0, sb.lbuf[:], ALU.max, ALU.add))(),
                    waits=tmax(tl_, tzb, prev_dt_readers), sem="V")
                tdh = p.op("vector", (lambda c=c: lambda e: e.tensor_copy(
                    sb.dth[:, c, :], sb.dt_scr[:]))(), waits=(tdt,), sem="V")
                tdl = p.op("vector", (lambda c=c: lambda e: e.tensor_sub(
                    sb.dtl[:, c, :], sb.dt_scr[:], sb.dth[:, c, :]))(), waits=(tdh,), sem="V")
                tu = p.op("vector", (lambda c=c: lambda e: e.tensor_mul(
                    sb.u_tmp[:], sb.dt_scr[:], sb.xhat[:, c, :]))(), waits=(tdt,), sem="V")
                th_ = p.op("vector", (lambda c=c: lambda e: e.tensor_copy(
                    sb.uh[:, c, :], sb.u_tmp[:]))(), waits=(tu,), sem="V")
                tl2 = p.op("vector", (lambda c=c: lambda e: e.tensor_sub(
                    sb.ul[:, c, :], sb.u_tmp[:], sb.uh[:, c, :]))(), waits=(th_,), sem="V")
                dthl_toks.append(tmax(tdh, tdl))
                u_toks.append(tl2)
                prev_dt_readers = tmax(tdl, tu)

            tBp = tCp = None
            for c in range(ND):
                sp_ = (c == ND - 1)
                tBp = p.op("tensor", (lambda c=c, sp_=sp_: lambda e: e.matmul(
                    PS[4][0:16, :], sb.wxpT[:, c, 0:16], sb.xhatb[:, c, :], start=(c == 0), stop=sp_))(),
                    waits=tmax(t_cast, io.get("ps4_free")) if c == 0 else (), sem="T" if sp_ else None)
            for c in range(ND):
                sp_ = (c == ND - 1)
                tCp = p.op("tensor", (lambda c=c, sp_=sp_: lambda e: e.matmul(
                    PS[5][0:16, :], sb.wxpT[:, c, 16:32], sb.xhatb[:, c, :], start=(c == 0), stop=sp_))(),
                    waits=tmax(io.get("ps5_free")) if c == 0 else (), sem="T" if sp_ else None)
            tBr = p.op("vector", lambda e: e.tensor_scalar_add(sb.rowB[:], PS[4][0:16, :], sb.bxp[:, 0:1]),
                       waits=(tBp,), sem="V")
            tCr = p.op("vector", lambda e: e.tensor_scalar_add(sb.rowC[:], PS[5][0:16, :], sb.bxp[:, 1:2]),
                       waits=(tCp,), sem="V")
            io["ps4_free"] = tBr
            io["ps5_free"] = tCr
            tBh = p.op("vector", lambda e: e.tensor_copy(sb.Bh[:], sb.rowB[:]), waits=(tBr,), sem="V")
            tBl = p.op("vector", lambda e: e.tensor_sub(sb.Bl[:], sb.rowB[:], sb.Bh[:]), waits=(tBh,), sem="V")
            tCh = p.op("vector", lambda e: e.tensor_copy(sb.Ch[:], sb.rowC[:]), waits=(tCr,), sem="V")
            tCl = p.op("vector", lambda e: e.tensor_sub(sb.Cl[:], sb.rowC[:], sb.Ch[:]), waits=(tCh,), sem="V")
            p.op("tensor", lambda e: e.matmul(PS[6][:], sb.SB16[:], sb.Bh[:], start=True, stop=False),
                 waits=tmax(tBh, io.get("ps6_free")), sem=None)
            tb2 = p.op("tensor", lambda e: e.matmul(PS[6][:], sb.SB16[:], sb.Bl[:], start=False, stop=True),
                       waits=(tBl,), sem="T")
            p.op("tensor", lambda e: e.matmul(PS[7][:], sb.SB16[:], sb.Ch[:], start=True, stop=False),
                 waits=tmax(tCh, io.get("ps7_free")), sem=None)
            tc2 = p.op("tensor", lambda e: e.matmul(PS[7][:], sb.SB16[:], sb.Cl[:], start=False, stop=True),
                       waits=(tCl,), sem="T")
            tBbc = p.op("vector", lambda e: e.tensor_copy(sb.Bbc[:], PS[6][:]), waits=(tb2,), sem="V")
            tCbc = p.op("vector", lambda e: e.tensor_copy(sb.Cbc[:], PS[7][:]), waits=(tc2,), sem="V")
            io["ps6_free"] = tBbc
            io["ps7_free"] = tCbc

            y_done = {}
            rep_free = {0: None, 1: None}
            ubc_free = {0: None, 1: None}
            buf_free = {0: None, 1: None}
            pend_y = None

            def emit_pend(pend):
                c_, j_, tmb_, ttm_ = pend
                ps_y = PS[4 + (c_ % 2)]
                ty = p.op("tensor", (lambda j_=j_, tmb_=tmb_, ps_y=ps_y: lambda e: e.matmul(
                    ps_y[:], sb.LS[:, j_ * 128:(j_ + 1) * 128], tmb_[:],
                    start=(j_ == 0), stop=(j_ == 15)))(),
                    waits=tmax(ttm_, (y_done.get(c_ % 2) if j_ == 0 else None),
                               (io.get(f"ps{4 + c_ % 2}_free") if j_ == 0 else None)),
                    sem="T" if j_ == 15 else None)
                if j_ == 15:
                    tev = p.op("scalar", (lambda c_=c_, ps_y=ps_y: lambda e: e.activation(
                        sb.ybf[:, c_, :], ps_y[:], AF.Copy))(), waits=(ty,), sem="S")
                    y_done[c_ % 2] = tev

            for j2 in range(128):
                c, j = j2 // 16, j2 % 16
                r = j2 % 2
                ps_rep = PS[0 + r]
                ps_ubc = PS[2 + r]
                dAb = sb.dA0 if r == 0 else sb.dA1
                dBb = sb.dB0 if r == 0 else sb.dB1
                stb = sb.st0 if r == 0 else sb.st1
                tmb = sb.tmp0 if r == 0 else sb.tmp1

                p.op("tensor", (lambda c=c, j=j, ps_rep=ps_rep: lambda e: e.matmul(
                    ps_rep[:], sb.LR[:, j * 128:(j + 1) * 128], sb.dth[:, c, :], start=True, stop=False))(),
                    waits=tmax(dthl_toks[c], rep_free[r]), sem=None)
                t2_ = p.op("tensor", (lambda c=c, j=j, ps_rep=ps_rep: lambda e: e.matmul(
                    ps_rep[:], sb.LR[:, j * 128:(j + 1) * 128], sb.dtl[:, c, :], start=False, stop=True))(),
                    sem="T")
                tdA = p.op("scalar", (lambda dAb=dAb, ps_rep=ps_rep, j2=j2: lambda e: e.activation(
                    dAb[:], ps_rep[:], AF.Exp, scale=sb.negA[:, j2:j2 + 1]))(),
                    waits=tmax(t2_, buf_free[r]), sem="S")
                rep_free[r] = tdA
                p.op("tensor", (lambda c=c, j=j, ps_ubc=ps_ubc: lambda e: e.matmul(
                    ps_ubc[:], sb.LR[:, j * 128:(j + 1) * 128], sb.uh[:, c, :], start=True, stop=False))(),
                    waits=tmax(u_toks[c], ubc_free[r]), sem=None)
                t4_ = p.op("tensor", (lambda c=c, j=j, ps_ubc=ps_ubc: lambda e: e.matmul(
                    ps_ubc[:], sb.LR[:, j * 128:(j + 1) * 128], sb.ul[:, c, :], start=False, stop=True))(),
                    sem="T")
                if pend_y is not None:
                    emit_pend(pend_y)
                tdB = p.op("vector", (lambda dBb=dBb, ps_ubc=ps_ubc: lambda e: e.tensor_mul(
                    dBb[:], ps_ubc[:], sb.Bbc[:]))(), waits=tmax(t4_, tBbc, buf_free[r]), sem="V")
                ubc_free[r] = tdB
                tsc = p.op("vector", (lambda dAb=dAb, dBb=dBb, stb=stb, j2=j2: lambda e: e.tensor_tensor_scan(
                    stb[:], dAb[:], dBb[:], sb.state_cols[:, j2:j2 + 1], ALU.mult, ALU.add))(),
                    waits=tmax(tdA, tdB, scan_prev.get(j2, scan_prev_all)), sem="V")
                tsv = p.op("vector", (lambda stb=stb, j2=j2: lambda e: e.tensor_copy(
                    sb.state_cols[:, j2:j2 + 1], stb[:, TC - 1:TC]))(), waits=(tsc,), sem="V")
                scan_prev[j2] = tsv
                ttm = p.op("gpsimd", (lambda stb=stb, tmb=tmb: lambda e: e.tensor_tensor(
                    tmb[:], stb[:], sb.Cbc[:], ALU.mult))(), waits=tmax(tsv, tCbc), sem="G")
                buf_free[r] = ttm
                pend_y = (c, j, tmb, ttm)
            emit_pend(pend_y)
            pend_y = None
            io["ps4_free"] = y_done[0]
            io["ps5_free"] = y_done[1]
            y_all = tmax(y_done[0], y_done[1])

            proj_toks = []
            for o in range(ND):
                wsl = o % 2
                wob = sb.wou0 if wsl == 0 else sb.wou1
                two = p.dma("gpsimd", (lambda o=o, wob=wob: lambda e: e.dma_start(
                    out=wob[:], in_=dr["woutT"][:, o * 128:(o + 1) * 128]
                    .rearrange("(c p) o2 -> p c o2", p=128)))(),
                    waits=tmax(io.get(f"wou{wsl}_rd")), sem=f"would{wsl}")
                ko = 6 + (o % 2)
                ps_o = PS[ko]
                tpo = None
                for c in range(ND):
                    sp_ = (c == ND - 1)
                    w0 = tmax(y_all, two, io.get(f"ps{ko}_free")) if c == 0 else ()
                    tpo = p.op("tensor", (lambda o=o, c=c, sp_=sp_, ps_o=ps_o, wob=wob: lambda e: e.matmul(
                        ps_o[:], wob[:, c, :], sb.ybf[:, c, :],
                        start=(c == 0), stop=sp_))(),
                        waits=w0, sem="T" if sp_ else None)
                io[f"wou{wsl}_rd"] = (tpo,)
                tst = p.op("vector", (lambda o=o, ps_o=ps_o, xc=xc: lambda e: e.scalar_tensor_tensor(
                    xc[:, o, :], ps_o[:], sb.bout[:, o:o + 1], xc[:, o, :], ALU.add, ALU.add))(),
                    waits=tmax(tpo, wtok), sem="V")
                io[f"ps{ko}_free"] = tst
                proj_toks.append(tst)

            tst_d = p.dma("sync", (lambda i=i, xc=xc: lambda e: e.dma_start(
                out=xout[:, i * TC:(i + 1) * TC].rearrange("(c p) t -> p c t", p=128), in_=xc[:]))(),
                waits=tuple(proj_toks), sem=f"xst{s}")
            store_toks.append(tst_d)
            xc_consumed[s] = tmax(tst_d, *proj_toks)
            scan_prev_all = scan_prev[127]

        p.op("sync", lambda e: None, waits=tmax(*store_toks))
    return None




_A_DINS = [("wdtT", [1, D], F32), ("bdt", [128, ND], F32), ("wxpT", [D, 32], F32),
           ("bxp", [16, 2], F32), ("woutT", [D, D], F32), ("bout", [128, ND], F32),
           ("Acl", [128, 128], F32), ("LR", [128, 2048], BF16), ("LS", [128, 2048], BF16),
           ("SB16", [16, 128], BF16), ("ones_col_bf", [128, 1], BF16),
           ("ones_row_f", [1, 128], F32), ("cst", [128, 8], F32)]


def _standalone(emit, dins, extra=()):
    nc = bass.Bass("TRN2", target_bir_lowering=False)
    dr = {"xin": nc.dram_tensor("xin", [D, T], F32, kind="ExternalInput").ap()}
    for (nm, shp, dt) in dins:
        dr[nm] = nc.dram_tensor(nm, shp, dt, kind="ExternalInput").ap()
    dr["xout"] = nc.dram_tensor("xout", [D, T], F32, kind="ExternalOutput").ap()
    for (nm, shp, dt, kind) in extra:
        dr[nm] = nc.dram_tensor(nm, shp, dt, kind=kind).ap()
    with ExitStack() as ctx:
        p = Prog(nc)
        PSb23 = ctx.enter_context(nc.psum_tensor("PSb23", [128, 2 * TC], F32))
        PSb67 = ctx.enter_context(nc.psum_tensor("PSb67", [128, 2 * TC], F32))
        PS = [ctx.enter_context(nc.psum_tensor("PSs0", [128, TC], F32)),
              ctx.enter_context(nc.psum_tensor("PSs1", [128, TC], F32)),
              PSb23[:, 0:TC], PSb23[:, TC:2 * TC],
              ctx.enter_context(nc.psum_tensor("PSs4", [128, TC], F32)),
              ctx.enter_context(nc.psum_tensor("PSs5", [128, TC], F32)),
              PSb67[:, 0:TC], PSb67[:, TC:2 * TC],
              PSb23[:], PSb67[:]]
        allocd = {}
        def palloc(name, shape, dt=F32):
            if name not in allocd:
                allocd[name] = ctx.enter_context(nc.sbuf_tensor("s_" + name, shape, dt))
            return allocd[name]
        emit(nc, p, palloc, PS, dr)
        p.build(ctx)
    return nc


def build_launchA():
    return _standalone(emit_A, _A_DINS)

def _acl(A_log):
    out = np.empty((128, 128), np.float32)
    for j2 in range(128):
        cc, j = j2 // 16, j2 % 16
        for pp in range(128):
            out[pp, j2] = A_log[cc * 128 + j * 8 + pp // 16, pp % 16]
    return out


def _prep_A(inputs, b):
    c = _CONSTS
    x = np.ascontiguousarray(np.asarray(inputs["L0_attn_x"][b]).T)
    return {
        "xin": x,
        "wdtT": np.ascontiguousarray(np.asarray(inputs["L0_attn_W_dt"]).reshape(1, D)),
        "bdt": np.ascontiguousarray(np.asarray(inputs["L0_attn_b_dt"]).reshape(ND, 128).T),
        "wxpT": np.ascontiguousarray(np.asarray(inputs["L0_attn_W_xp"]).T),
        "bxp": np.ascontiguousarray(np.asarray(inputs["L0_attn_b_xp"]).reshape(2, 16).T),
        "woutT": np.ascontiguousarray(np.asarray(inputs["L0_attn_W_out"]).T),
        "bout": np.ascontiguousarray(np.asarray(inputs["L0_attn_b_out"]).reshape(ND, 128).T),
        "Acl": _acl(np.asarray(inputs["L0_attn_A_log"])),
        "LR": c["LR"], "LS": c["LS"], "SB16": c["SB16"],
        "ones_col_bf": c["ones_col_bf"], "ones_row_f": c["ones_row_f"], "cst": c["cst"],
    }


def kernel_unfused(**inputs):
    cores = list(range(NCORE))
    resA = run_bass_kernel_spmd(build_launchA(), [_prep_A(inputs, b) for b in cores], core_ids=cores)
    x1 = [np.asarray(r["xout"]) for r in resA.results]
    resB = run_bass_kernel_spmd(build_launchB(), [_prep_B(inputs, b, x1[b]) for b in cores], core_ids=cores)
    x2 = [np.asarray(r["xout"]) for r in resB.results]
    resC = run_bass_kernel_spmd(build_launchC(), [_prep_C(inputs, b, x2[b]) for b in cores], core_ids=cores)
    x3 = [np.asarray(r["xout"]) for r in resC.results]
    resD1 = run_bass_kernel_spmd(build_launchD1(), [_prep_D1(inputs, b, x3[b]) for b in cores], core_ids=cores)
    hds = [np.asarray(r["hd"]) for r in resD1.results]
    resD2 = run_bass_kernel_spmd(build_launchD2(), [_prep_D2(inputs, b, x3[b], hds[b]) for b in cores], core_ids=cores)
    out = np.stack([np.asarray(r["xout"]).T for r in resD2.results])
    return out.astype(np.float32)



SLAB = 1024
NSLAB = T // SLAB
HB = 512
NHB = 4096 // HB
NHC = HB // 128


def emit_norm2(p, sb, io, PS, xcv, xc_tok, xhatb_v, banks=(6, 7, 4, 5), eps_col=3,
               rms=False, scale_col=None):
    kA, kB, kS1, kS2 = banks
    psA, psB, psS1, psS2 = PS[kA], PS[kB], PS[kS1], PS[kS2]
    t_sq = t_xb = None
    tsx = tsq = None
    if not rms:
        t_xb = p.op("vector", lambda e: e.tensor_copy(sb.xb[:], xcv), waits=xc_tok, sem="V")
    t_sq = p.op("scalar", lambda e: e.activation(sb.sq[:], xcv, AF.Square),
                waits=xc_tok, sem="S")
    if not rms:
        for c in range(ND):
            sp = (c == ND - 1)
            tsx = p.op("tensor", (lambda c=c, sp=sp: lambda e: e.matmul(
                psS1[0:1, :], sb.ones_col_bf[:], sb.xb[:, c, :], start=(c == 0), stop=sp))(),
                waits=tmax(t_xb, io.get(f"ps{kS1}_free")) if c == 0 else (), sem="T" if sp else None)
    for c in range(ND):
        sp = (c == ND - 1)
        tsq = p.op("tensor", (lambda c=c, sp=sp: lambda e: e.matmul(
            psS2[0:1, :], sb.ones_col_bf[:], sb.sq[:, c, :], start=(c == 0), stop=sp))(),
            waits=tmax(t_sq, io.get(f"ps{kS2}_free")) if c == 0 else (), sem="T" if sp else None)
    if rms:
        t2 = p.op("vector", lambda e: e.tensor_scalar_mul(sb.row_t1[:], psS2[0:1, :], 1.0 / D),
                  waits=(tsq,), sem="V")
        t5 = p.op("scalar", lambda e: e.activation(sb.row_t2[:], sb.row_t1[:], AF.Ln,
                                                   bias=sb.cst[0:1, eps_col:eps_col + 1]),
                  waits=(t2,), sem="S")
        t6 = p.op("scalar", lambda e: e.activation(sb.row_a[:], sb.row_t2[:], AF.Exp, scale=-0.5),
                  waits=(t5,), sem="S")
        io[f"ps{kS2}_free"] = t2
        tA = p.op("tensor", lambda e: e.matmul(psA[:], sb.ones_row_f[:], sb.row_a[:],
                                               start=True, stop=True),
                  waits=tmax(t6, io.get(f"ps{kA}_free")), sem="T")
        last = None
        for c in range(ND):
            if scale_col is not None:
                last = p.op("vector", (lambda c=c: lambda e: e.scalar_tensor_tensor(
                    xhatb_v[:, c, :], xcv[:, c, :], scale_col[:, c:c + 1], psA[:],
                    ALU.mult, ALU.mult))(),
                    waits=tmax(tA, xc_tok) if c == 0 else (), sem="V")
            else:
                last = p.op("vector", (lambda c=c: lambda e: e.tensor_mul(
                    xhatb_v[:, c, :], xcv[:, c, :], psA[:]))(),
                    waits=tmax(tA, xc_tok) if c == 0 else (), sem="V")
        io[f"ps{kA}_free"] = last
        return last
    t1 = p.op("vector", lambda e: e.tensor_scalar_mul(sb.row_nm[:], psS1[0:1, :], -1.0 / D),
              waits=(tsx,), sem="V")
    t2 = p.op("vector", lambda e: e.tensor_scalar_mul(sb.row_t1[:], psS2[0:1, :], 1.0 / D),
              waits=(tsq,), sem="V")
    t3 = p.op("vector", lambda e: e.tensor_mul(sb.row_t2[:], sb.row_nm[:], sb.row_nm[:]),
              waits=(t1,), sem="V")
    t4 = p.op("vector", lambda e: e.tensor_sub(sb.row_t1[:], sb.row_t1[:], sb.row_t2[:]),
              waits=(t2, t3), sem="V")
    t5 = p.op("scalar", lambda e: e.activation(sb.row_t2[:], sb.row_t1[:], AF.Ln,
                                               bias=sb.cst[0:1, eps_col:eps_col + 1]),
              waits=(t4,), sem="S")
    t6 = p.op("scalar", lambda e: e.activation(sb.row_a[:], sb.row_t2[:], AF.Exp, scale=-0.5),
              waits=(t5,), sem="S")
    t7 = p.op("vector", lambda e: e.tensor_mul(sb.row_b[:], sb.row_nm[:], sb.row_a[:]),
              waits=(t6,), sem="V")
    tA = p.op("tensor", lambda e: e.matmul(psA[:], sb.ones_row_f[:], sb.row_a[:], start=True, stop=True),
              waits=tmax(t6, io.get(f"ps{kA}_free")), sem="T")
    tB = p.op("tensor", lambda e: e.matmul(psB[:], sb.ones_row_f[:], sb.row_b[:], start=True, stop=True),
              waits=tmax(t7, io.get(f"ps{kB}_free")), sem="T")
    last = None
    for c in range(ND):
        tm_ = p.op("vector", (lambda c=c: lambda e: e.tensor_mul(sb.ln_scr[:], xcv[:, c, :], psA[:]))(),
                   waits=tmax(tA, xc_tok, io.get("ln_scr_free")) if c == 0 else (io.get("ln_scr_free"),),
                   sem="V")
        last = p.op("vector", (lambda c=c: lambda e: e.tensor_add(
            xhatb_v[:, c, :], sb.ln_scr[:], psB[:]))(),
            waits=tmax(tm_, (tB,) if c == 0 else None), sem="V")
        io["ln_scr_free"] = last
    io[f"ps{kS1}_free"] = t1
    io[f"ps{kS2}_free"] = t2
    io[f"ps{kA}_free"] = last
    io[f"ps{kB}_free"] = last
    return last


def emit_B(nc, p, palloc, PS, dr):
    xin = dr["xin"]
    xout = dr["xout"]

    with _nullctx():
        sb = _SB()
        def alloc(name, shape, dt=F32):
            t = palloc(name, shape, dt)
            setattr(sb, name, t)
            return t
        alloc("xc", [128, ND, SLAB])
        alloc("xhatb", [128, ND, SLAB], BF16)
        alloc("xhatw", [128, ND, SLAB], BF16)
        alloc("out_acc", [128, ND, SLAB])
        alloc("xb", [128, ND, TC], BF16)
        alloc("sq", [128, ND, TC], BF16)
        alloc("ln_scr", [128, TC])
        for r in ("row_nm", "row_t1", "row_t2", "row_a", "row_b"):
            alloc(r, [1, TC])
        alloc("wg_bf", [128, ND, 8], BF16)
        alloc("bg_bc", [128, 8])
        alloc("rsc", [128, 48])
        alloc("w_T8", [8, SLAB])
        alloc("wbc", [128, SLAB])
        alloc("W1b0", [128, ND, HB], BF16); alloc("W1b1", [128, ND, HB], BF16)
        alloc("W2b0", [128, ND, HB], BF16); alloc("W2b1", [128, ND, HB], BF16)
        alloc("W3b0", [128, NHC, D], BF16); alloc("W3b1", [128, NHC, D], BF16)
        alloc("sg0", [128, TC], BF16); alloc("sg1", [128, TC], BF16)
        alloc("h0", [128, NHC, TC], BF16); alloc("h1", [128, NHC, TC], BF16)
        alloc("identf", [128, 128])
        alloc("E8", [8, 1024])
        alloc("ones_col_bf", [128, 1], BF16)
        alloc("ones_row_f", [1, 128])
        alloc("cst", [128, 8])


        wtoks = []
        for nm_ in ("identf", "E8", "ones_col_bf", "ones_row_f", "cst", "bgrow_"):
            pass
        for nm_ in ("identf", "E8", "ones_col_bf", "ones_row_f", "cst"):
            wtoks.append(p.dma("sync", (lambda nm_=nm_: lambda e: e.dma_start(
                out=getattr(sb, nm_)[:], in_=dr[nm_]))(), sem="wld"))
        wtoks.append(p.dma("gpsimd", lambda e: e.dma_start(
            out=sb.wg_bf[:], in_=dr["wgT"].rearrange("(c p) e -> p c e", p=128)), sem="wldg"))
        wtok = tmax(*wtoks)
        tbg0 = p.dma("sync", lambda e: e.dma_start(out=sb.rsc[0:1, 0:8], in_=dr["bgrow"]), sem="wld")
        tbgm = p.op("tensor", lambda e: e.matmul(PS[6][:, 0:8], sb.ones_row_f[:], sb.rsc[0:1, 0:8],
                                                 start=True, stop=True),
                    waits=tmax(wtok, tbg0), sem="T")
        tbg = p.op("vector", lambda e: e.tensor_copy(sb.bg_bc[:], PS[6][:, 0:8]), waits=(tbgm,), sem="V")
        wtok = tmax(wtok, tbg)
        io = {"ps6_free": tbg}

        store_toks = []
        xc_consumed = ()
        acc_consumed = ()
        wslot_free = {0: (), 1: ()}
        for s in range(NSLAB):
            ldx = p.dma("sync", (lambda s=s: lambda e: e.dma_start(
                out=sb.xc[:], in_=xin[:, s * SLAB:(s + 1) * SLAB].rearrange("(c p) t -> p c t", p=128)))(),
                waits=xc_consumed, sem="xld")
            xc_tok = tmax(ldx, wtok)
            t_ms = p.op("vector", lambda e: e.memset(sb.out_acc[:].rearrange("p c t -> p (c t)"), 0.0),
                        waits=acc_consumed, sem="V")

            ln_last = None
            for q in range(SLAB // TC):
                ln_last = emit_norm2(p, sb, io, PS,
                                     sb.xc[:, :, q * TC:(q + 1) * TC], xc_tok,
                                     sb.xhatb[:, :, q * TC:(q + 1) * TC])

            wT_toks = []
            for tt in range(SLAB // 128):
                tlg = None
                for c in range(ND):
                    sp = (c == ND - 1)
                    tlg = p.op("tensor", (lambda c=c, tt=tt, sp=sp: lambda e: e.matmul(
                        PS[4][:, 0:8], sb.xhatb[:, c, tt * 128:(tt + 1) * 128], sb.wg_bf[:, c, :],
                        start=(c == 0), stop=sp))(),
                        waits=tmax(ln_last, io.get("ps4_free")) if c == 0 else (),
                        sem="T" if sp else None)
                lg = sb.rsc[:, 0:8]; msk = sb.rsc[:, 8:16]; tmp = sb.rsc[:, 16:24]
                e_ = sb.rsc[:, 24:32]; wraw = sb.rsc[:, 32:40]
                m1 = sb.rsc[:, 40:41]; m1n = sb.rsc[:, 41:42]; m2 = sb.rsc[:, 42:43]
                den = sb.rsc[:, 43:44]; dinv = sb.rsc[:, 44:45]; wt = sb.rsc[:, 0:8]
                ta = p.op("vector", (lambda lg=lg: lambda e: e.tensor_add(lg, PS[4][:, 0:8], sb.bg_bc[:]))(),
                          waits=tmax(tlg, io.get("rsc_free")), sem="V")
                io["ps4_free"] = ta
                tb = p.op("vector", (lambda: lambda e: e.reduce_max(m1, lg, axis=AX.X))(),
                          waits=(ta,), sem="V")
                tc_ = p.op("vector", (lambda: lambda e: e.tensor_single_scalar(msk, lg, m1[:, 0:1], ALU.is_equal))(),
                           waits=(tb,), sem="V")
                td = p.op("vector", (lambda: lambda e: e.tensor_scalar_mul(m1n, m1, -1.0))(),
                          waits=(tb,), sem="V")
                te_ = p.op("vector", (lambda: lambda e: e.scalar_tensor_tensor(
                    tmp, msk, -1e9, lg, ALU.mult, ALU.add))(), waits=(tc_,), sem="V")
                tf = p.op("vector", (lambda: lambda e: e.reduce_max(m2, tmp, axis=AX.X))(),
                          waits=(te_,), sem="V")
                tg = p.op("scalar", (lambda: lambda e: e.activation(e_, lg, AF.Exp, bias=m1n[:, 0:1]))(),
                          waits=(td, te_), sem="S")
                th = p.op("vector", (lambda: lambda e: e.tensor_single_scalar(msk, lg, m2[:, 0:1], ALU.is_ge))(),
                          waits=(tf, tg), sem="V")
                ti = p.op("vector", (lambda: lambda e: e.tensor_mul(wraw, e_, msk))(),
                          waits=(th, tg), sem="V")
                tj = p.op("vector", (lambda: lambda e: e.reduce_sum(den, wraw, axis=AX.X))(),
                          waits=(ti,), sem="V")
                tk = p.op("vector", (lambda: lambda e: e.reciprocal(dinv, den))(),
                          waits=(tj,), sem="V")
                tl = p.op("vector", (lambda: lambda e: e.tensor_single_scalar(wt, wraw, dinv[:, 0:1], ALU.mult))(),
                          waits=(tk,), sem="V")
                ttr = p.op("tensor", (lambda wt=wt: lambda e: e.transpose(
                    PS[5][0:8, 0:128], wt, sb.identf[:]))(),
                    waits=tmax(tl, io.get("ps5_free")), sem="T")
                tcp = p.op("vector", (lambda tt=tt: lambda e: e.tensor_copy(
                    sb.w_T8[:, tt * 128:(tt + 1) * 128], PS[5][0:8, 0:128]))(),
                    waits=(ttr,), sem="V")
                io["ps5_free"] = tcp
                io["rsc_free"] = ttr
                wT_toks.append(tcp)
            wT_tok = tmax(*wT_toks)

            for ei in range(8):
                wbc_toks = []
                for q in range(SLAB // TC):
                    tw1 = p.op("tensor", (lambda ei=ei, q=q: lambda e: e.matmul(
                        PS[6][:], sb.E8[0:8, ei * 128:(ei + 1) * 128],
                        sb.w_T8[0:8, q * TC:(q + 1) * TC], start=True, stop=True))(),
                        waits=tmax(wT_tok, io.get("ps6_free")), sem="T")
                    tw2 = p.op("vector", (lambda q=q: lambda e: e.tensor_copy(
                        sb.wbc[:, q * TC:(q + 1) * TC], PS[6][:]))(),
                        waits=tmax(tw1, io.get("wbc_free")), sem="V")
                    io["ps6_free"] = tw2
                    wbc_toks.append(tw2)
                twb = tmax(*wbc_toks)
                txw_toks = []
                for c in range(ND):
                    txw = p.op("vector", (lambda c=c: lambda e: e.tensor_mul(
                        sb.xhatw[:, c, :], sb.xhatb[:, c, :], sb.wbc[:]))(),
                        waits=tmax(twb, io.get("xhatw_free")) if c == 0 else (), sem="V")
                    txw_toks.append(txw)
                txw_tok = txw_toks[-1]

                for hb in range(NHB):
                    sl = hb % 2
                    W1b = sb.W1b0 if sl == 0 else sb.W1b1
                    W2b = sb.W2b0 if sl == 0 else sb.W2b1
                    W3b = sb.W3b0 if sl == 0 else sb.W3b1
                    t_w1 = p.dma("gpsimd", (lambda ei=ei, hb=hb, W1b=W1b: lambda e: e.dma_start(
                        out=W1b[:], in_=dr["W1T"][ei, :, hb * HB:(hb + 1) * HB]
                        .rearrange("(c p) h -> p c h", p=128)))(),
                        waits=wslot_free[sl], sem=f"w1ld{sl}")
                    t_w2 = p.dma("gpsimd", (lambda ei=ei, hb=hb, W2b=W2b: lambda e: e.dma_start(
                        out=W2b[:], in_=dr["W2T"][ei, :, hb * HB:(hb + 1) * HB]
                        .rearrange("(c p) h -> p c h", p=128)))(),
                        sem=f"w2ld{sl}")
                    t_w3 = p.dma("gpsimd", (lambda ei=ei, hb=hb, W3b=W3b: lambda e: e.dma_start(
                        out=W3b[:], in_=dr["W3T"][ei, hb * HB:(hb + 1) * HB, :]
                        .rearrange("(c p) o -> p c o", p=128)))(),
                        sem=f"w3ld{sl}")
                    wrd = []
                    for tcq in range(SLAB // TC):
                        h_sl = tcq % 2
                        hbuf = sb.h0 if h_sl == 0 else sb.h1
                        h_toks = []
                        for hc in range(NHC):
                            gr = hc % 2
                            ps_g = PS[0 + gr]
                            ps_u = PS[2 + gr]
                            sg = sb.sg0 if gr == 0 else sb.sg1
                            tgm = None
                            for c in range(ND):
                                sp = (c == ND - 1)
                                tgm = p.op("tensor", (lambda c=c, hc=hc, tcq=tcq, sp=sp, ps_g=ps_g, W1b=W1b: lambda e: e.matmul(
                                    ps_g[:], W1b[:, c, hc * 128:(hc + 1) * 128],
                                    sb.xhatb[:, c, tcq * TC:(tcq + 1) * TC], start=(c == 0), stop=sp))(),
                                    waits=tmax(t_w1, ln_last, io.get(f"ps{0 + gr}_free")) if c == 0 else (),
                                    sem="T" if sp else None)
                            tsl = p.op("scalar", (lambda sg=sg, ps_g=ps_g: lambda e: e.activation(
                                sg[:], ps_g[:], AF.Silu))(),
                                waits=tmax(tgm, io.get(f"sg{gr}_free")), sem="S")
                            io[f"ps{0 + gr}_free"] = tsl
                            tum = None
                            for c in range(ND):
                                sp = (c == ND - 1)
                                tum = p.op("tensor", (lambda c=c, hc=hc, tcq=tcq, sp=sp, ps_u=ps_u, W2b=W2b: lambda e: e.matmul(
                                    ps_u[:], W2b[:, c, hc * 128:(hc + 1) * 128],
                                    sb.xhatw[:, c, tcq * TC:(tcq + 1) * TC], start=(c == 0), stop=sp))(),
                                    waits=tmax(t_w2, txw_tok, io.get(f"ps{2 + gr}_free")) if c == 0 else (),
                                    sem="T" if sp else None)
                            wrd.append(tgm); wrd.append(tum)
                            thm = p.op("vector", (lambda hbuf=hbuf, hc=hc, sg=sg, ps_u=ps_u: lambda e: e.tensor_mul(
                                hbuf[:, hc, :], sg[:], ps_u[:]))(),
                                waits=tmax(tsl, tum, io.get(f"h{h_sl}_free")), sem="V")
                            io[f"ps{2 + gr}_free"] = thm
                            io[f"sg{gr}_free"] = thm
                            h_toks.append(thm)
                        acc_toks = []
                        for o in range(ND):
                            dr_ = o % 2
                            ps_d = PS[4 + dr_]
                            tdm = None
                            for hc in range(NHC):
                                sp = (hc == NHC - 1)
                                tdm = p.op("tensor", (lambda o=o, hc=hc, sp=sp, ps_d=ps_d, W3b=W3b, hbuf=hbuf: lambda e: e.matmul(
                                    ps_d[:], W3b[:, hc, o * 128:(o + 1) * 128], hbuf[:, hc, :],
                                    start=(hc == 0), stop=sp))(),
                                    waits=tmax(t_w3, h_toks[-1], io.get(f"ps{4 + dr_}_free")) if hc == 0 else (),
                                    sem="T" if sp else None)
                            tac = p.op("vector", (lambda o=o, tcq=tcq, ps_d=ps_d: lambda e: e.tensor_add(
                                sb.out_acc[:, o, tcq * TC:(tcq + 1) * TC],
                                sb.out_acc[:, o, tcq * TC:(tcq + 1) * TC], ps_d[:]))(),
                                waits=tmax(tdm, t_ms), sem="V")
                            io[f"ps{4 + dr_}_free"] = tac
                            wrd.append(tdm)
                            acc_toks.append(tac)
                        io[f"h{h_sl}_free"] = acc_toks[-1]
                    wslot_free[sl] = tmax(*wrd)
                io["xhatw_free"] = tmax(wslot_free[0], wslot_free[1])
                io["wbc_free"] = tmax(wslot_free[0], wslot_free[1])

            res_toks = []
            last_acc = io.get("ps4_free"), io.get("ps5_free")
            for o in range(ND):
                tr = p.op("vector", (lambda o=o: lambda e: e.tensor_add(
                    sb.xc[:, o, :], sb.xc[:, o, :], sb.out_acc[:, o, :]))(),
                    waits=tmax(*last_acc) if o == 0 else (), sem="V")
                res_toks.append(tr)
            tst_d = p.dma("sync", (lambda s=s: lambda e: e.dma_start(
                out=xout[:, s * SLAB:(s + 1) * SLAB].rearrange("(c p) t -> p c t", p=128),
                in_=sb.xc[:]))(), waits=tuple(res_toks), sem="xst")
            store_toks.append(tst_d)
            xc_consumed = tmax(tst_d, *res_toks)
            acc_consumed = tmax(*res_toks)

        p.op("sync", lambda e: None, waits=tmax(*store_toks))
    return None



_B_DINS = [("wgT", [D, 8], F32), ("bgrow", [1, 8], F32), ("W1T", [8, D, 4096], F32),
           ("W2T", [8, D, 4096], F32), ("W3T", [8, 4096, D], F32),
           ("identf", [128, 128], F32), ("E8", [8, 1024], F32),
           ("ones_col_bf", [128, 1], BF16), ("ones_row_f", [1, 128], F32),
           ("cst", [128, 8], F32)]


def build_launchB():
    return _standalone(emit_B, _B_DINS)

def _prep_B(inputs, b, xin, cache={}):
    c = _CONSTS
    key = id(inputs.get("L0_ffn_Wgate"))
    if cache.get("key") != key:
        E8 = np.zeros((8, 1024), np.float32)
        for ei in range(8):
            E8[ei, ei * 128:(ei + 1) * 128] = 1.0
        cache.clear()
        cache.update(
            key=key,
            wgT=np.ascontiguousarray(np.asarray(inputs["L0_ffn_wg"]).T),
            bgrow=np.ascontiguousarray(np.asarray(inputs["L0_ffn_bg"]).reshape(1, 8)),
            W1T=np.ascontiguousarray(np.asarray(inputs["L0_ffn_Wgate"]).transpose(0, 2, 1)),
            W2T=np.ascontiguousarray(np.asarray(inputs["L0_ffn_Wup"]).transpose(0, 2, 1)),
            W3T=np.ascontiguousarray(np.asarray(inputs["L0_ffn_Wdown"]).transpose(0, 2, 1)),
            identf=np.eye(128, dtype=np.float32), E8=E8)
    return {
        "xin": xin,
        "wgT": cache["wgT"], "bgrow": cache["bgrow"],
        "W1T": cache["W1T"], "W2T": cache["W2T"], "W3T": cache["W3T"],
        "identf": cache["identf"], "E8": cache["E8"],
        "ones_col_bf": c["ones_col_bf"], "ones_row_f": c["ones_row_f"], "cst": c["cst"],
    }



NKBMAX = T // 128


def emit_C(nc, p, palloc, PS, dr):
    xin = dr["xin"]
    xout = dr["xout"]
    KTd = dr["KTd"]

    with _nullctx():
        sb = _SB()
        def alloc(name, shape, dt=F32):
            t = palloc(name, shape, dt)
            setattr(sb, name, t)
            return t
        alloc("xc", [128, ND, TC])
        alloc("xb", [128, ND, TC], BF16)
        alloc("sq", [128, ND, TC], BF16)
        alloc("ln_scr", [128, TC])
        for r in ("row_nm", "row_t1", "row_t2", "row_a", "row_b"):
            alloc(r, [1, TC])
        alloc("xhatb", [128, ND, TC], BF16)
        alloc("kt0", [128, TC], BF16); alloc("kt1", [128, TC], BF16)
        alloc("kbuf", [128, T], BF16)
        alloc("vtok", [128, NKBMAX, 16, 65], BF16)
        alloc("QT", [128, ND, TC], BF16)
        alloc("att", [128, ND, TC], BF16)
        alloc("qsb", [128, TC], BF16)
        alloc("t1s", [128, TC], BF16)
        alloc("t2s", [128, TC], BF16)
        alloc("sqk", [128, TC], BF16)
        alloc("mk2", [1, 16]); alloc("mneg", [1, 16])
        alloc("q2", [1, TC]); alloc("mrow", [1, TC])
        alloc("mcol", [128, 16])
        alloc("linv", [1, TC]); alloc("lbc", [64, TC])
        alloc("Pt0", [128, 2 * TC], BF16); alloc("Pt1", [128, 2 * TC], BF16)
        alloc("wk0", [128, ND, 128], BF16); alloc("wk1", [128, ND, 128], BF16)
        alloc("wv0", [128, ND, TC], BF16)
        alloc("wo0", [128, ND, 128], BF16); alloc("wo1", [128, ND, 128], BF16)
        alloc("cos_bc", [128, T], BF16)
        alloc("sin_sgn", [128, T], BF16)
        alloc("Rswap", [128, 128], BF16)
        alloc("mask01", [128, 4 * TC], BF16)
        alloc("ones_col_bf", [128, 1], BF16)
        alloc("ones_row_f", [1, 128])
        alloc("cst", [128, 8])

        PSA = [PS[0], PS[1], PS[4], PS[5]]
        PS6 = PS[6]
        PS7 = PS[7]
        psl = {6: PS6, 7: PS7, 4: PS[4], 5: PS[5]}

        wtoks = []
        for nm_ in ("cos_bc", "sin_sgn", "Rswap", "mask01", "ones_col_bf", "ones_row_f", "cst"):
            wtoks.append(p.dma("sync", (lambda nm_=nm_: lambda e: e.dma_start(
                out=getattr(sb, nm_)[:], in_=dr[nm_]))(), sem="wld"))
        wtok = tmax(*wtoks)
        t_mk = p.op("vector", lambda e: e.memset(sb.mk2[:], 0.0), sem="V")
        t_vo = p.op("vector", lambda e: e.memset(sb.vtok[:, :, :, 64:65], 1.0), sem="V")
        io = {}

        def emit_rope(dst_view, src_ps, cos_v, sin_v, rot_ps, rot_key, scale, waits_in,
                      dst_war=()):
            tev = p.op("scalar", (lambda: lambda e: e.activation(
                sb.qsb[:], src_ps[:], AF.Copy, scale=scale))(),
                waits=tmax(waits_in, io.get("qsb_free")), sem="S")
            trm = p.op("tensor", (lambda: lambda e: e.matmul(
                rot_ps[:], sb.Rswap[:], sb.qsb[:], start=True, stop=True))(),
                waits=tmax(tev, io.get(rot_key)), sem="T")
            tp1 = p.op("gpsimd", (lambda: lambda e: e.tensor_tensor(
                sb.t1s[:], sb.qsb[:], cos_v, ALU.mult))(),
                waits=tmax(tev, io.get("t1s_free")), sem="G")
            tp2 = p.op("vector", (lambda: lambda e: e.tensor_tensor(
                sb.t2s[:], rot_ps[:], sin_v, ALU.mult))(),
                waits=tmax(trm, io.get("t2s_free")), sem="V")
            io[rot_key] = tp2
            tp3 = p.op("vector", (lambda dst_view=dst_view: lambda e: e.tensor_add(
                dst_view, sb.t1s[:], sb.t2s[:]))(), waits=tmax(tp1, tp2, dst_war), sem="V")
            io["qsb_free"] = tp1
            io["t1s_free"] = tp3
            io["t2s_free"] = tp3
            return tp3, tev

        wkslot = {0: (), 1: ()}
        wvslot = {0: (), 1: ()}
        ktslot = {0: (), 1: ()}
        xc_consumed = ()
        vtok_done = []
        for i in range(NTC):
            tsl = (i * TC, (i + 1) * TC)
            ldx = p.dma("sync", (lambda i=i: lambda e: e.dma_start(
                out=sb.xc[:], in_=xin[:, i * TC:(i + 1) * TC].rearrange("(c p) t -> p c t", p=128)))(),
                waits=xc_consumed, sem="xld")
            xc_tok = tmax(ldx, wtok)
            ln_last = emit_norm2(p, sb, io, psl, sb.xc[:], xc_tok, sb.xhatb[:])

            for rc in range(ND):
                sl = rc % 2
                wk = sb.wk0 if sl == 0 else sb.wk1
                kt = sb.kt0 if sl == 0 else sb.kt1
                twk = p.dma("gpsimd", (lambda rc=rc, wk=wk: lambda e: e.dma_start(
                    out=wk[:], in_=dr["qkvT"][:, D + rc * 128:D + (rc + 1) * 128]
                    .rearrange("(c p) o -> p c o", p=128)))(),
                    waits=wkslot[sl], sem=f"wkld{sl}")
                ps_k = PSA[sl]
                tkm = None
                for c in range(ND):
                    sp = (c == ND - 1)
                    tkm = p.op("tensor", (lambda c=c, sp=sp, wk=wk, ps_k=ps_k: lambda e: e.matmul(
                        ps_k[:], wk[:, c, :], sb.xhatb[:, c, :], start=(c == 0), stop=sp))(),
                        waits=tmax(twk, ln_last, io.get(f"pa{sl}")) if c == 0 else (),
                        sem="T" if sp else None)
                wkslot[sl] = (tkm,)
                rope_tok, tev = emit_rope(kt[:], ps_k,
                                          sb.cos_bc[:, tsl[0]:tsl[1]], sb.sin_sgn[:, tsl[0]:tsl[1]],
                                          PSA[2 + sl], f"ps{4 + sl}_free", 0.125,
                                          (tkm,), dst_war=ktslot[sl])
                io[f"pa{sl}"] = tev
                tkst = p.dma("sync", (lambda rc=rc, i=i, kt=kt: lambda e: e.dma_start(
                    out=KTd[rc * 128:(rc + 1) * 128, i * TC:(i + 1) * TC], in_=kt[:]))(),
                    waits=(rope_tok,), sem=f"ktst{sl}")
                ktd_last = tkst
                tsq = p.op("scalar", (lambda kt=kt: lambda e: e.activation(
                    sb.sqk[:], kt[:], AF.Square))(),
                    waits=tmax(rope_tok, io.get("sqk_free")), sem="S")
                ktslot[sl] = tmax(tkst, tsq)
                for hh in range(2):
                    base = hh * 64
                    psn = PS6 if hh == 0 else PS7
                    rowt = sb.row_t1 if hh == 0 else sb.row_t2
                    tnm = p.op("tensor", (lambda base=base, psn=psn: lambda e: e.matmul(
                        psn[0:1, :], sb.ones_col_bf[base:base + 64, :], sb.sqk[base:base + 64, :],
                        start=True, stop=True))(),
                        waits=tmax(tsq, io.get(f"ps{6 + hh}_free")), sem="T")
                    trm_ = p.op("vector", (lambda psn=psn, rowt=rowt: lambda e: e.reduce_max(
                        rowt[0:1, 0:1], psn[0:1, :], axis=AX.X))(), waits=(tnm,), sem="V")
                    io[f"ps{6 + hh}_free"] = trm_
                    h = 2 * rc + hh
                    tmk = p.op("vector", (lambda h=h, rowt=rowt: lambda e: e.tensor_tensor(
                        sb.mk2[0:1, h:h + 1], sb.mk2[0:1, h:h + 1], rowt[0:1, 0:1], ALU.max))(),
                        waits=(trm_, t_mk), sem="V")
                    io["sqk_free"] = tmk

            for vc in range(2):
                wv = sb.wv0
                twv = p.dma("gpsimd", (lambda vc=vc, wv=wv: lambda e: e.dma_start(
                    out=wv[:], in_=dr["qkvT"][:, 2 * D + vc * TC:2 * D + (vc + 1) * TC]
                    .rearrange("(c p) o -> p c o", p=128)))(),
                    waits=tmax(wvslot[0], wvslot[1]), sem="wvld")
                for ttt in range(TC // 128):
                    ps_v = PSA[ttt % 2]
                    tvm = None
                    for c in range(ND):
                        sp = (c == ND - 1)
                        tvm = p.op("tensor", (lambda c=c, ttt=ttt, sp=sp, wv=wv, ps_v=ps_v: lambda e: e.matmul(
                            ps_v[:], sb.xhatb[:, c, ttt * 128:(ttt + 1) * 128], wv[:, c, :],
                            start=(c == 0), stop=sp))(),
                            waits=tmax(twv, ln_last, io.get(f"pa{ttt % 2}")) if c == 0 else (),
                            sem="T" if sp else None)
                    wvslot[ttt % 2] = (tvm,)
                    kb = i * 4 + ttt
                    tev2 = None
                    for hh in range(8):
                        h = vc * 8 + hh
                        tev2 = p.op("scalar", (lambda kb=kb, h=h, hh=hh, ps_v=ps_v: lambda e: e.activation(
                            sb.vtok[:, kb, h, 0:64], ps_v[:, hh * 64:(hh + 1) * 64], AF.Copy))(),
                            waits=(tvm,) if hh == 0 else (), sem="S")
                    io[f"pa{ttt % 2}"] = tev2
                    vtok_done.append(tev2)
            xc_consumed = (ln_last,)
        vtok_tok = tmax(*vtok_done, t_vo)
        kt_tok = io.get("sqk_free")
        ktd_tok = tmax(("ktst0", p.counts.get("ktst0", 0)), ("ktst1", p.counts.get("ktst1", 0)))

        store_toks = []
        kbslot = {0: (), 1: ()}
        for qc in range(NTC):
            tsl = (qc * TC, (qc + 1) * TC)
            ldx = p.dma("sync", (lambda qc=qc: lambda e: e.dma_start(
                out=sb.xc[:], in_=xin[:, qc * TC:(qc + 1) * TC].rearrange("(c p) t -> p c t", p=128)))(),
                waits=xc_consumed, sem="xld")
            xc_tok = tmax(ldx, wtok)
            ln_last = emit_norm2(p, sb, io, psl, sb.xc[:], xc_tok, sb.xhatb[:])

            q_toks = []
            for rc in range(ND):
                sl = rc % 2
                wk = sb.wk0 if sl == 0 else sb.wk1
                twk = p.dma("gpsimd", (lambda rc=rc, wk=wk: lambda e: e.dma_start(
                    out=wk[:], in_=dr["qkvT"][:, rc * 128:(rc + 1) * 128]
                    .rearrange("(c p) o -> p c o", p=128)))(),
                    waits=wkslot[sl], sem=f"wkld{sl}")
                ps_k = PSA[sl]
                tkm = None
                for c in range(ND):
                    sp = (c == ND - 1)
                    tkm = p.op("tensor", (lambda c=c, sp=sp, wk=wk, ps_k=ps_k: lambda e: e.matmul(
                        ps_k[:], wk[:, c, :], sb.xhatb[:, c, :], start=(c == 0), stop=sp))(),
                        waits=tmax(twk, ln_last, io.get(f"pa{sl}")) if c == 0 else (),
                        sem="T" if sp else None)
                wkslot[sl] = (tkm,)
                rope_tok, tev = emit_rope(sb.QT[:, rc, :], ps_k,
                                          sb.cos_bc[:, tsl[0]:tsl[1]], sb.sin_sgn[:, tsl[0]:tsl[1]],
                                          PSA[2 + sl], f"ps{4 + sl}_free", 1.0, (tkm,))
                io[f"pa{sl}"] = tev
                q_toks.append(rope_tok)
            q_tok = tmax(*q_toks)

            att_toks = []
            nkb = (qc + 1) * 4
            mtoks = {}
            for h in range(16):
                rc, hh = h // 2, h % 2
                base = hh * 64
                tsq = p.op("scalar", (lambda rc=rc, base=base: lambda e: e.activation(
                    sb.sqk[0:64, :], sb.QT[base:base + 64, rc, :], AF.Square))(),
                    waits=tmax(q_tok, io.get("sqk_free")), sem="S")
                tq2 = p.op("tensor", (lambda: lambda e: e.matmul(
                    PS[4][0:1, :], sb.ones_col_bf[0:64, :], sb.sqk[0:64, :], start=True, stop=True))(),
                    waits=tmax(tsq, io.get("ps4_free")), sem="T")
                io["sqk_free"] = tq2
                tm1 = p.op("vector", (lambda h=h: lambda e: e.tensor_single_scalar(
                    sb.q2[:], PS[4][0:1, :], sb.mk2[0:1, h:h + 1], ALU.mult))(),
                    waits=tmax(tq2, kt_tok), sem="V")
                io["ps4_free"] = tm1
                tm2 = p.op("scalar", (lambda: lambda e: e.activation(
                    sb.mrow[:], sb.q2[:], AF.Ln))(), waits=(tm1,), sem="S")
                tm3 = p.op("scalar", (lambda: lambda e: e.activation(
                    sb.mrow[:], sb.mrow[:], AF.Exp, scale=0.5))(), waits=(tm2,), sem="S")
                tm4 = p.op("vector", (lambda h=h: lambda e: e.reduce_max(
                    sb.mneg[0:1, h:h + 1], sb.mrow[:], axis=AX.X, negate=True))(),
                    waits=tmax(tm3, io.get("mneg_free")), sem="V")
                mtoks[h] = tm4
            mtoksp = {}
            for hp in range(8):
                h0, h1 = 2 * hp, 2 * hp + 1
                tpm = p.op("vector", (lambda h0=h0, h1=h1: lambda e: e.tensor_tensor(
                    sb.mneg[0:1, h0:h0 + 1], sb.mneg[0:1, h0:h0 + 1],
                    sb.mneg[0:1, h1:h1 + 1], ALU.min))(),
                    waits=tmax(mtoks[h0], mtoks[h1]), sem="V")
                tm5 = p.op("tensor", (lambda h0=h0: lambda e: e.matmul(
                    PS[5][:, 0:1], sb.ones_row_f[:], sb.mneg[0:1, h0:h0 + 1],
                    start=True, stop=True))(),
                    waits=tmax(tpm, io.get("ps5_free")), sem="T")
                tm6 = p.op("vector", (lambda hp=hp: lambda e: e.tensor_copy(
                    sb.mcol[:, hp:hp + 1], PS[5][:, 0:1]))(),
                    waits=tmax(tm5, io.get("mcol_free")), sem="V")
                io["ps5_free"] = tm6
                mtoksp[hp] = tm6
            io["mneg_free"] = None

            for hp in range(8):
                h0, h1 = 2 * hp, 2 * hp + 1
                rc = hp
                tkl = p.dma("sync", (lambda h0=h0, nkb=nkb: lambda e: e.dma_start(
                    out=sb.kbuf[:, 0:nkb * 128],
                    in_=KTd[h0 * 64:h0 * 64 + 128, 0:nkb * 128]))(),
                    waits=tmax(kbslot[0], ktd_tok if qc == 0 and hp == 0 else None),
                    sem="kbl0")
                kread = []

                def emit_O2(pend_):
                    kb_, pt_, tmask_ = pend_
                    p.op("tensor", (lambda kb_=kb_, h0=h0, pt_=pt_, nkb=nkb: lambda e: e.matmul(
                        PS[0][0:65, :], sb.vtok[:, kb_, h0, :], pt_[:, 0:TC],
                        start=(kb_ == 0), stop=(kb_ == nkb - 1)))(),
                        waits=tmax(tmask_, vtok_tok,
                                   (io.get("pa0") if kb_ == 0 else None),
                                   (io.get("pa1") if kb_ == 0 else None)),
                        sem=None)
                    t_ = p.op("tensor", (lambda kb_=kb_, h1=h1, pt_=pt_, nkb=nkb: lambda e: e.matmul(
                        PS[1][0:65, :], sb.vtok[:, kb_, h1, :], pt_[:, TC:2 * TC],
                        start=(kb_ == 0), stop=(kb_ == nkb - 1)))(),
                        sem="T")
                    io["ptw0" if pt_ is sb.Pt0 else "ptw1"] = t_
                    return t_

                pend_o = None
                for kb in range(nkb):
                    u = kb % 2
                    psw = PS[8 + u]
                    pslo = PS[2 + u] if u == 0 else PS[6]
                    pshi = PS[3] if u == 0 else PS[7]
                    pt = sb.Pt0 if u == 0 else sb.Pt1
                    p.op("tensor", (lambda kb=kb, rc=rc, u=u: lambda e: e.matmul(
                        (PS[2] if u == 0 else PS[6])[:], sb.kbuf[0:64, kb * 128:(kb + 1) * 128],
                        sb.QT[0:64, rc, :], start=True, stop=True))(),
                        waits=tmax(io.get(f"psw{u}_free"), q_tok, tkl), sem=None)
                    tsB = p.op("tensor", (lambda kb=kb, rc=rc, u=u: lambda e: e.matmul(
                        (PS[3] if u == 0 else PS[7])[:], sb.kbuf[64:128, kb * 128:(kb + 1) * 128],
                        sb.QT[64:128, rc, :], start=True, stop=True))(),
                        sem="T")
                    kread.append(tsB)
                    texp = p.op("scalar", (lambda pt=pt, psw=psw, hp=hp: lambda e: e.activation(
                        pt[:], psw, AF.Exp, bias=sb.mcol[:, hp:hp + 1]))(),
                        waits=tmax(tsB, mtoksp[hp], io.get("ptw0" if u == 0 else "ptw1")),
                        sem="S")
                    io[f"psw{u}_free"] = texp
                    tmask = texp
                    off = kb * 128 - qc * TC
                    if 0 <= off < TC:
                        oi = off // 128
                        tmask = p.op("gpsimd", (lambda pt=pt, oi=oi: lambda e: e.tensor_tensor(
                            pt[:, 0:TC], pt[:, 0:TC],
                            sb.mask01[:, oi * TC:(oi + 1) * TC], ALU.mult))(),
                            waits=(texp,), sem="G")
                        tmask = p.op("gpsimd", (lambda pt=pt, oi=oi: lambda e: e.tensor_tensor(
                            pt[:, TC:2 * TC], pt[:, TC:2 * TC],
                            sb.mask01[:, oi * TC:(oi + 1) * TC], ALU.mult))(),
                            waits=(texp,), sem="G")
                    if pend_o is not None:
                        emit_O2(pend_o)
                    pend_o = (kb, pt, tmask)
                emit_O2(pend_o)
                kbslot[0] = tmax(*kread)
                for hx, base_x, psox, lb in ((h0, 0, 0, 4), (h1, 64, 1, 5)):
                    tli = p.op("scalar", (lambda psox=psox: lambda e: e.activation(
                        sb.linv[:], PS[psox][64:65, :], AF.Ln))(),
                        waits=tmax(io.get("ptw0"), io.get("ptw1"), io.get("linv_free")), sem="S")
                    tlb = p.op("tensor", (lambda lb=lb: lambda e: e.matmul(
                        PS[lb][0:64, :], sb.ones_row_f[0:1, 0:64], sb.linv[:],
                        start=True, stop=True))(),
                        waits=tmax(tli, io.get(f"ps{lb}_free")), sem="T")
                    tle = p.op("scalar", (lambda lb=lb: lambda e: e.activation(
                        sb.lbc[:], PS[lb][0:64, :], AF.Exp, scale=-1.0))(),
                        waits=tmax(tlb, io.get("lbc_free")), sem="S")
                    io[f"ps{lb}_free"] = tle
                    tat = p.op("vector", (lambda base_x=base_x, rc=rc, psox=psox: lambda e: e.tensor_mul(
                        sb.att[base_x:base_x + 64, rc, :], PS[psox][0:64, :], sb.lbc[:]))(),
                        waits=(tle,), sem="V")
                    io[f"pa{psox}"] = tat
                    io["lbc_free"] = tat
                    io["linv_free"] = tlb
                    att_toks.append(tat)
            att_tok = tmax(*att_toks)
            io["mcol_free"] = att_tok
            io["mneg_free"] = att_tok
            io["ps6_free"] = tmax(io.get("psw1_free"), io.get("ps6_free"))
            io["ps7_free"] = tmax(io.get("psw1_free"), io.get("ps7_free"))
            io["ps2_free"] = io.get("psw0_free")
            io["ps3_free"] = io.get("psw0_free")

            res_toks = []
            for o in range(ND):
                wsl = o % 2
                wob = sb.wo0 if wsl == 0 else sb.wo1
                two = p.dma("gpsimd", (lambda o=o, wob=wob: lambda e: e.dma_start(
                    out=wob[:], in_=dr["woT"][:, o * 128:(o + 1) * 128]
                    .rearrange("(c p) o2 -> p c o2", p=128)))(),
                    waits=io.get(f"wo{wsl}_rd", ()), sem=f"wold{wsl}")
                ps_o = PS6 if o % 2 == 0 else PS7
                tpo = None
                for c in range(ND):
                    sp = (c == ND - 1)
                    tpo = p.op("tensor", (lambda o=o, c=c, sp=sp, ps_o=ps_o, wob=wob: lambda e: e.matmul(
                        ps_o[:], wob[:, c, :], sb.att[:, c, :],
                        start=(c == 0), stop=sp))(),
                        waits=tmax(att_tok, two, io.get(f"ps{6 + o % 2}_free")) if c == 0 else (),
                        sem="T" if sp else None)
                io[f"wo{wsl}_rd"] = (tpo,)
                tr = p.op("vector", (lambda o=o, ps_o=ps_o: lambda e: e.tensor_add(
                    sb.xc[:, o, :], sb.xc[:, o, :], ps_o[:]))(), waits=(tpo,), sem="V")
                io[f"ps{6 + o % 2}_free"] = tr
                res_toks.append(tr)
            tst_d = p.dma("sync", (lambda qc=qc: lambda e: e.dma_start(
                out=xout[:, qc * TC:(qc + 1) * TC].rearrange("(c p) t -> p c t", p=128),
                in_=sb.xc[:]))(), waits=tuple(res_toks), sem="xst")
            store_toks.append(tst_d)
            xc_consumed = tmax(tst_d, *res_toks)

        p.op("sync", lambda e: None, waits=tmax(*store_toks))
    return None



_C_DINS = [("qkvT", [D, 3 * D], F32), ("woT", [D, D], F32), ("cos_bc", [128, T], BF16),
           ("sin_sgn", [128, T], BF16), ("Rswap", [128, 128], BF16),
           ("mask01", [128, 4 * TC], BF16), ("ones_col_bf", [128, 1], BF16),
           ("ones_row_f", [1, 128], F32), ("cst", [128, 8], F32)]


def build_launchC():
    return _standalone(emit_C, _C_DINS, extra=[("KTd", [D, T], BF16, "Internal")])

def _mask01():
    m = np.zeros((128, 4 * TC), np.float32)
    for oi in range(4):
        off = oi * 128
        for pp in range(128):
            for f in range(TC):
                if off + pp <= f:
                    m[pp, oi * TC + f] = 1.0
    return m.astype(BF)

_MASK01 = None


def _prep_C(inputs, b, xin):
    global _MASK01
    c = _CONSTS
    if _MASK01 is None:
        _MASK01 = _mask01()
    return {
        "xin": xin,
        "qkvT": np.ascontiguousarray(np.asarray(inputs["L1_attn_qkv"]).T),
        "woT": np.ascontiguousarray(np.asarray(inputs["L1_attn_wo"]).T),
        "cos_bc": c["cos_bc"], "sin_sgn": c["sin_sgn"], "Rswap": c["Rswap"],
        "mask01": _MASK01,
        "ones_col_bf": c["ones_col_bf"], "ones_row_f": c["ones_row_f"], "cst": c["cst"],
    }




def emit_D1(nc, p, palloc, PS, dr):
    xin = dr["xin"]
    lnxd = dr["lnxd"]
    hd = dr["hd"]
    KTd = dr["KTd"]

    with _nullctx():
        sb = _SB()
        def alloc(name, shape, dt=F32):
            t = palloc(name, shape, dt)
            setattr(sb, name, t)
            return t
        alloc("xc", [128, ND, TC])
        alloc("lnxb", [128, ND, TC], BF16)
        alloc("xb", [128, ND, TC], BF16)
        alloc("sq", [128, ND, TC], BF16)
        alloc("ln_scr", [128, TC])
        for r in ("row_nm", "row_t1", "row_t2", "row_a", "row_b"):
            alloc(r, [1, TC])
        alloc("xhatb", [128, ND, TC], BF16)
        alloc("kt0", [128, TC], BF16)
        alloc("kbuf", [128, T], BF16)
        alloc("vtok", [128, NKBMAX, 16, 65], BF16)
        alloc("QT", [128, ND, TC], BF16)
        alloc("att", [128, ND, TC], BF16)
        alloc("qsb", [128, TC], BF16)
        alloc("t1s", [128, TC], BF16)
        alloc("t2s", [128, TC], BF16)
        alloc("sqk", [128, TC], BF16)
        alloc("mk2", [1, 16]); alloc("mneg", [1, 16])
        alloc("q2", [1, TC]); alloc("mrow", [1, TC])
        alloc("mcol", [128, 16])
        alloc("linv", [1, TC]); alloc("lbc", [64, TC])
        alloc("Pt0", [128, 2 * TC], BF16); alloc("Pt1", [128, 2 * TC], BF16)
        alloc("wk0", [128, ND, 128], BF16); alloc("wk1", [128, ND, 128], BF16)
        alloc("wv0", [128, ND, TC], BF16)
        alloc("wo0", [128, ND, 128], BF16); alloc("wo1", [128, ND, 128], BF16)
        alloc("sc1", [128, ND])
        alloc("cos_bc", [128, T], BF16)
        alloc("sin_sgn", [128, T], BF16)
        alloc("Rswap", [128, 128], BF16)
        alloc("mask01", [128, 4 * TC], BF16)
        alloc("ones_col_bf", [128, 1], BF16)
        alloc("ones_row_f", [1, 128])
        alloc("cst", [128, 8])

        PSA = [PS[0], PS[1], PS[4], PS[5]]
        PS6 = PS[6]
        PS7 = PS[7]
        psl = {6: PS6, 7: PS7, 4: PS[4], 5: PS[5]}

        wtoks = []
        for nm_ in ("cos_bc", "sin_sgn", "Rswap", "mask01", "ones_col_bf", "ones_row_f",
                    "cst", "sc1"):
            wtoks.append(p.dma("sync", (lambda nm_=nm_: lambda e: e.dma_start(
                out=getattr(sb, nm_)[:], in_=dr[nm_]))(), sem="wld"))
        wtok = tmax(*wtoks)
        t_mk = p.op("vector", lambda e: e.memset(sb.mk2[:], 0.0), sem="V")
        t_vo = p.op("vector", lambda e: e.memset(sb.vtok[:, :, :, 64:65], 1.0), sem="V")
        io = {}

        def emit_rope(dst_view, src_ps, cos_v, sin_v, rot_ps, rot_key, scale, waits_in,
                      dst_war=()):
            tev = p.op("scalar", (lambda: lambda e: e.activation(
                sb.qsb[:], src_ps[:], AF.Copy, scale=scale))(),
                waits=tmax(waits_in, io.get("qsb_free")), sem="S")
            trm = p.op("tensor", (lambda: lambda e: e.matmul(
                rot_ps[:], sb.Rswap[:], sb.qsb[:], start=True, stop=True))(),
                waits=tmax(tev, io.get(rot_key)), sem="T")
            tp1 = p.op("gpsimd", (lambda: lambda e: e.tensor_tensor(
                sb.t1s[:], sb.qsb[:], cos_v, ALU.mult))(),
                waits=tmax(tev, io.get("t1s_free")), sem="G")
            tp2 = p.op("vector", (lambda: lambda e: e.tensor_tensor(
                sb.t2s[:], rot_ps[:], sin_v, ALU.mult))(),
                waits=tmax(trm, io.get("t2s_free")), sem="V")
            io[rot_key] = tp2
            tp3 = p.op("vector", (lambda dst_view=dst_view: lambda e: e.tensor_add(
                dst_view, sb.t1s[:], sb.t2s[:]))(), waits=tmax(tp1, tp2, dst_war), sem="V")
            io["qsb_free"] = tp1
            io["t1s_free"] = tp3
            io["t2s_free"] = tp3
            return tp3, tev

        wkslot = {0: (), 1: ()}
        wvslot = {0: (), 1: ()}
        ktslot = {0: (), 1: ()}
        xc_consumed = ()
        vtok_done = []
        ktd_last = None
        for i in range(NTC):
            tsl = (i * TC, (i + 1) * TC)
            ldx = p.dma("sync", (lambda i=i: lambda e: e.dma_start(
                out=sb.xc[:], in_=xin[:, i * TC:(i + 1) * TC].rearrange("(c p) t -> p c t", p=128)))(),
                waits=xc_consumed, sem="xld")
            xc_tok = tmax(ldx, wtok)
            ln_last = emit_norm2(p, sb, io, psl, sb.xc[:], xc_tok, sb.lnxb[:])
            tlnst = p.dma("sync", (lambda i=i: lambda e: e.dma_start(
                out=lnxd[:, i * TC:(i + 1) * TC].rearrange("(c p) t -> p c t", p=128),
                in_=sb.lnxb[:]))(), waits=(ln_last,), sem="lnst")
            rms_last = emit_norm2(p, sb, io, psl, sb.lnxb[:], (ln_last,), sb.xhatb[:],
                                  eps_col=4, rms=True, scale_col=sb.sc1)
            ln_last = rms_last

            for rc in range(ND):
                sl = rc % 2
                wk = sb.wk0 if sl == 0 else sb.wk1
                kt = sb.kt0
                twk = p.dma("gpsimd", (lambda rc=rc, wk=wk: lambda e: e.dma_start(
                    out=wk[:], in_=dr["wk"][:, rc * 128:(rc + 1) * 128]
                    .rearrange("(c p) o -> p c o", p=128)))(),
                    waits=wkslot[sl], sem=f"wkld{sl}")
                ps_k = PSA[sl]
                tkm = None
                for c in range(ND):
                    sp = (c == ND - 1)
                    tkm = p.op("tensor", (lambda c=c, sp=sp, wk=wk, ps_k=ps_k: lambda e: e.matmul(
                        ps_k[:], wk[:, c, :], sb.xhatb[:, c, :], start=(c == 0), stop=sp))(),
                        waits=tmax(twk, ln_last, io.get(f"pa{sl}")) if c == 0 else (),
                        sem="T" if sp else None)
                wkslot[sl] = (tkm,)
                rope_tok, tev = emit_rope(kt[:], ps_k,
                                          sb.cos_bc[:, tsl[0]:tsl[1]], sb.sin_sgn[:, tsl[0]:tsl[1]],
                                          PSA[2 + sl], f"ps{4 + sl}_free", 0.125,
                                          (tkm,), dst_war=tmax(ktslot[0], ktslot[1]))
                io[f"pa{sl}"] = tev
                tkst = p.dma("sync", (lambda rc=rc, i=i, kt=kt: lambda e: e.dma_start(
                    out=KTd[rc * 128:(rc + 1) * 128, i * TC:(i + 1) * TC], in_=kt[:]))(),
                    waits=(rope_tok,), sem=f"ktst{sl}")
                tsq = p.op("scalar", (lambda kt=kt: lambda e: e.activation(
                    sb.sqk[:], kt[:], AF.Square))(),
                    waits=tmax(rope_tok, io.get("sqk_free")), sem="S")
                ktslot[sl] = tmax(tkst, tsq)
                for hh in range(2):
                    base = hh * 64
                    psn = PS6 if hh == 0 else PS7
                    rowt = sb.row_t1 if hh == 0 else sb.row_t2
                    tnm = p.op("tensor", (lambda base=base, psn=psn: lambda e: e.matmul(
                        psn[0:1, :], sb.ones_col_bf[base:base + 64, :], sb.sqk[base:base + 64, :],
                        start=True, stop=True))(),
                        waits=tmax(tsq, io.get(f"ps{6 + hh}_free")), sem="T")
                    trm_ = p.op("vector", (lambda psn=psn, rowt=rowt: lambda e: e.reduce_max(
                        rowt[0:1, 0:1], psn[0:1, :], axis=AX.X))(), waits=(tnm,), sem="V")
                    io[f"ps{6 + hh}_free"] = trm_
                    h = 2 * rc + hh
                    tmk = p.op("vector", (lambda h=h, rowt=rowt: lambda e: e.tensor_tensor(
                        sb.mk2[0:1, h:h + 1], sb.mk2[0:1, h:h + 1], rowt[0:1, 0:1], ALU.max))(),
                        waits=(trm_, t_mk), sem="V")
                    io["sqk_free"] = tmk

            for vc in range(2):
                wv = sb.wv0
                twv = p.dma("gpsimd", (lambda vc=vc, wv=wv: lambda e: e.dma_start(
                    out=wv[:], in_=dr["wv"][:, vc * TC:(vc + 1) * TC]
                    .rearrange("(c p) o -> p c o", p=128)))(),
                    waits=tmax(wvslot[0], wvslot[1]), sem="wvld")
                for ttt in range(TC // 128):
                    ps_v = PSA[ttt % 2]
                    tvm = None
                    for c in range(ND):
                        sp = (c == ND - 1)
                        tvm = p.op("tensor", (lambda c=c, ttt=ttt, sp=sp, wv=wv, ps_v=ps_v: lambda e: e.matmul(
                            ps_v[:], sb.xhatb[:, c, ttt * 128:(ttt + 1) * 128], wv[:, c, :],
                            start=(c == 0), stop=sp))(),
                            waits=tmax(twv, ln_last, io.get(f"pa{ttt % 2}")) if c == 0 else (),
                            sem="T" if sp else None)
                    wvslot[ttt % 2] = (tvm,)
                    kb = i * 4 + ttt
                    tev2 = None
                    for hh in range(8):
                        h = vc * 8 + hh
                        tev2 = p.op("scalar", (lambda kb=kb, h=h, hh=hh, ps_v=ps_v: lambda e: e.activation(
                            sb.vtok[:, kb, h, 0:64], ps_v[:, hh * 64:(hh + 1) * 64], AF.Copy))(),
                            waits=(tvm,) if hh == 0 else (), sem="S")
                    io[f"pa{ttt % 2}"] = tev2
                    vtok_done.append(tev2)
            xc_consumed = tmax(ln_last, tlnst)
        vtok_tok = tmax(*vtok_done, t_vo)
        kt_tok = io.get("sqk_free")
        ktd_tok = tmax(("ktst0", p.counts.get("ktst0", 0)), ("ktst1", p.counts.get("ktst1", 0)))
        lnx_tok = ("lnst", p.counts.get("lnst", 0))

        store_toks = []
        kbslot = {0: (), 1: ()}
        lnx_consumed = ()
        for qc in range(NTC):
            tsl = (qc * TC, (qc + 1) * TC)
            ldl = p.dma("sync", (lambda qc=qc: lambda e: e.dma_start(
                out=sb.lnxb[:], in_=lnxd[:, qc * TC:(qc + 1) * TC]
                .rearrange("(c p) t -> p c t", p=128)))(),
                waits=tmax(lnx_consumed, lnx_tok, xc_consumed), sem="lnld")
            lnx_tok2 = tmax(ldl, wtok)
            ln_last = emit_norm2(p, sb, io, psl, sb.lnxb[:], lnx_tok2, sb.xhatb[:],
                                 eps_col=4, rms=True, scale_col=sb.sc1)

            q_toks = []
            for rc in range(ND):
                sl = rc % 2
                wk = sb.wk0 if sl == 0 else sb.wk1
                twk = p.dma("gpsimd", (lambda rc=rc, wk=wk: lambda e: e.dma_start(
                    out=wk[:], in_=dr["wq"][:, rc * 128:(rc + 1) * 128]
                    .rearrange("(c p) o -> p c o", p=128)))(),
                    waits=wkslot[sl], sem=f"wkld{sl}")
                ps_k = PSA[sl]
                tkm = None
                for c in range(ND):
                    sp = (c == ND - 1)
                    tkm = p.op("tensor", (lambda c=c, sp=sp, wk=wk, ps_k=ps_k: lambda e: e.matmul(
                        ps_k[:], wk[:, c, :], sb.xhatb[:, c, :], start=(c == 0), stop=sp))(),
                        waits=tmax(twk, ln_last, io.get(f"pa{sl}")) if c == 0 else (),
                        sem="T" if sp else None)
                wkslot[sl] = (tkm,)
                rope_tok, tev = emit_rope(sb.QT[:, rc, :], ps_k,
                                          sb.cos_bc[:, tsl[0]:tsl[1]], sb.sin_sgn[:, tsl[0]:tsl[1]],
                                          PSA[2 + sl], f"ps{4 + sl}_free", 1.0, (tkm,))
                io[f"pa{sl}"] = tev
                q_toks.append(rope_tok)
            q_tok = tmax(*q_toks)

            att_toks = []
            nkb = (qc + 1) * 4
            mtoks = {}
            for h in range(16):
                rc, hh = h // 2, h % 2
                base = hh * 64
                tsq = p.op("scalar", (lambda rc=rc, base=base: lambda e: e.activation(
                    sb.sqk[0:64, :], sb.QT[base:base + 64, rc, :], AF.Square))(),
                    waits=tmax(q_tok, io.get("sqk_free")), sem="S")
                tq2 = p.op("tensor", (lambda: lambda e: e.matmul(
                    PS[4][0:1, :], sb.ones_col_bf[0:64, :], sb.sqk[0:64, :], start=True, stop=True))(),
                    waits=tmax(tsq, io.get("ps4_free")), sem="T")
                io["sqk_free"] = tq2
                tm1 = p.op("vector", (lambda h=h: lambda e: e.tensor_single_scalar(
                    sb.q2[:], PS[4][0:1, :], sb.mk2[0:1, h:h + 1], ALU.mult))(),
                    waits=tmax(tq2, kt_tok), sem="V")
                io["ps4_free"] = tm1
                tm2 = p.op("scalar", (lambda: lambda e: e.activation(
                    sb.mrow[:], sb.q2[:], AF.Ln))(), waits=(tm1,), sem="S")
                tm3 = p.op("scalar", (lambda: lambda e: e.activation(
                    sb.mrow[:], sb.mrow[:], AF.Exp, scale=0.5))(), waits=(tm2,), sem="S")
                tm4 = p.op("vector", (lambda h=h: lambda e: e.reduce_max(
                    sb.mneg[0:1, h:h + 1], sb.mrow[:], axis=AX.X, negate=True))(),
                    waits=tmax(tm3, io.get("mneg_free")), sem="V")
                mtoks[h] = tm4
            mtoksp = {}
            for hp in range(8):
                h0, h1 = 2 * hp, 2 * hp + 1
                tpm = p.op("vector", (lambda h0=h0, h1=h1: lambda e: e.tensor_tensor(
                    sb.mneg[0:1, h0:h0 + 1], sb.mneg[0:1, h0:h0 + 1],
                    sb.mneg[0:1, h1:h1 + 1], ALU.min))(),
                    waits=tmax(mtoks[h0], mtoks[h1]), sem="V")
                tm5 = p.op("tensor", (lambda h0=h0: lambda e: e.matmul(
                    PS[5][:, 0:1], sb.ones_row_f[:], sb.mneg[0:1, h0:h0 + 1],
                    start=True, stop=True))(),
                    waits=tmax(tpm, io.get("ps5_free")), sem="T")
                tm6 = p.op("vector", (lambda hp=hp: lambda e: e.tensor_copy(
                    sb.mcol[:, hp:hp + 1], PS[5][:, 0:1]))(),
                    waits=tmax(tm5, io.get("mcol_free")), sem="V")
                io["ps5_free"] = tm6
                mtoksp[hp] = tm6
            io["mneg_free"] = None

            for hp in range(8):
                h0, h1 = 2 * hp, 2 * hp + 1
                rc = hp
                tkl = p.dma("sync", (lambda h0=h0, nkb=nkb: lambda e: e.dma_start(
                    out=sb.kbuf[:, 0:nkb * 128],
                    in_=KTd[h0 * 64:h0 * 64 + 128, 0:nkb * 128]))(),
                    waits=tmax(kbslot[0], ktd_tok if qc == 0 and hp == 0 else None),
                    sem="kbl0")
                kread = []

                def emit_O2(pend_):
                    kb_, pt_, tmask_ = pend_
                    p.op("tensor", (lambda kb_=kb_, h0=h0, pt_=pt_, nkb=nkb: lambda e: e.matmul(
                        PS[0][0:65, :], sb.vtok[:, kb_, h0, :], pt_[:, 0:TC],
                        start=(kb_ == 0), stop=(kb_ == nkb - 1)))(),
                        waits=tmax(tmask_, vtok_tok,
                                   (io.get("pa0") if kb_ == 0 else None),
                                   (io.get("pa1") if kb_ == 0 else None)),
                        sem=None)
                    t_ = p.op("tensor", (lambda kb_=kb_, h1=h1, pt_=pt_, nkb=nkb: lambda e: e.matmul(
                        PS[1][0:65, :], sb.vtok[:, kb_, h1, :], pt_[:, TC:2 * TC],
                        start=(kb_ == 0), stop=(kb_ == nkb - 1)))(),
                        sem="T")
                    io["ptw0" if pt_ is sb.Pt0 else "ptw1"] = t_
                    return t_

                pend_o = None
                for kb in range(nkb):
                    u = kb % 2
                    psw = PS[8 + u]
                    pslo = PS[2 + u] if u == 0 else PS[6]
                    pshi = PS[3] if u == 0 else PS[7]
                    pt = sb.Pt0 if u == 0 else sb.Pt1
                    p.op("tensor", (lambda kb=kb, rc=rc, u=u: lambda e: e.matmul(
                        (PS[2] if u == 0 else PS[6])[:], sb.kbuf[0:64, kb * 128:(kb + 1) * 128],
                        sb.QT[0:64, rc, :], start=True, stop=True))(),
                        waits=tmax(io.get(f"psw{u}_free"), q_tok, tkl), sem=None)
                    tsB = p.op("tensor", (lambda kb=kb, rc=rc, u=u: lambda e: e.matmul(
                        (PS[3] if u == 0 else PS[7])[:], sb.kbuf[64:128, kb * 128:(kb + 1) * 128],
                        sb.QT[64:128, rc, :], start=True, stop=True))(),
                        sem="T")
                    kread.append(tsB)
                    texp = p.op("scalar", (lambda pt=pt, psw=psw, hp=hp: lambda e: e.activation(
                        pt[:], psw, AF.Exp, bias=sb.mcol[:, hp:hp + 1]))(),
                        waits=tmax(tsB, mtoksp[hp], io.get("ptw0" if u == 0 else "ptw1")),
                        sem="S")
                    io[f"psw{u}_free"] = texp
                    tmask = texp
                    off = kb * 128 - qc * TC
                    if 0 <= off < TC:
                        oi = off // 128
                        tmask = p.op("gpsimd", (lambda pt=pt, oi=oi: lambda e: e.tensor_tensor(
                            pt[:, 0:TC], pt[:, 0:TC],
                            sb.mask01[:, oi * TC:(oi + 1) * TC], ALU.mult))(),
                            waits=(texp,), sem="G")
                        tmask = p.op("gpsimd", (lambda pt=pt, oi=oi: lambda e: e.tensor_tensor(
                            pt[:, TC:2 * TC], pt[:, TC:2 * TC],
                            sb.mask01[:, oi * TC:(oi + 1) * TC], ALU.mult))(),
                            waits=(texp,), sem="G")
                    if pend_o is not None:
                        emit_O2(pend_o)
                    pend_o = (kb, pt, tmask)
                emit_O2(pend_o)
                kbslot[0] = tmax(*kread)
                for hx, base_x, psox, lb in ((h0, 0, 0, 4), (h1, 64, 1, 5)):
                    tli = p.op("scalar", (lambda psox=psox: lambda e: e.activation(
                        sb.linv[:], PS[psox][64:65, :], AF.Ln))(),
                        waits=tmax(io.get("ptw0"), io.get("ptw1"), io.get("linv_free")), sem="S")
                    tlb = p.op("tensor", (lambda lb=lb: lambda e: e.matmul(
                        PS[lb][0:64, :], sb.ones_row_f[0:1, 0:64], sb.linv[:],
                        start=True, stop=True))(),
                        waits=tmax(tli, io.get(f"ps{lb}_free")), sem="T")
                    tle = p.op("scalar", (lambda lb=lb: lambda e: e.activation(
                        sb.lbc[:], PS[lb][0:64, :], AF.Exp, scale=-1.0))(),
                        waits=tmax(tlb, io.get("lbc_free")), sem="S")
                    io[f"ps{lb}_free"] = tle
                    tat = p.op("vector", (lambda base_x=base_x, rc=rc, psox=psox: lambda e: e.tensor_mul(
                        sb.att[base_x:base_x + 64, rc, :], PS[psox][0:64, :], sb.lbc[:]))(),
                        waits=(tle,), sem="V")
                    io[f"pa{psox}"] = tat
                    io["lbc_free"] = tat
                    io["linv_free"] = tlb
                    att_toks.append(tat)
            att_tok = tmax(*att_toks)
            io["mcol_free"] = att_tok
            io["mneg_free"] = att_tok
            io["ps6_free"] = tmax(io.get("psw1_free"), io.get("ps6_free"))
            io["ps7_free"] = tmax(io.get("psw1_free"), io.get("ps7_free"))
            io["ps2_free"] = io.get("psw0_free")
            io["ps3_free"] = io.get("psw0_free")

            res_toks = []
            for o in range(ND):
                wsl = o % 2
                wob = sb.wo0 if wsl == 0 else sb.wo1
                two = p.dma("gpsimd", (lambda o=o, wob=wob: lambda e: e.dma_start(
                    out=wob[:], in_=dr["wo"][:, o * 128:(o + 1) * 128]
                    .rearrange("(c p) o2 -> p c o2", p=128)))(),
                    waits=io.get(f"wo{wsl}_rd", ()), sem=f"wold{wsl}")
                ps_o = PS6 if o % 2 == 0 else PS7
                tpo = None
                for c in range(ND):
                    sp = (c == ND - 1)
                    tpo = p.op("tensor", (lambda o=o, c=c, sp=sp, ps_o=ps_o, wob=wob: lambda e: e.matmul(
                        ps_o[:], wob[:, c, :], sb.att[:, c, :],
                        start=(c == 0), stop=sp))(),
                        waits=tmax(att_tok, two, io.get(f"ps{6 + o % 2}_free")) if c == 0 else (),
                        sem="T" if sp else None)
                io[f"wo{wsl}_rd"] = (tpo,)
                tr = p.op("vector", (lambda o=o, ps_o=ps_o: lambda e: e.tensor_add(
                    sb.lnxb[:, o, :], sb.lnxb[:, o, :], ps_o[:]))(), waits=(tpo,), sem="V")
                io[f"ps{6 + o % 2}_free"] = tr
                res_toks.append(tr)
            tst_d = p.dma("sync", (lambda qc=qc: lambda e: e.dma_start(
                out=hd[:, qc * TC:(qc + 1) * TC].rearrange("(c p) t -> p c t", p=128),
                in_=sb.lnxb[:]))(), waits=tuple(res_toks), sem="xst")
            store_toks.append(tst_d)
            lnx_consumed = tmax(tst_d, *res_toks)
            xc_consumed = ()

        p.op("sync", lambda e: None, waits=tmax(*store_toks))
    return None



_D1_DINS = [("wq", [D, D], F32), ("wk", [D, D], F32), ("wv", [D, D], F32),
            ("wo", [D, D], F32), ("sc1", [128, ND], F32), ("cos_bc", [128, T], BF16),
            ("sin_sgn", [128, T], BF16), ("Rswap", [128, 128], BF16),
            ("mask01", [128, 4 * TC], BF16), ("ones_col_bf", [128, 1], BF16),
            ("ones_row_f", [1, 128], F32), ("cst", [128, 8], F32)]


def build_launchD1():
    nc = bass.Bass("TRN2", target_bir_lowering=False)
    dr = {"xin": nc.dram_tensor("xin", [D, T], F32, kind="ExternalInput").ap()}
    for (nm, shp, dt) in _D1_DINS:
        dr[nm] = nc.dram_tensor(nm, shp, dt, kind="ExternalInput").ap()
    dr["lnxd"] = nc.dram_tensor("lnxd", [D, T], BF16, kind="ExternalOutput").ap()
    dr["hd"] = nc.dram_tensor("hd", [D, T], BF16, kind="ExternalOutput").ap()
    dr["KTd"] = nc.dram_tensor("KTd", [D, T], BF16, kind="Internal").ap()
    with ExitStack() as ctx:
        p = Prog(nc)
        PSb23 = ctx.enter_context(nc.psum_tensor("PSb23", [128, 2 * TC], F32))
        PSb67 = ctx.enter_context(nc.psum_tensor("PSb67", [128, 2 * TC], F32))
        PS = [ctx.enter_context(nc.psum_tensor("PSs0", [128, TC], F32)),
              ctx.enter_context(nc.psum_tensor("PSs1", [128, TC], F32)),
              PSb23[:, 0:TC], PSb23[:, TC:2 * TC],
              ctx.enter_context(nc.psum_tensor("PSs4", [128, TC], F32)),
              ctx.enter_context(nc.psum_tensor("PSs5", [128, TC], F32)),
              PSb67[:, 0:TC], PSb67[:, TC:2 * TC],
              PSb23[:], PSb67[:]]
        allocd = {}
        def palloc(name, shape, dt=F32):
            if name not in allocd:
                allocd[name] = ctx.enter_context(nc.sbuf_tensor("s_" + name, shape, dt))
            return allocd[name]
        emit_D1(nc, p, palloc, PS, dr)
        p.build(ctx)
    return nc

def _prep_D1(inputs, b, xin):
    global _MASK01
    c = _CONSTS
    if _MASK01 is None:
        _MASK01 = _mask01()
    return {
        "xin": xin,
        "wq": np.ascontiguousarray(np.asarray(inputs["L1_ffn_wq"])),
        "wk": np.ascontiguousarray(np.asarray(inputs["L1_ffn_wk"])),
        "wv": np.ascontiguousarray(np.asarray(inputs["L1_ffn_wv"])),
        "wo": np.ascontiguousarray(np.asarray(inputs["L1_ffn_wo"])),
        "sc1": np.ascontiguousarray(np.asarray(inputs["L1_ffn_scale1"]).reshape(ND, 128).T),
        "cos_bc": c["cos_bc"], "sin_sgn": c["sin_sgn"], "Rswap": c["Rswap"],
        "mask01": _MASK01,
        "ones_col_bf": c["ones_col_bf"], "ones_row_f": c["ones_row_f"], "cst": c["cst"],
    }



D2SLAB = 1024
ND2SLAB = T // D2SLAB


def emit_D2(nc, p, palloc, PS, dr):
    xin = dr["xin"]
    hdin = dr["hdin"]
    xout = dr["xout"]

    with _nullctx():
        sb = _SB()
        def alloc(name, shape, dt=F32):
            t = palloc(name, shape, dt)
            setattr(sb, name, t)
            return t
        alloc("hsl", [128, ND, D2SLAB], BF16)
        alloc("xhatb", [128, ND, D2SLAB], BF16)
        alloc("out_acc", [128, ND, D2SLAB])
        alloc("x3s", [128, D2SLAB])
        alloc("xb", [128, ND, TC], BF16)
        alloc("sq", [128, ND, TC], BF16)
        alloc("ln_scr", [128, TC])
        for r in ("row_nm", "row_t1", "row_t2", "row_a", "row_b"):
            alloc(r, [1, TC])
        alloc("rw_bf", [128, ND, 8], BF16)
        alloc("noise_s", [128, 8, 8])
        alloc("rsc", [128, 48])
        alloc("w_T8", [8, D2SLAB])
        alloc("wbc", [128, D2SLAB])
        alloc("W1b", [128, ND, D], BF16)
        alloc("W2b", [128, ND, D], BF16)
        alloc("WOb", [128, ND, D], BF16)
        alloc("sg0", [128, TC], BF16); alloc("sg1", [128, TC], BF16)
        alloc("u2w0", [128, TC]); alloc("u2w1", [128, TC])
        alloc("h0", [128, ND, TC], BF16); alloc("h1", [128, ND, TC], BF16)
        alloc("tmpd", [128, TC])
        alloc("b1c", [128, ND, 8])
        alloc("b2c", [128, ND, 8])
        alloc("boutr", [8, D])
        alloc("sc2", [128, ND])
        alloc("identf", [128, 128])
        alloc("E8", [8, 1024])
        alloc("ones_col_bf", [128, 1], BF16)
        alloc("ones_row_f", [1, 128])
        alloc("cst", [128, 8])

        psl = {6: PS[6], 7: PS[7], 4: PS[4], 5: PS[5]}

        wtoks = []
        for nm_ in ("b1c", "b2c", "boutr", "sc2", "identf", "E8", "ones_col_bf",
                    "ones_row_f", "cst"):
            wtoks.append(p.dma("sync", (lambda nm_=nm_: lambda e: e.dma_start(
                out=getattr(sb, nm_)[:], in_=dr[nm_]))(), sem="wld"))
        wtoks.append(p.dma("gpsimd", lambda e: e.dma_start(
            out=sb.rw_bf[:], in_=dr["rwT"].rearrange("(c p) e -> p c e", p=128)), sem="wldg"))
        wtok = tmax(*wtoks)
        io = {}

        store_toks = []
        h_consumed = ()
        acc_consumed = ()
        w_consumed = ()
        for s in range(ND2SLAB):
            ldh = p.dma("sync", (lambda s=s: lambda e: e.dma_start(
                out=sb.hsl[:], in_=hdin[:, s * D2SLAB:(s + 1) * D2SLAB]
                .rearrange("(c p) t -> p c t", p=128)))(),
                waits=h_consumed, sem="hld")
            ldn = p.dma("sync", (lambda s=s: lambda e: e.dma_start(
                out=sb.noise_s[:], in_=dr["noise"][s * D2SLAB:(s + 1) * D2SLAB, :]
                .rearrange("(tt p) e -> p tt e", p=128)))(),
                waits=io.get("noise_free", ()), sem="nld")
            h_tok = tmax(ldh, wtok)

            ln_last = None
            for q in range(D2SLAB // TC):
                ln_last = emit_norm2(p, sb, io, psl,
                                     sb.hsl[:, :, q * TC:(q + 1) * TC], h_tok,
                                     sb.xhatb[:, :, q * TC:(q + 1) * TC],
                                     eps_col=4, rms=True, scale_col=sb.sc2)

            wT_toks = []
            for tt in range(D2SLAB // 128):
                tlg = None
                for c in range(ND):
                    sp = (c == ND - 1)
                    tlg = p.op("tensor", (lambda c=c, tt=tt, sp=sp: lambda e: e.matmul(
                        PS[4][:, 0:8], sb.xhatb[:, c, tt * 128:(tt + 1) * 128], sb.rw_bf[:, c, :],
                        start=(c == 0), stop=sp))(),
                        waits=tmax(ln_last, io.get("ps4_free")) if c == 0 else (),
                        sem="T" if sp else None)
                lg = sb.rsc[:, 0:8]; msk = sb.rsc[:, 8:16]; tmp = sb.rsc[:, 16:24]
                e_ = sb.rsc[:, 24:32]; wraw = sb.rsc[:, 32:40]
                m1 = sb.rsc[:, 40:41]; m1n = sb.rsc[:, 41:42]; m2 = sb.rsc[:, 42:43]
                den = sb.rsc[:, 43:44]; dinv = sb.rsc[:, 44:45]; wt = sb.rsc[:, 0:8]
                ta = p.op("vector", (lambda lg=lg, tt=tt: lambda e: e.tensor_add(
                    lg, PS[4][:, 0:8], sb.noise_s[:, tt, :]))(),
                    waits=tmax(tlg, ldn, io.get("rsc_free")), sem="V")
                io["ps4_free"] = ta
                tb = p.op("vector", (lambda lg=lg, m1=m1: lambda e: e.reduce_max(m1, lg, axis=AX.X))(),
                          waits=(ta,), sem="V")
                tc_ = p.op("vector", (lambda lg=lg, msk=msk, m1=m1: lambda e: e.tensor_single_scalar(
                    msk, lg, m1[:, 0:1], ALU.is_equal))(), waits=(tb,), sem="V")
                td = p.op("vector", (lambda m1=m1, m1n=m1n: lambda e: e.tensor_scalar_mul(
                    m1n, m1, -1.0))(), waits=(tb,), sem="V")
                te_ = p.op("vector", (lambda msk=msk, tmp=tmp, lg=lg: lambda e: e.scalar_tensor_tensor(
                    tmp, msk, -1e9, lg, ALU.mult, ALU.add))(), waits=(tc_,), sem="V")
                tf = p.op("vector", (lambda tmp=tmp, m2=m2: lambda e: e.reduce_max(m2, tmp, axis=AX.X))(),
                          waits=(te_,), sem="V")
                tg = p.op("scalar", (lambda lg=lg, e_=e_, m1n=m1n: lambda e: e.activation(
                    e_, lg, AF.Exp, bias=m1n[:, 0:1]))(), waits=(td, te_), sem="S")
                tden = p.op("vector", (lambda e_=e_, den=den: lambda e: e.reduce_sum(den, e_, axis=AX.X))(),
                            waits=(tg,), sem="V")
                tk = p.op("vector", (lambda den=den, dinv=dinv: lambda e: e.reciprocal(dinv, den))(),
                          waits=(tden,), sem="V")
                th = p.op("vector", (lambda lg=lg, msk=msk, m2=m2: lambda e: e.tensor_single_scalar(
                    msk, lg, m2[:, 0:1], ALU.is_ge))(), waits=(tf, tg), sem="V")
                ti = p.op("vector", (lambda e_=e_, msk=msk, wraw=wraw: lambda e: e.tensor_mul(
                    wraw, e_, msk))(), waits=(th,), sem="V")
                tl = p.op("vector", (lambda wraw=wraw, dinv=dinv, wt=wt: lambda e: e.tensor_single_scalar(
                    wt, wraw, dinv[:, 0:1], ALU.mult))(), waits=(tk, ti), sem="V")
                ttr = p.op("tensor", (lambda wt=wt: lambda e: e.transpose(
                    PS[5][0:8, 0:128], wt, sb.identf[:]))(),
                    waits=tmax(tl, io.get("ps5_free")), sem="T")
                tcp = p.op("vector", (lambda tt=tt: lambda e: e.tensor_copy(
                    sb.w_T8[:, tt * 128:(tt + 1) * 128], PS[5][0:8, 0:128]))(),
                    waits=(ttr,), sem="V")
                io["ps5_free"] = tcp
                io["rsc_free"] = ttr
                wT_toks.append(tcp)
            wT_tok = tmax(*wT_toks)
            io["noise_free"] = wT_tok

            init_toks = {}
            for o in range(ND):
                for q in range(D2SLAB // TC):
                    tbm = p.op("tensor", (lambda o=o, q=q: lambda e: e.matmul(
                        PS[6 + q % 2][:], sb.boutr[0:8, o * 128:(o + 1) * 128],
                        sb.w_T8[0:8, q * TC:(q + 1) * TC], start=True, stop=True))(),
                        waits=tmax(wT_tok, io.get(f"ps{6 + q % 2}_free")), sem="T")
                    tbc = p.op("vector", (lambda o=o, q=q: lambda e: e.tensor_copy(
                        sb.out_acc[:, o, q * TC:(q + 1) * TC], PS[6 + q % 2][:]))(),
                        waits=tmax(tbm, acc_consumed), sem="V")
                    io[f"ps{6 + q % 2}_free"] = tbc
                    init_toks[(o, q)] = tbc
            acc_init = tmax(*init_toks.values())

            for ei in range(8):
                wbc_toks = []
                for q in range(D2SLAB // TC):
                    tw1 = p.op("tensor", (lambda ei=ei, q=q: lambda e: e.matmul(
                        PS[6][:], sb.E8[0:8, ei * 128:(ei + 1) * 128],
                        sb.w_T8[0:8, q * TC:(q + 1) * TC], start=True, stop=True))(),
                        waits=tmax(wT_tok, io.get("ps6_free")), sem="T")
                    tw2 = p.op("vector", (lambda q=q: lambda e: e.tensor_copy(
                        sb.wbc[:, q * TC:(q + 1) * TC], PS[6][:]))(),
                        waits=tmax(tw1, io.get("wbc_free")), sem="V")
                    io["ps6_free"] = tw2
                    wbc_toks.append(tw2)
                twb = tmax(*wbc_toks)

                t_w1 = p.dma("gpsimd", (lambda ei=ei: lambda e: e.dma_start(
                    out=sb.W1b[:], in_=dr["W1"][ei].rearrange("(c p) h -> p c h", p=128)))(),
                    waits=io.get("W1b_rd", ()), sem="w1ld")
                t_w2 = p.dma("gpsimd", (lambda ei=ei: lambda e: e.dma_start(
                    out=sb.W2b[:], in_=dr["W2"][ei].rearrange("(c p) h -> p c h", p=128)))(),
                    waits=io.get("W2b_rd", ()), sem="w2ld")
                t_w3 = p.dma("gpsimd", (lambda ei=ei: lambda e: e.dma_start(
                    out=sb.WOb[:], in_=dr["WO"][ei].rearrange("(c p) o -> p c o", p=128)))(),
                    waits=io.get("WOb_rd", ()), sem="w3ld")
                w1rd = []; w2rd = []; w3rd = []
                for tcq in range(D2SLAB // TC):
                    h_sl = tcq % 2
                    hbuf = sb.h0 if h_sl == 0 else sb.h1
                    h_toks = []
                    for hc in range(ND):
                        gr = hc % 2
                        ps_g = PS[0 + gr]
                        ps_u = PS[2 + gr]
                        sg = sb.sg0 if gr == 0 else sb.sg1
                        u2w = sb.u2w0 if gr == 0 else sb.u2w1
                        tgm = None
                        for c in range(ND):
                            sp = (c == ND - 1)
                            tgm = p.op("tensor", (lambda c=c, hc=hc, tcq=tcq, sp=sp, ps_g=ps_g: lambda e: e.matmul(
                                ps_g[:], sb.W1b[:, c, hc * 128:(hc + 1) * 128],
                                sb.xhatb[:, c, tcq * TC:(tcq + 1) * TC], start=(c == 0), stop=sp))(),
                                waits=tmax(t_w1, ln_last, io.get(f"ps{0 + gr}_free")) if c == 0 else (),
                                sem="T" if sp else None)
                        w1rd.append(tgm)
                        tsl_ = p.op("scalar", (lambda sg=sg, ps_g=ps_g, hc=hc, ei=ei: lambda e: e.activation(
                            sg[:], ps_g[:], AF.Silu, bias=sb.b1c[:, hc, ei:ei + 1]))(),
                            waits=tmax(tgm, io.get(f"sg{gr}_free")), sem="S")
                        io[f"ps{0 + gr}_free"] = tsl_
                        tum = None
                        for c in range(ND):
                            sp = (c == ND - 1)
                            tum = p.op("tensor", (lambda c=c, hc=hc, tcq=tcq, sp=sp, ps_u=ps_u: lambda e: e.matmul(
                                ps_u[:], sb.W2b[:, c, hc * 128:(hc + 1) * 128],
                                sb.xhatb[:, c, tcq * TC:(tcq + 1) * TC], start=(c == 0), stop=sp))(),
                                waits=tmax(t_w2, io.get(f"ps{2 + gr}_free")) if c == 0 else (),
                                sem="T" if sp else None)
                        w2rd.append(tum)
                        tuw = p.op("vector", (lambda u2w=u2w, hc=hc, ei=ei, ps_u=ps_u: lambda e: e.tensor_scalar_add(
                            u2w[:], ps_u[:], sb.b2c[:, hc, ei:ei + 1]))(),
                            waits=tmax(tum, io.get(f"u2w{gr}_free")), sem="V")
                        io[f"ps{2 + gr}_free"] = tuw
                        thm = p.op("vector", (lambda hbuf=hbuf, hc=hc, sg=sg, u2w=u2w: lambda e: e.tensor_mul(
                            hbuf[:, hc, :], sg[:], u2w[:]))(),
                            waits=tmax(tsl_, tuw, io.get(f"h{h_sl}_free")), sem="V")
                        io[f"sg{gr}_free"] = thm
                        io[f"u2w{gr}_free"] = thm
                        h_toks.append(thm)
                    acc_toks = []
                    for o in range(ND):
                        dr_ = o % 2
                        ps_d = PS[4 + dr_]
                        tdm = None
                        for hc in range(ND):
                            sp = (hc == ND - 1)
                            tdm = p.op("tensor", (lambda o=o, hc=hc, sp=sp, ps_d=ps_d, hbuf=hbuf: lambda e: e.matmul(
                                ps_d[:], sb.WOb[:, hc, o * 128:(o + 1) * 128], hbuf[:, hc, :],
                                start=(hc == 0), stop=sp))(),
                                waits=tmax(t_w3, h_toks[-1], io.get(f"ps{4 + dr_}_free")) if hc == 0 else (),
                                sem="T" if sp else None)
                        w3rd.append(tdm)
                        tmu = p.op("vector", (lambda o=o, tcq=tcq, ps_d=ps_d: lambda e: e.tensor_mul(
                            sb.tmpd[:], ps_d[:], sb.wbc[:, tcq * TC:(tcq + 1) * TC]))(),
                            waits=tmax(tdm, io.get("tmpd_free")), sem="V")
                        io[f"ps{4 + dr_}_free"] = tmu
                        tac = p.op("vector", (lambda o=o, tcq=tcq: lambda e: e.tensor_add(
                            sb.out_acc[:, o, tcq * TC:(tcq + 1) * TC],
                            sb.out_acc[:, o, tcq * TC:(tcq + 1) * TC], sb.tmpd[:]))(),
                            waits=tmax(tmu, acc_init), sem="V")
                        io["tmpd_free"] = tac
                        acc_toks.append(tac)
                    io[f"h{h_sl}_free"] = acc_toks[-1]
                io["W1b_rd"] = tmax(*w1rd)
                io["W2b_rd"] = tmax(*w2rd)
                io["WOb_rd"] = tmax(*w3rd)
                io["wbc_free"] = tmax(*w2rd)
                io["wbc_free"] = io.get("tmpd_free")

            res_toks = []
            for o in range(ND):
                ldx = p.dma("sync", (lambda s=s, o=o: lambda e: e.dma_start(
                    out=sb.x3s[:], in_=xin[o * 128:(o + 1) * 128, s * D2SLAB:(s + 1) * D2SLAB]))(),
                    waits=tmax(io.get("x3s_free")), sem="x3ld")
                t1_ = p.op("vector", (lambda o=o: lambda e: e.tensor_add(
                    sb.out_acc[:, o, :], sb.out_acc[:, o, :], sb.x3s[:]))(),
                    waits=tmax(ldx, io.get("tmpd_free")), sem="V")
                io["x3s_free"] = t1_
                t2_ = p.op("vector", (lambda o=o: lambda e: e.tensor_add(
                    sb.out_acc[:, o, :], sb.out_acc[:, o, :], sb.hsl[:, o, :]))(),
                    waits=(t1_,), sem="V")
                res_toks.append(t2_)
            tst_d = p.dma("sync", (lambda s=s: lambda e: e.dma_start(
                out=xout[:, s * D2SLAB:(s + 1) * D2SLAB].rearrange("(c p) t -> p c t", p=128),
                in_=sb.out_acc[:]))(), waits=tuple(res_toks), sem="xst")
            store_toks.append(tst_d)
            h_consumed = tmax(tst_d, *res_toks)
            acc_consumed = tmax(tst_d,)

        p.op("sync", lambda e: None, waits=tmax(*store_toks))
    return None



_D2_DINS = [("rwT", [D, 8], F32), ("noise", [T, 8], F32), ("W1", [8, D, D], F32),
            ("W2", [8, D, D], F32), ("WO", [8, D, D], F32), ("b1c", [128, ND, 8], F32),
            ("b2c", [128, ND, 8], F32), ("boutr", [8, D], F32), ("sc2", [128, ND], F32),
            ("identf", [128, 128], F32), ("E8", [8, 1024], F32),
            ("ones_col_bf", [128, 1], BF16), ("ones_row_f", [1, 128], F32),
            ("cst", [128, 8], F32)]


def build_launchD2():
    return _standalone(emit_D2, _D2_DINS,
                       extra=[("hdin", [D, T], BF16, "ExternalInput")])

def _prep_D2(inputs, b, xin, hdin):
    c = _CONSTS
    E8 = np.zeros((8, 1024), np.float32)
    for ei in range(8):
        E8[ei, ei * 128:(ei + 1) * 128] = 1.0
    noise = _gumbel_noise()[b] * 0.05
    return {
        "xin": xin,
        "hdin": hdin,
        "rwT": np.ascontiguousarray(np.asarray(inputs["L1_ffn_router_w"])),
        "noise": np.ascontiguousarray(noise.astype(np.float32)),
        "W1": np.ascontiguousarray(np.asarray(inputs["L1_ffn_w1"])),
        "W2": np.ascontiguousarray(np.asarray(inputs["L1_ffn_w2"])),
        "WO": np.ascontiguousarray(np.asarray(inputs["L1_ffn_wout"])),
        "b1c": np.ascontiguousarray(np.asarray(inputs["L1_ffn_b1"]).T.reshape(ND, 128, 8).transpose(1, 0, 2)),
        "b2c": np.ascontiguousarray(np.asarray(inputs["L1_ffn_b2"]).T.reshape(ND, 128, 8).transpose(1, 0, 2)),
        "boutr": np.ascontiguousarray(np.asarray(inputs["L1_ffn_bout"])),
        "sc2": np.ascontiguousarray(np.asarray(inputs["L1_ffn_scale2"]).reshape(ND, 128).T),
        "identf": np.eye(128, dtype=np.float32),
        "E8": E8,
        "ones_col_bf": c["ones_col_bf"], "ones_row_f": c["ones_row_f"], "cst": c["cst"],
    }



_SHARED_SPECS = [("cst", [128, 8], F32), ("ones_col_bf", [128, 1], BF16),
                 ("ones_row_f", [1, 128], F32), ("identf", [128, 128], F32),
                 ("E8", [8, 1024], F32), ("SB16", [16, 128], BF16)]


def build_fused():
    nc = bass.Bass("TRN2", target_bir_lowering=False)
    shared_dr = {}
    def shared_din(name, shape, dt):
        if name not in shared_dr:
            shared_dr[name] = nc.dram_tensor(name, shape, dt, kind="ExternalInput").ap()
        return shared_dr[name]
    def mkdr(pref, dins):
        dr = {}
        for (nm, shp, dt) in dins:
            if nm in SHARED_SBUF:
                dr[nm] = shared_din(nm, shp, dt)
            else:
                dr[nm] = nc.dram_tensor(f"{pref}_{nm}", shp, dt, kind="ExternalInput").ap()
        return dr
    drA = mkdr("A", _A_DINS)
    drB = mkdr("B", _B_DINS)
    drC = mkdr("C", _C_DINS)
    drD1 = mkdr("D1", _D1_DINS)
    drD2 = mkdr("D2", _D2_DINS)
    xin = nc.dram_tensor("A_xin", [D, T], F32, kind="ExternalInput").ap()
    xout = nc.dram_tensor("xout", [D, T], F32, kind="ExternalOutput").ap()
    x1d = nc.dram_tensor("x1d", [D, T], F32, kind="Internal").ap()
    x2d = nc.dram_tensor("x2d", [D, T], F32, kind="Internal").ap()
    x3d = nc.dram_tensor("x3d", [D, T], F32, kind="Internal").ap()
    lnxd = nc.dram_tensor("lnxd", [D, T], BF16, kind="Internal").ap()
    hdd = nc.dram_tensor("hdd", [D, T], BF16, kind="Internal").ap()
    KTd = nc.dram_tensor("KTd", [D, T], BF16, kind="Internal").ap()
    drA["xin"] = xin; drA["xout"] = x1d
    drB["xin"] = x1d; drB["xout"] = x2d
    drC["xin"] = x2d; drC["xout"] = x3d; drC["KTd"] = KTd
    drD1["xin"] = x3d; drD1["lnxd"] = lnxd; drD1["hd"] = hdd; drD1["KTd"] = KTd
    drD2["xin"] = x3d; drD2["hdin"] = hdd; drD2["xout"] = xout

    with ExitStack() as ctx:
        p = Prog(nc)
        PSb23 = ctx.enter_context(nc.psum_tensor("PSb23", [128, 2 * TC], F32))
        PSb67 = ctx.enter_context(nc.psum_tensor("PSb67", [128, 2 * TC], F32))
        PS = [ctx.enter_context(nc.psum_tensor("PSs0", [128, TC], F32)),
              ctx.enter_context(nc.psum_tensor("PSs1", [128, TC], F32)),
              PSb23[:, 0:TC], PSb23[:, TC:2 * TC],
              ctx.enter_context(nc.psum_tensor("PSs4", [128, TC], F32)),
              ctx.enter_context(nc.psum_tensor("PSs5", [128, TC], F32)),
              PSb67[:, 0:TC], PSb67[:, TC:2 * TC],
              PSb23[:], PSb67[:]]
        palloc = PhaseAlloc(nc)
        palloc.prealloc_shared(_SHARED_SPECS)
        for pref, emit, dr in (("A", emit_A, drA), ("B", emit_B, drB), ("C", emit_C, drC),
                               ("D1", emit_D1, drD1), ("D2", emit_D2, drD2)):
            palloc.start(pref)
            emit(nc, p, palloc, PS, dr)
            p.fence()
        p.build(ctx)
    return nc


def kernel(**inputs):
    cores = list(range(NCORE))
    nc = build_fused()
    in_maps = []
    for b in cores:
        m = {}
        def put(pref, d):
            for k, v in d.items():
                if k in ("xin", "hdin"):
                    continue
                if k in SHARED_SBUF:
                    m[k] = v
                else:
                    m[f"{pref}_{k}"] = v
        pa = _prep_A(inputs, b)
        m["A_xin"] = pa.pop("xin")
        put("A", pa)
        put("B", _prep_B(inputs, b, None))
        put("C", _prep_C(inputs, b, None))
        put("D1", _prep_D1(inputs, b, None))
        put("D2", _prep_D2(inputs, b, None, None))
        in_maps.append(m)
    res = run_bass_kernel_spmd(nc, in_maps, core_ids=cores)
    out = np.stack([np.asarray(r["xout"]).T for r in res.results])
    return out.astype(np.float32)
